# version 18
# baseline (speedup 1.0000x reference)
import numpy as np
from contextlib import ExitStack
import concourse.bass as bass
import concourse.mybir as mybir
from concourse.alu_op_type import AluOpType
from concourse.bass_utils import run_bass_kernel_spmd

F32 = mybir.dt.float32
BF16 = mybir.dt.bfloat16
AF = mybir.ActivationFunctionType
AX = mybir.AxisListType
f32 = np.float32
ADD, MUL, SUB = AluOpType.add, AluOpType.mult, AluOpType.subtract
MAXO, MINO, BYP = AluOpType.max, AluOpType.min, AluOpType.bypass

D = 1024
T = 4096
NP = 8
CW = 256
NCOL = T // CW
CHUNK = 64
EPS = f32(1e-5)

ENGINES = ("pe", "dve", "act", "sp", "pool")


NLANE = 4


class Ctx:
    def __init__(self, nc):
        self.nc = nc
        self.es = ExitStack()
        self.q = {e: [] for e in ENGINES}
        self.cnt = {}
        self.sem = {}
        names = list(ENGINES)
        for qn in ("spq", "poolq", "actq"):
            for l in range(NLANE):
                names.append(f"{qn}{l}")
        for e in names:
            self.sem[e] = self.es.enter_context(nc.semaphore(f"s_{e}"))
            self.cnt[e] = 0
        self.lane_rr = {"spq": 0, "poolq": 0, "actq": 0}
        self.last_wait = {e: {} for e in ENGINES}

    def tok(self, e):
        return (e, self.cnt[e])

    def _waits(self, e, waits):
        out = []
        lw = self.last_wait[e]
        for w in waits:
            if w is None:
                continue
            s, v = w
            if s == e or v <= 0:
                continue
            if lw.get(s, 0) >= v:
                continue
            lw[s] = v
            out.append((s, v))
        return out

    def emit(self, e, fn, waits=(), inc=None):
        import traceback as _tb
        fn._src = "".join(_tb.format_stack(limit=6)[:-2])
        wl = self._waits(e, waits)
        serial = e in ("dve", "act")
        do_inc = serial if inc is None else inc
        selfwait = self.cnt[e] if (serial and self.cnt[e] > 0) else None
        if do_inc:
            self.cnt[e] += 1
        n = self.cnt[e]
        self.q[e].append((wl, fn, n if do_inc else None, selfwait))
        return (e, n)

    def dma(self, e, out, in_, waits=(), **kw):
        q = {"sp": "spq", "pool": "poolq", "act": "actq"}[e]
        lane = self.lane_rr[q]
        self.lane_rr[q] = (lane + 1) % NLANE
        ln = f"{q}{lane}"
        wl = self._waits(e, list(waits) + [(ln, self.cnt[ln])])
        self.cnt[ln] += 16
        n = self.cnt[ln]
        sem = self.sem[ln]

        def fn(eng, out=out, in_=in_, kw=kw):
            return eng.dma_start(out=out, in_=in_, **kw)

        self.q[e].append((wl, fn, ("dma", sem), None))
        return (ln, n)

    def wait(self, e, *toks):
        wl = self._waits(e, [t for t in toks if t is not None])
        if wl:
            self.q[e].append((wl, None, None, None))

    def barrier(self):
        toks = [(s_, self.cnt[s_]) for s_ in self.sem if s_ not in ("sp", "pool")]
        for e in ENGINES:
            self.wait(e, *toks)

    def play(self):
        nc = self.nc
        sems = self.sem

        def mk(e):
            queue = self.q[e]

            def body(eng):
                for wl, fn, inc, selfwait in queue:
                    if selfwait:
                        eng.wait_ge(sems[e], selfwait)
                    for (s, v) in wl:
                        eng.wait_ge(sems[s], v)
                    if fn is None:
                        continue
                    try:
                        inst = fn(eng)
                    except Exception:
                        import traceback
                        print("FAILING OP on", e, "emitted at:")
                        print(getattr(fn, "_src", "?"))
                        raise
                    if inst is None:
                        continue
                    if isinstance(inc, tuple):
                        inst.then_inc(inc[1], 16)
                    elif inc is not None:
                        inst.then_inc(sems[e], 1)
            return body

        with nc.Block() as block:
            block.tensor(mk("pe"))
            block.vector(mk("dve"))
            block.scalar(mk("act"))
            block.sync(mk("sp"))
            block.gpsimd(mk("pool"))
        self.es.close()


def tt(c, out, a, b, op, w=()):
    return c.emit("dve", lambda v: v.tensor_tensor(out=out, in0=a, in1=b, op=op), w)


def _sc(x):
    if isinstance(x, (np.floating, np.integer)):
        return float(x)
    return x


def ts(c, out, a, s1, op0, s2=None, op1=None, w=()):
    kw = {}
    if op1 is not None:
        kw["op1"] = op1
    s1 = _sc(s1); s2 = _sc(s2)
    return c.emit("dve", lambda v: v.tensor_scalar(out=out, in0=a, scalar1=s1,
                                                   scalar2=s2, op0=op0, **kw), w)


def stt(c, out, in0, sc, in1, op0, op1, w=()):
    sc = _sc(sc)
    return c.emit("dve", lambda v: v.scalar_tensor_tensor(
        out=out, in0=in0, scalar=sc, in1=in1, op0=op0, op1=op1), w)


def vcopy(c, out, a, w=()):
    return c.emit("dve", lambda v: v.tensor_copy(out=out, in_=a), w)


def vmemset(c, ap, val, w=()):
    return c.emit("dve", lambda v: v.memset(ap, val), w)


def vred(c, out, a, op, w=()):
    return c.emit("dve", lambda v: v.tensor_reduce(out=out, in_=a, axis=AX.X, op=op), w)


def vmax8(c, out, a, w=()):
    return c.emit("dve", lambda v: v.max(out=out, in_=a), w)


def vmaxidx(c, out, mx, vals, w=()):
    return c.emit("dve", lambda v: v.max_index(out=out, in_max=mx, in_values=vals), w)


def act(c, out, a, func, bias=None, scale=None, w=()):
    kw = {}
    if bias is not None:
        kw["bias"] = np.float32(bias) if isinstance(bias, (float, np.floating)) else bias
    if scale is not None:
        kw["scale"] = np.float32(scale) if isinstance(scale, (float, np.floating)) else scale
    return c.emit("act", lambda s: s.activation(out=out, in_=a, func=func, **kw), w)


def mm(c, out, lhsT, rhs, start, stop, w=(), inc=False):
    return c.emit("pe", lambda t: t.matmul(out, lhsT, rhs, start=start, stop=stop),
                  w, inc=inc)


def transp(c, out, in_, ident, w=(), inc=False):
    return c.emit("pe", lambda t: t.transpose(out, in_, ident), w, inc=inc)


def bcast(ap_, np_=NP):
    return ap_.unsqueeze(1).broadcast_to((ap_.shape[0], np_, ap_.shape[1]))


def bcast_t(ap_, n):
    return ap_.unsqueeze(2).broadcast_to((ap_.shape[0], ap_.shape[1], n))



def build_program(parts):
    nc = bass.Bass("TRN2", target_bir_lowering=False)
    es = ExitStack()
    c = Ctx(nc)

    def dram(name, shape, dt=F32, kind="ExternalInput"):
        return nc.dram_tensor(name, shape, dt, kind=kind).ap()

    def sb(name, shape, dt=F32):
        return es.enter_context(nc.sbuf_tensor(name, shape, dt))

    def psum(name, shape, dt=F32):
        return es.enter_context(nc.psum_tensor(name, shape, dt))

    need_A = "A" in parts
    need_F = ("F1" in parts) or ("F2" in parts)
    need_B = "B" in parts

    Rin = dram("Rin", [NP * 128, T])
    Rout = dram("Rout", [NP * 128, T // 2], kind="ExternalOutput")
    consts = dram("consts", [128, 400])

    if need_B:
        qTd = dram("qTd", [128, 32 * 256])
        combd = dram("combd", [128, 32 * 256])
        zscr = dram("zscr", [D, T], BF16, kind="Internal")
        qbs = dram("qbs", [128, 32 * 256], BF16, kind="Internal")
        cbs = dram("cbs", [128, 32 * 256], BF16, kind="Internal")
    if need_F:
        sigsT_raw = dram("sigsT_raw", [D, 64])
        dirsT = dram("dirsT", [D, 64])
        dirsL = dram("dirsL", [64, D])
        cw1 = dram("cw1", [D, 256])
        cw2 = dram("cw2", [128, 4])
        w1f = dram("w1f", [2, 1024])
        w2blk = dram("w2blk", [1024, 64])
        vecsF = dram("vecsF", [128, 20])
        b2tmd = dram("b2tmd", [128, 64])
        b1fd = dram("b1fd", [128, 8])
    if need_A:
        w_inT = dram("w_inT", [D, 128])
        w_gateT = dram("w_gateT", [128, 8192])
        w_outT = dram("w_outT", [128, D])
        w1T = dram("w1T", [128, 32768])
        w2T = dram("w2T", [128, 32768])
        vecsA = dram("vecsA", [128, 48])
        bmlp1 = dram("bmlp1", [128, 32])
        eig = dram("eig", [64, 2])
        couple = dram("couple", [64, 64])
        halo = dram("halo", [128, 10])
        wgs = dram("wgs", [128, 8192], BF16, kind="Internal")
        w1s = dram("w1s", [128, 32768], BF16, kind="Internal")
        w2s = dram("w2s", [128, 32768], BF16, kind="Internal")

    R = sb("R", [128, NP * T])
    ones_bf = sb("ones_bf", [128, 128], BF16)
    onesf = sb("onesf", [128, 128])
    ident_bf = sb("ident_bf", [128, 128], BF16)
    smallsB = sb("smallsB", [128, 12])
    xn_bf = sb("xn_bf", [128, NP * CW], BF16)
    if need_A or need_F:
        u32 = sb("u32", [128, NP * CW])
    rbf = xn_bf[:, 0:NP * CW]
    sqbf = sb("sqbf", [128, NP * CW], BF16)
    bc = sb("bc", [128, 8 * CW])
    hid_bf = sb("hid_bf", [128, 32 * CW], BF16)
    wbig = sb("wbig", [128, 2 * 2048], BF16)
    wbig2 = sb("wbig2", [128, 2 * 1024], BF16)
    if need_B:
        gram = hid_bf[:, 0:4096].bitcast(F32)
        gram_bf = hid_bf[:, 4096:6144]
        t1T = hid_bf[:, 6144:8192]
        Msb = wbig[:, 0:2048]
        ztT = wbig[:, 2048:4096]
        wqa = wbig2[:, 0:2048]
        wcba = hid_bf[:, 0:2048]
    if need_A or need_F:
        x2n_bf = sb("x2n_bf", [128, NP * CW], BF16)
        gate_bf = sb("gate_bf", [128, NP * CW], BF16)
    if need_A:
        ccb = sb("ccb", [128, CW], BF16)
        carry = sb("carry", [64, 2])
        lam = sb("lam", [64, 20])
        lamt = sb("lamt", [64, 4])
        vA = sb("vA", [128, 48])
        bm1 = sb("bm1", [128, 32])
        eig_sb = sb("eig_sb", [64, 2])
        couple_sb = sb("couple_sb", [64, 64])
        winsb = sb("winsb", [128, 8 * 128], BF16)
        halo_sb = sb("halo_sb", [128, 10])
        woutsb = sb("woutsb", [128, 8 * 128], BF16)

    if need_F:
        wsc = sb("wsc", [128, 8 * 72], BF16)
        wdT = sb("wdT", [128, 8 * 64], BF16)
        wdirs = sb("wdirs", [64, D], BF16)
        wc1 = sb("wc1", [128, 8 * 256], BF16)
        wc2 = sb("wc2", [128, 4], BF16)
        ww1f = sb("ww1f", [2, 1024], BF16)
        ww2blk = sb("ww2blk", [128, 8 * 64], BF16)
        vF = sb("vF", [128, 20])
        b1f_sb = sb("b1f_sb", [128, 8])
        knotsB = sb("knotsB", [128, 4])
        dB = sb("dB", [128, 7])
        drows = sb("drows", [64, 2])
        iota64 = sb("iota64", [128, 64], BF16)
        clus64 = sb("clus64", [128, 64], BF16)
        identf = sb("identf", [128, 128])
        calF = sb("calF", [128, 10])
        idxu = sb("idxu", [128, 32], mybir.dt.uint32)
        tblp = x2n_bf[:, 0:640]
        prodp = x2n_bf[:, 640:1280]
        extrF = x2n_bf[:, 640:1152].bitcast(F32)
        rhsd_r = x2n_bf[:, 1280:1792].bitcast(F32)
        rowsP = gate_bf[:, 0:64].bitcast(F32)
        mxcB = gate_bf[:, 64:96]
        idxfB = gate_bf[:, 96:160].bitcast(F32)
        mxtF = gate_bf[:, 160:224].bitcast(F32)
        mskg = gate_bf[:, 224:480].bitcast(F32)
        magsbP = [gate_bf[:, 480:736].bitcast(F32),
                  gate_bf[:, 736:992].bitcast(F32)]
        rowsEP = [gate_bf[:, 992:1008].bitcast(F32),
                  gate_bf[:, 1008:1024].bitcast(F32)]
        egl = hid_bf[:, 4096:4608]
        xgl = hid_bf[:, 4608:5120]
        rhsd_m = gate_bf[:, 1536:2048].bitcast(F32)
        uu = hid_bf[:, 2048:2560].bitcast(F32)[0:72, :]
        gts = hid_bf[:, 2560:3072].bitcast(F32)[0:72, :]
        gts_bf = hid_bf[0:72, 3072:3328]
        zd_bf = hid_bf[0:64, 3328:3584]
        h1_bf = hid_bf[:, 3584:4096]
        gtmP = [hid_bf[:, 5120:5280], gate_bf[:, 1024:1184]]
        ztm = hid_bf[:, 5280:5408]
        ptm = hid_bf[:, 5408:5536]
        pT_bf = hid_bf[0:64, 5536:5792]
        oh = hid_bf[:, 5792:6304].bitcast(F32)
        comp_bf = hid_bf[0:2, 6304:6560]
        rowsc = sb("rowsc", [128, 12])
        b2tm = sb("b2tm", [128, 64], BF16)

    pb = [psum(f"pb{i}", [128, 512]) for i in range(7)]
    pbb = psum("pbb", [128, 1024], BF16)

    t_ones = c.dma("pool", ones_bf[:], consts[:, 0:128])
    t_onesf = c.dma("pool", onesf[:], consts[:, 0:128])
    t_id = c.dma("pool", ident_bf[:], consts[:, 128:256])
    t_sm = c.dma("pool", smallsB[:], consts[:, 256:268])
    for p in range(NP):
        t_R = c.dma("sp", R[:, p * T:(p + 1) * T], Rin[p * 128:(p + 1) * 128, :])

    c.barrier()

    def slot(i):
        return bc[:, i * CW:(i + 1) * CW]

    if need_A:
        cc = bc[0:64, 0:2 * CW]
        cct = bc[0:64, 3 * CW:4 * CW]

    state = {"stats_pe": None, "stats_dve": None, "bc_guard": None}

    def col_stats_mm(src, extra=(), bs=None):
        b1, b2 = bs if bs else (pb[5][:, 0:CW], pb[6][:, 0:CW])
        for p in range(NP):
            act(c, sqbf[:, p * CW:(p + 1) * CW], src(p), AF.Square,
                w=([state["stats_pe"]] + list(extra)) if p == 0 else ())
        t_sq = c.tok("act")
        c.wait("pe", t_sq, t_onesf, state["stats_dve"])
        for p in range(NP):
            mm(c, b1, onesf[:], src(p),
               start=(p == 0), stop=(p == NP - 1))
        last = None
        for p in range(NP):
            last = mm(c, b2, ones_bf[:], sqbf[:, p * CW:(p + 1) * CW],
                      start=(p == 0), stop=(p == NP - 1), inc=(p == NP - 1))
        state["stats_pe"] = last
        return last

    def col_stats_fin(mslot, vslot, last, bs=None):
        b1, b2 = bs if bs else (pb[5][:, 0:CW], pb[6][:, 0:CW])
        ts(c, slot(mslot), b1, f32(1.0 / D), MUL, w=[last])
        state["stats_dve"] = ts(c, slot(vslot), b2, f32(1.0 / D), MUL)
        tt(c, slot(7), slot(mslot), slot(mslot), MUL)
        return tt(c, slot(vslot), slot(vslot), slot(7), SUB)

    def col_stats(src, mslot, vslot, extra=()):
        last = col_stats_mm(src, extra)
        return col_stats_fin(mslot, vslot, last)

    def rstd_into(vslot, oslot):
        act(c, slot(oslot), slot(vslot), AF.Ln, bias=EPS, w=[c.tok("dve")])
        a = act(c, slot(oslot), slot(oslot), AF.Exp, scale=f32(-0.5))
        c.wait("dve", a)
        return a

    def Rcol(p, off, n=CW):
        return R[:, p * T + off: p * T + off + n]

    def Rview(off, n=CW):
        return R[:].rearrange("q (p t) -> q p t", p=NP)[:, :, off:off + n]

    def colview(tensor, npt=NP, n=CW):
        return tensor[:].rearrange("q (p t) -> q p t", p=npt)

    def sublayer_A():
        t_vA = c.dma("pool", vA[:], vecsA[:, :])
        t_bm1 = c.dma("pool", bm1[:], bmlp1[:, :])
        t_eig = c.dma("pool", eig_sb[:], eig[:, :])
        t_cp = c.dma("pool", couple_sb[:], couple[:, :])
        t_win = c.dma("pool", winsb[:].rearrange("p (k m) -> p k m", k=8),
                      w_inT[:, :].rearrange("(k p) m -> p k m", p=128))
        t_wout = c.dma("pool", woutsb[:], w_outT[:, :])
        t_wg = c.dma("pool", wgs[:, :], w_gateT[:, :])
        t_w1 = c.dma("pool", w1s[:, :], w1T[:, :])
        t_w2 = c.dma("pool", w2s[:, :], w2T[:, :])
        g1 = vA[:, 0:8]; b1 = vA[:, 8:16]; g2 = vA[:, 16:24]; b2 = vA[:, 24:32]
        bgate = vA[:, 32:40]; bmlp2v = vA[:, 40:48]

        ld_c = eig_sb[:, 0:1]; fq = eig_sb[:, 1:2]
        mag = lamt[:, 0:1]; fr = lamt[:, 1:2]; q0 = lamt[:, 2:3]; q1 = lamt[:, 3:4]
        act(c, mag, ld_c, AF.Sigmoid, w=[t_eig])
        vcopy(c, fr, fq, w=[t_eig])
        for _ in range(6):
            act(c, q0, fr, AF.Abs, w=[c.tok("dve")])
            act(c, q0, q0, AF.Sign, bias=f32(-np.pi))
            ts(c, q0, q0, f32(0.0), MAXO, w=[c.tok("act")])
            act(c, q1, fr, AF.Sign, w=[c.tok("dve")])
            tt(c, q0, q0, q1, MUL, w=[c.tok("act")])
            stt(c, fr, q0, f32(-2.0 * np.pi), fr, MUL, ADD)
        act(c, lam[:, 16:17], fr, AF.Sin, w=[c.tok("dve")])
        ts(c, q0, fr, f32(-1.0), MUL, f32(np.pi / 2), ADD, w=[c.tok("act")])
        act(c, q1, q0, AF.Sign, bias=f32(-np.pi), w=[c.tok("dve")])
        ts(c, q1, q1, f32(0.0), MAXO, w=[c.tok("act")])
        stt(c, q0, q1, f32(-2.0 * np.pi), q0, MUL, ADD)
        act(c, lam[:, 17:18], q0, AF.Sin, w=[c.tok("dve")])
        tt(c, lam[:, 0:1], mag, lam[:, 17:18], MUL, w=[c.tok("act")])
        tt(c, lam[:, 1:2], mag, lam[:, 16:17], MUL)
        for i in range(1, 6):
            pr = lam[:, 2 * i - 2:2 * i - 1]; pi_ = lam[:, 2 * i - 1:2 * i]
            qr = lam[:, 2 * i:2 * i + 1]; qi = lam[:, 2 * i + 1:2 * i + 2]
            tt(c, q0, pr, pr, MUL)
            tt(c, q1, pi_, pi_, MUL)
            tt(c, qr, q0, q1, SUB)
            tt(c, q0, pr, pi_, MUL)
            ts(c, qi, q0, f32(2.0), MUL)
        for i in range(6):
            ts(c, lam[:, 12 + i:13 + i], lam[:, 2 * i + 1:2 * i + 2], f32(-1.0), MUL)
        t_lam = c.tok("dve")
        t_halo = c.dma("pool", halo_sb[:], halo[:, :])
        hb = slot(6)
        act(c, hb[:, 16:24], halo_sb[:, 0:8], AF.Square, w=[t_halo])
        t_hsq = c.tok("act")
        c.wait("pe", t_halo, t_hsq, t_onesf)
        for p in range(NP):
            mm(c, pb[2][:, 0:1], onesf[:], halo_sb[:, p:p + 1],
               start=(p == 0), stop=(p == NP - 1))
        for p in range(NP):
            t_hs2 = mm(c, pb[2][:, 1:2], onesf[:], hb[:, 16 + p:17 + p],
                       start=(p == 0), stop=(p == NP - 1), inc=(p == NP - 1))
        ts(c, hb[:, 0:1], pb[2][:, 0:1], f32(1.0 / D), MUL, w=[t_hs2])
        ts(c, hb[:, 1:2], pb[2][:, 1:2], f32(1.0 / D), MUL)
        tt(c, hb[:, 2:3], hb[:, 0:1], hb[:, 0:1], MUL)
        tt(c, hb[:, 1:2], hb[:, 1:2], hb[:, 2:3], SUB)
        act(c, hb[:, 2:3], hb[:, 1:2], AF.Ln, bias=EPS, w=[c.tok("dve")])
        t_ha = act(c, hb[:, 2:3], hb[:, 2:3], AF.Exp, scale=f32(-0.5))
        tt(c, hb[:, 3:4], hb[:, 2:3], hb[:, 2:3], MUL, w=[t_ha])
        tt(c, hb[:, 4:5], hb[:, 1:2], hb[:, 3:4], MUL)
        act(c, hb[:, 5:6], hb[:, 4:5], AF.Ln, bias=EPS, w=[c.tok("dve")])
        t_ha2 = act(c, hb[:, 5:6], hb[:, 5:6], AF.Exp, scale=f32(-0.5))
        tt(c, hb[:, 6:7], hb[:, 2:3], hb[:, 5:6], MUL, w=[t_ha2])
        ts(c, hb[:, 8:16], halo_sb[:, 0:8], hb[:, 0:1], AluOpType.subtract)
        ts(c, hb[:, 8:16], hb[:, 8:16], hb[:, 6:7], MUL)
        tt(c, hb[:, 8:16], hb[:, 8:16], vA[:, 0:8], MUL, w=[t_vA])
        tt(c, hb[:, 8:16], hb[:, 8:16], vA[:, 8:16], ADD)
        t_xnp = vcopy(c, ccb[:, 0:8], hb[:, 8:16])
        c.wait("pe", t_xnp, t_win)
        for k in range(NP):
            mm(c, pb[2][0:64, 2:3], winsb[:, k * 128:k * 128 + 64],
               ccb[:, k:k + 1], start=(k == 0), stop=(k == NP - 1))
        for k in range(NP):
            t_bp = mm(c, pb[2][0:64, 3:4], winsb[:, k * 128 + 64:(k + 1) * 128],
                      ccb[:, k:k + 1], start=(k == 0), stop=(k == NP - 1),
                      inc=(k == NP - 1))
        ts(c, carry[:, 0:1], pb[2][0:64, 2:3], halo_sb[0:64, 8:9], MUL, w=[t_bp])
        t_carry = ts(c, carry[:, 1:2], pb[2][0:64, 3:4], halo_sb[0:64, 8:9], MUL)
        c.wait("pe", t_carry)

        c.wait("pe", t_win, t_wout, t_cp, t_R)
        c.wait("dve", t_vA, t_bm1, t_lam, t_carry)
        c.wait("act", t_vA, t_bm1, t_R)
        prevA = {"gate_pe": None, "x1_dve": None, "ccb_act": None,
                 "mlp2_dve": None, "m1_last": None, "u32_act": None,
                 "gate_act": None, "stats_dve_g": None}
        colstate = {}

        def slab1(col):
            off = col * CW
            col_stats(lambda p: Rcol(p, off), 0, 1,
                      extra=[prevA["gate_pe"]])
            rstd_into(1, 2)
            tt(c, slot(7), slot(2), slot(2), MUL)
            tt(c, slot(3), slot(1), slot(7), MUL)
            rstd_into(3, 3)
            tt(c, slot(4), slot(2), slot(3), MUL)
            tt(c, slot(5), slot(0), slot(4), MUL)
            tt(c, slot(6), slot(0), slot(2), MUL)
            prevA["stats_dve_g"] = state["stats_dve"]
            xv = colview(xn_bf)
            tt(c, xv, Rview(off), bcast(slot(4)), MUL, w=[prevA["gate_pe"]])
            t_xnp = tt(c, xv, xv, bcast(slot(5)), SUB)
            for p in range(NP):
                t_xn = act(c, xn_bf[:, p * CW:(p + 1) * CW], xn_bf[:, p * CW:(p + 1) * CW],
                           AF.Identity, scale=g1[:, p:p + 1], bias=b1[:, p:p + 1],
                           w=[t_xnp])
            uv = colview(u32)
            tt(c, uv, Rview(off), bcast(slot(2)), MUL, w=[prevA["u32_act"]])
            t_y = tt(c, uv, uv, bcast(slot(6)), SUB)
            c.wait("pe", t_xn, prevA["x1_dve"])
            for k in range(NP):
                mm(c, pb[2][0:64, 0:CW], winsb[:, k * 128:k * 128 + 64],
                   xn_bf[:, k * CW:(k + 1) * CW], start=(k == 0),
                   stop=(k == NP - 1))
            for k in range(NP):
                t_beta = mm(c, pb[2][0:64, CW:2 * CW], winsb[:, k * 128 + 64:(k + 1) * 128],
                            xn_bf[:, k * CW:(k + 1) * CW], start=(k == 0),
                            stop=(k == NP - 1), inc=(k == NP - 1))
            t_c0 = vcopy(c, cc[:, :], pb[2][0:64, 0:2 * CW], w=[t_beta])
            cr = cc[:, 0:CW]; ci = cc[:, CW:2 * CW]
            tmp = cct[:, 0:CW]
            d, i = 1, 0
            while d < CHUNK:
                pr = lam[:, 2 * i:2 * i + 1]
                pi_ = lam[:, 2 * i + 1:2 * i + 2]
                nli = lam[:, 12 + i:13 + i]
                crv = cr.rearrange("k (n s) -> k n s", s=CHUNK)
                civ = ci.rearrange("k (n s) -> k n s", s=CHUNK)
                tmv = tmp.rearrange("k (n s) -> k n s", s=CHUNK)
                dst_r = crv[:, :, 0:CHUNK - d]; src_r = crv[:, :, d:CHUNK]
                dst_i = civ[:, :, 0:CHUNK - d]; src_i = civ[:, :, d:CHUNK]
                tm = tmv[:, :, 0:CHUNK - d]
                vcopy(c, tm, src_r)
                stt(c, dst_r, src_r, pr, dst_r, MUL, ADD)
                stt(c, dst_r, src_i, nli, dst_r, MUL, ADD)
                stt(c, dst_i, src_i, pr, dst_i, MUL, ADD)
                stt(c, dst_i, tm, pi_, dst_i, MUL, ADD)
                d *= 2
                i += 1
            if col == NCOL // 2:
                ts(c, carry[:], carry[:], halo_sb[0:64, 9:10], MUL)
            nch = CW // CHUNK
            for half in range(2):
                cch = cc[:, half * CW:(half + 1) * CW]
                ccv2 = cch.rearrange("k (n s) -> k n s", s=CHUNK)
                bv = pb[2][0:64, half * CW:(half + 1) * CW].rearrange(
                    "k (n s) -> k n s", s=CHUNK)
                tt(c, ccv2[:, 1:nch, 0:1], ccv2[:, 1:nch, 0:1],
                   bv[:, 0:nch - 1, CHUNK - 1:CHUNK], ADD)
                tt(c, cch[:, 0:1], cch[:, 0:1], carry[:, half:half + 1], ADD)
                vcopy(c, carry[:, half:half + 1],
                      pb[2][0:64, (half + 1) * CW - 1:(half + 1) * CW])
            colstate[col] = [None, t_y, None, None, None, ([], []), c.tok("dve")]

        def slab1b(col):
            c.wait("pe", c.tok("dve"), prevA["ccb_act"])
            mm(c, pb[2][0:64, 0:CW], couple_sb[:], cc[:, 0:CW], start=True, stop=True)
            t_cpl = mm(c, pb[2][64:128, 0:CW], couple_sb[:], cc[:, CW:2 * CW],
                       start=True, stop=True, inc=True)
            act(c, ccb[0:64, :], pb[2][0:64, 0:CW], AF.Identity, w=[t_cpl])
            t_ccb = act(c, ccb[64:128, :], pb[2][64:128, 0:CW], AF.Identity)
            prevA["ccb_act"] = t_ccb
            colstate[col][0] = t_ccb

        def gate_step(col, do):
            gst = colstate[col][5]
            gate_mil, t_gb_hist = gst
            grp, ds = do // 2, do % 2
            if ds == 0:
                gw = ([gate_mil[grp - 2], t_wg] if grp >= 2
                      else [t_wg, prevA["m1_last"]])
                tks = []
                for sub in range(4):
                    tks.append(c.dma(
                        "sp", wbig[:, grp % 2 * 2048 + sub * 512:
                                   grp % 2 * 2048 + (sub + 1) * 512],
                        wgs[:, grp * 2048 + sub * 512:grp * 2048 + (sub + 1) * 512],
                        waits=gw))
                c.wait("pe", *tks)
            wslice = wbig[:, grp % 2 * 2048:(grp % 2 + 1) * 2048]
            gb_bank = pb[3] if do % 2 == 0 else pb[5]
            if do >= 2:
                c.wait("pe", t_gb_hist[do - 2])
            elif do == 0:
                c.wait("pe", prevA["gate_act"], prevA["stats_dve_g"])
            for k in range(NP):
                t_g = mm(c, gb_bank[:, 0:CW],
                         wslice[:, (k * 256 + ds * 128):(k * 256 + ds * 128 + 128)],
                         xn_bf[:, k * CW:(k + 1) * CW], start=(k == 0),
                         stop=(k == NP - 1), inc=(k == NP - 1))
            if ds == 1:
                gate_mil.append(t_g)
            t_gb = act(c, gate_bf[:, do * CW:(do + 1) * CW], gb_bank[:, 0:CW],
                       AF.Sigmoid, bias=bgate[:, do:do + 1],
                       w=[t_g, colstate[col][6]])
            t_gb_hist.append(t_gb)

        def slab2a(col):
            off = col * CW
            st = colstate[col]
            t_ccb, t_y = st[0], st[1]
            gate_mil, t_gb_hist = st[5]
            c.wait("pe", t_ccb, t_y)
            c.wait("pe", t_gb_hist[-2])
            t_h_hist = []
            t_hx1_hist = []
            for do in range(NP):
                hb = pb[2] if do % 2 == 0 else pb[4]
                if do >= 2:
                    c.wait("pe", t_hx1_hist[do - 2])
                t_h = mm(c, hb[:, 0:CW], woutsb[:, do * 128:(do + 1) * 128],
                         ccb[:], start=True, stop=True, inc=True)
                t_h_hist.append(t_h)
                tt(c, slot(7), gate_bf[:, do * CW:(do + 1) * CW], hb[:, 0:CW], MUL,
                   w=[t_gb_hist[do], t_h])
                t_hx1 = tt(c, u32[:, do * CW:(do + 1) * CW],
                           u32[:, do * CW:(do + 1) * CW], slot(7), ADD)
                t_hx1_hist.append(t_hx1)
            gate_mil.append(t_h_hist[-1])
            tt(c, Rview(off), Rview(off), colview(u32), ADD, w=[t_hx1])
            col_stats(lambda p: u32[:, p * CW:(p + 1) * CW], 0, 1,
                      extra=[c.tok("dve")])
            rstd_into(1, 2)
            tt(c, slot(5), slot(0), slot(2), MUL)
            uv2 = colview(u32)
            tt(c, uv2, uv2, bcast(slot(2)), MUL)
            t_x2p = tt(c, uv2, uv2, bcast(slot(5)), SUB)
            for p in range(NP):
                t_x2n = act(c, x2n_bf[:, p * CW:(p + 1) * CW], u32[:, p * CW:(p + 1) * CW],
                            AF.Identity, scale=g2[:, p:p + 1], bias=b2[:, p:p + 1],
                            w=[t_x2p])
            prevA["gate_pe"] = gate_mil[-1]
            prevA["gate_act"] = t_gb_hist[-1]
            prevA["x1_dve"] = t_hx1
            prevA["u32_act"] = c.tok("act")
            st[2] = t_x2n
            st[3] = gate_mil
            st[4] = t_hx1

        def slab2b1(col):
            off = col * CW
            st = colstate[col]
            t_x2n, gate_mil = st[2], st[3]
            c.wait("pe", t_x2n, t_w1)
            t_sil = c.tok("act")
            m1_mil = []
            for grp in range(16):
                wslice = wbig[:, grp % 2 * 2048:(grp % 2 + 1) * 2048]
                gw = ([m1_mil[2 * (grp - 2) + 1], t_w1] if grp >= 2
                      else [t_w1, gate_mil[-1]])
                tks = []
                for sub in range(4):
                    tks.append(c.dma(
                        "sp", wbig[:, grp % 2 * 2048 + sub * 512:
                                   grp % 2 * 2048 + (sub + 1) * 512],
                        w1s[:, grp * 2048 + sub * 512:grp * 2048 + (sub + 1) * 512],
                        waits=gw))
                c.wait("pe", *tks)
                for hs in range(2):
                    h = grp * 2 + hs
                    mb = (pb[0], pb[1], pb[4])[h % 3]
                    if h >= 3:
                        c.wait("pe", t_sil_hist[h - 3])
                    for k in range(NP):
                        t_m1 = mm(c, mb[:, 0:CW],
                                  wslice[:, (k * 256 + hs * 128):(k * 256 + hs * 128 + 128)],
                                  x2n_bf[:, k * CW:(k + 1) * CW], start=(k == 0),
                                  stop=(k == NP - 1), inc=(k == NP - 1))
                    m1_mil.append(t_m1)
                    t_sil = act(c, hid_bf[:, h * CW:(h + 1) * CW], mb[:, 0:CW],
                                AF.Silu, bias=bm1[:, h:h + 1], w=[t_m1])
                    if h == 0:
                        t_sil_hist = []
                    t_sil_hist.append(t_sil)
            colstate[col][2] = (t_sil, gate_mil, m1_mil)

        def slab2b2(col):
            off = col * CW
            t_sil, gate_mil, m1_mil = colstate[col][2]
            colstate.pop(col)
            prevA["m1_last"] = m1_mil[-1]
            c.wait("pe", t_sil, t_w2)
            t_radd = None
            m2_mil = []
            for do in range(NP):
                buf = do % 2
                if do >= 2:
                    c.wait("pe", t_radd_hist[do - 2])
                for q in range(4):
                    qi = do * 4 + q
                    wslice = wbig2[:, (qi % 2) * 1024:(qi % 2 + 1) * 1024]
                    gw = ([m2_mil[qi - 2], t_w2] if qi >= 2
                          else [t_w2, t_sil, gate_mil[-1]])
                    tks = []
                    for sub in range(2):
                        tks.append(c.dma(
                            "sp", wbig2[:, (qi % 2) * 1024 + sub * 512:
                                        (qi % 2) * 1024 + (sub + 1) * 512],
                            w2s[:, qi * 1024 + sub * 512:qi * 1024 + (sub + 1) * 512],
                            waits=gw))
                    c.wait("pe", *tks)
                    for k8 in range(8):
                        k = q * 8 + k8
                        t_m2 = mm(c, pb[0 + buf][:, 0:CW], wslice[:, k8 * 128:(k8 + 1) * 128],
                                  hid_bf[:, k * CW:(k + 1) * CW], start=(k == 0),
                                  stop=(k == 31), inc=(k8 == 7))
                    m2_mil.append(t_m2)
                t_radd = stt(c, Rcol(do, off), pb[0 + buf][:, 0:CW], bmlp2v[:, do:do + 1],
                             Rcol(do, off), ADD, ADD, w=[t_m2])
                if do == 0:
                    t_radd_hist = []
                t_radd_hist.append(t_radd)
                if col + 1 in colstate:
                    gate_step(col + 1, do)
            prevA["mlp2_dve"] = t_radd

        slab1(0)
        slab1b(0)
        for do in range(NP):
            gate_step(0, do)
        for col in range(NCOL):
            slab2a(col)
            if col + 1 < NCOL:
                slab1(col + 1)
            slab2b1(col)
            if col + 1 < NCOL:
                slab1b(col + 1)
            slab2b2(col)
        c.barrier()

    def prologue_F():
        t_vF = c.dma("pool", vF[:], vecsF[:, :])
        t_b1f = c.dma("pool", b1f_sb[:], b1fd[:, :])
        t_wsc0 = c.dma("pool", wdT[:].rearrange("p (k m) -> p k m", k=8),
                       dirsT[:, :].rearrange("(k p) m -> p k m", p=128))
        t_wdir = c.dma("pool", wdirs[:], dirsL[:, :])
        ts(c, wdirs[:], wdirs[:], smallsB[0:64, 9:10], MUL, w=[t_wdir, t_sm])
        t_wc1 = c.dma("pool", wc1[:].rearrange("p (k m) -> p k m", k=8),
                      cw1[:, :].rearrange("(k p) m -> p k m", p=128))
        t_wc2 = c.dma("pool", wc2[:], cw2[:, :])
        t_w1f = c.dma("pool", ww1f[:], w1f[:, :])
        t_w2b = c.dma("pool", ww2blk[:].rearrange("p (k m) -> p k m", k=8),
                      w2blk[:, :].rearrange("(k p) m -> p k m", p=128))
        t_io = c.dma("pool", iota64[:], consts[:, 272:336])
        t_cl = c.dma("pool", clus64[:], consts[:, 336:400])
        t_b2 = c.dma("pool", b2tm[:], b2tmd[:, :])
        t_idf = c.dma("pool", identf[:], consts[:, 128:256])
        sraw = hid_bf[:, 0:1024].bitcast(F32)
        qa = hid_bf[:, 1024:2048].bitcast(F32)
        t_sr = c.dma("pool", sraw[:].rearrange("p (k m) -> p k m", k=8),
                     sigsT_raw[:, :].rearrange("(k p) m -> p k m", p=128))
        c.wait("act", t_sr)
        act(c, qa[:], sraw[:], AF.Sign, bias=f32(-0.3))
        t_q1 = c.tok("act")
        ts(c, qa[:], qa[:], 0.0, MAXO, w=[t_q1])
        act(c, sraw[:], sraw[:], AF.Sign, bias=f32(-0.3), scale=f32(-1.0),
            w=[c.tok("dve")])
        t_q2 = c.tok("act")
        ts(c, sraw[:], sraw[:], 0.0, MAXO, w=[t_q2])
        wscv = wsc[:].rearrange("p (k m) -> p k m", m=72)
        qv = qa[:].rearrange("p (k m) -> p k m", m=64)
        sv_ = sraw[:].rearrange("p (k m) -> p k m", m=64)
        tt(c, qv, qv, sv_, SUB)
        t_q = tt(c, wscv[:, :, 0:64], qv, qv, MINO)
        qg = qa[:].rearrange("p (k cl t) -> p k cl t", k=8, cl=8)
        red = hid_bf[:, 2048:2176].bitcast(F32)
        rv = red[:].rearrange("p (k cl) -> p k cl", k=8)
        vred(c, rv, qg, ADD)
        t_red = ts(c, red[:], red[:], 0.125, MUL)
        t_cs = act(c, red[:], red[:], AF.Sign, w=[t_red])
        tt(c, wscv[:, :, 64:72], rv, rv, MINO, w=[t_cs])
        c.wait("dve", t_sm)
        vcopy(c, knotsB[:, 0:1], smallsB[:, 0:1])
        ts(c, knotsB[:, 3:4], smallsB[:, 8:9], 1e-6, ADD)
        c.emit("dve", lambda v: v.reciprocal(out=knotsB[:, 1:2], in_=knotsB[:, 3:4]))
        vcopy(c, knotsB[:, 2:3], smallsB[:, 9:10])
        tt(c, dB[:], smallsB[:, 1:8], smallsB[:, 0:7], SUB)
        tt(c, calF[:, 0:6], dB[:, 0:6], dB[:, 1:7], SUB)
        vcopy(c, calF[:, 6:7], dB[:, 6:7])
        ts(c, calF[:, 0:7], calF[:, 0:7], 7.0, MUL)
        ts(c, calF[:, 8:10], vF[:, 16:18], float(1.0 / np.sqrt(2.0)), MUL,
           w=[t_vF])
        rowsb = hid_bf[:, 4096:5120].bitcast(F32)[0:1, :]
        c.wait("pe", t_wsc0, t_idf, c.tok("dve"), c.tok("act"))
        for k in range(NP):
            t_dm = mm(c, pb[0][0:64, 0:1], wdT[:, k * 64:(k + 1) * 64],
                      ones_bf[:, 0:1], start=(k == 0), stop=(k == NP - 1),
                      inc=(k == NP - 1))
        t_sq2 = c.dma("pool", qa[:, 0:512].rearrange("p (k m) -> p k m", k=8),
                      dirsT[:, :].rearrange("(k p) m -> p k m", p=128),
                      waits=[c.tok("dve"), c.tok("act")])
        t_sqa = act(c, sqbf[:, 0:512], qa[:, 0:512], AF.Square, w=[t_sq2])
        c.wait("pe", t_sqa)
        for k in range(NP):
            t_ds = mm(c, pb[1][0:64, 0:1], sqbf[:, k * 64:(k + 1) * 64],
                      ones_bf[:, 0:1], start=(k == 0), stop=(k == NP - 1),
                      inc=(k == NP - 1))
        ts(c, drows[:, 0:1], pb[0][0:64, 0:1], f32(1.0 / D), MUL, w=[t_dm, t_ds])
        ts(c, drows[:, 1:2], pb[1][0:64, 0:1], f32(1.0 / D), MUL)
        c.wait("pe", c.tok("dve"))
        transp(c, pb[2][0:1, 0:64], drows[:, 0:1], identf[0:64, 0:64], inc=True)
        t_tr = c.tok("pe")
        vcopy(c, rowsb[:, 0:64], pb[2][0:1, 0:64], w=[("pe", t_tr[1])])
        transp(c, pb[2][0:1, 0:64], drows[:, 1:2], identf[0:64, 0:64], inc=True,
               w=[c.tok("dve")])
        t_tr2 = c.tok("pe")
        vcopy(c, rowsb[:, 64:128], pb[2][0:1, 0:64], w=[("pe", t_tr2[1])])
        c.wait("pe", c.tok("dve"))
        mm(c, pb[3][:, 0:64], onesf[0:1, :], rowsb[0:1, 0:64], start=True, stop=True)
        t_bb = mm(c, pb[4][:, 0:64], onesf[0:1, :], rowsb[0:1, 64:128], start=True,
                  stop=True, inc=True)
        tb2 = tblp.rearrange("p (c w) -> p c w", c=2)
        vcopy(c, tb2[:, :, 128:192],
              pb[3][:, 0:64].unsqueeze(1).broadcast_to((128, 2, 64)), w=[t_bb])
        vcopy(c, tb2[:, :, 192:256],
              pb[4][:, 0:64].unsqueeze(1).broadcast_to((128, 2, 64)))
        c.wait("dve", t_b2)
        vcopy(c, tb2[:, :, 256:320],
              b2tm[:].unsqueeze(1).broadcast_to((128, 2, 64)))
        c.barrier()

    def sublayer_F(tag, fcols=None):
        gamma = vF[:, 0:8]; beta_ = vF[:, 8:16]; cb1 = vF[:, 16:18]
        cb2c = vF[0:2, 18:19]
        if fcols is None:
            fcols = NCOL
        tbz = tblp.rearrange("p (c w) -> p c w", c=2)
        tbl4 = tblp.rearrange("p (c t w) -> p c t w", c=2, t=5)
        pr4 = prodp.rearrange("p (c t w) -> p c t w", c=2, t=5)
        sels = rowsP[:, 16:26].rearrange("p (c t) -> p c t", c=2)

        def q(j):
            return rowsP[:, 2 * j:2 * j + 2].unsqueeze(2)

        def qb(j, n):
            return q(j).broadcast_to((128, 2, n))
        mskp = mskg.rearrange("p (c w) -> p c w", c=2)
        ohp = oh[:, 0:128]
        ohv3 = ohp.rearrange("p (c w) -> p c w", c=2)
        pr2f = oh[:, 128:256].rearrange("p (c w) -> p c w", c=2)
        identb = identf[:].unsqueeze(1).broadcast_to((128, 2, 128))
        rr3 = rhsd_r.rearrange("p (c w) -> p c w", c=2)
        rm3 = rhsd_m.rearrange("p (c w) -> p c w", c=2)
        ex3 = extrF.rearrange("p (c w) -> p c w", c=2)
        fst = {}

        fstats_bs = (pb[1][:, CW:2 * CW], pb[3][:, CW:2 * CW])

        def fEs(col):
            off = col * CW
            c.wait("pe", state.get("bc_guard"), state.get("f_guard"))
            fst[("S", col)] = col_stats_mm(lambda p: Rcol(p, off), bs=fstats_bs)

        def fEa(col):
            off = col * CW
            col_stats_fin(0, 1, fst.pop(("S", col)), bs=fstats_bs)
            rstd_into(1, 2)
            tt(c, slot(5), slot(0), slot(2), MUL)
            tt(c, slot(7), slot(2), slot(2), MUL)
            tt(c, slot(3), slot(1), slot(7), MUL)
            uv = colview(u32)
            tt(c, uv, Rview(off), bcast(slot(2)), MUL)
            tt(c, uv, uv, bcast(slot(5)), SUB)
            t_z = vcopy(c, colview(xn_bf), uv, w=[state.get("xn_guard")])
            c.wait("pe", t_z, state.get("pb2_guard"))
            for k in range(NP):
                mm(c, pb[2][0:72, 0:CW], wsc[:, k * 72:(k + 1) * 72],
                   xn_bf[:, k * CW:(k + 1) * CW], start=(k == 0), stop=(k == NP - 1))
            for k in range(NP):
                t_zd = mm(c, pb[2][0:64, CW:2 * CW], wdT[:, k * 64:(k + 1) * 64],
                          xn_bf[:, k * CW:(k + 1) * CW], start=(k == 0),
                          stop=(k == NP - 1), inc=(k == NP - 1))
            c.wait("pe", state.get("pb4_guard"))
            for half in range(2):
                for k in range(NP):
                    t_c1 = mm(c, pb[4][:, half * CW:(half + 1) * CW],
                              wc1[:, k * 256 + half * 128:k * 256 + half * 128 + 128],
                              xn_bf[:, k * CW:(k + 1) * CW], start=(k == 0),
                              stop=(k == NP - 1), inc=(k == NP - 1 and half == 1))
            state["xn_guard"] = t_c1
            for half in range(2):
                t_h1 = act(c, h1_bf[:, half * CW:(half + 1) * CW],
                           pb[4][:, half * CW:(half + 1) * CW],
                           AF.Gelu, bias=cb1[:, half:half + 1],
                           w=[t_c1, state.get("c2_guard")] if half == 0 else [t_c1])
            state["pb4_guard"] = t_h1
            t_zde = act(c, zd_bf[:], pb[2][0:64, CW:2 * CW], AF.Identity,
                        w=[t_zd, state.get("tp_guard")])
            t_sg = act(c, uu[:], pb[2][0:72, 0:CW], AF.Sigmoid,
                       scale=knotsB[0:72, 1:2], w=[t_zd, state.get("uu_guard")])
            state["pb2_guard"] = t_sg
            fst[("Ea", col)] = (t_sg, t_h1, t_zde)

        def fEb(col):
            par = col % 2
            gtm = gtmP[par]
            rowsE = rowsEP[par]
            t_sg, t_h1, t_zde = fst.pop(("Ea", col))
            for j, sl in enumerate((2, 5, 3)):
                tt(c, ex3, identb, slot(sl).rearrange("p (c w) -> p c w", c=2), MUL)
                vred(c, rowsE[:, 2 * j:2 * j + 2].unsqueeze(2), ex3, ADD)
            t_cal = ts(c, gts[:], uu[:], calF[0:72, 6:7], MUL, knotsB[0:72, 0:1],
                       ADD, w=[t_sg])
            for m in range(1, 7):
                ts(c, slot(7)[0:72, :], uu[:], float(m / 7.0), MINO)
                t_cal = stt(c, gts[:], slot(7)[0:72, :], calF[0:72, m - 1:m], gts[:],
                            MUL, ADD)
            state["uu_guard"] = t_cal
            t_gts = vcopy(c, gts_bf[:], gts[:], w=[state.get("tp_guard")])
            c.wait("pe", t_h1)
            for k2 in range(2):
                t_c2 = mm(c, pb[5][0:2, 0:CW], wc2[:, k2 * 2:(k2 + 1) * 2],
                          h1_bf[:, k2 * CW:(k2 + 1) * CW], start=(k2 == 0),
                          stop=(k2 == 1), inc=(k2 == 1))
            t_cp2 = act(c, comp_bf[:], pb[5][0:2, 0:CW], AF.Tanh,
                        bias=cb2c, w=[t_c2, state.get("hh_guard")])
            state["c2_guard"] = t_c2
            c.wait("pe", t_cp2, t_gts, state.get("bc_guard"))
            t_hh = None
            t_hh_hist = []
            for gch in range(8):
                if gch >= 2:
                    c.wait("pe", t_hh_hist[gch - 2])
                t_hmm = mm(c, pb[5 + gch % 2][:, 0:CW], ww1f[:, gch * 128:(gch + 1) * 128],
                           comp_bf[:], start=True, stop=True, inc=True)
                t_hh = act(c, hid_bf[:, gch * CW:(gch + 1) * CW], pb[5 + gch % 2][:, 0:CW],
                           AF.Relu, bias=b1f_sb[:, gch:gch + 1],
                           w=[t_hmm, state.get("magmm_guard")] if gch == 0
                             else [t_hmm])
                t_hh_hist.append(t_hh)
                if gch == 7:
                    state["hh_guard"] = t_hmm
            c.wait("pe", t_gts, t_zde, state.get("pbb_guard2"))
            for tch in range(2):
                transp(c, pbb[:, tch * 80:tch * 80 + 72],
                       gts_bf[:, tch * 128:(tch + 1) * 128], ident_bf[0:72, 0:72])
                transp(c, pbb[:, 160 + tch * 64:160 + (tch + 1) * 64],
                       zd_bf[:, tch * 128:(tch + 1) * 128], ident_bf[0:64, 0:64],
                       inc=(tch == 1))
            t_tp = c.tok("pe")
            state["tp_guard"] = t_tp
            vcopy(c, gtm[:, 0:72], pbb[:, 0:72], w=[t_tp])
            vcopy(c, gtm[:, 80:152], pbb[:, 80:152])
            t_ztm = vcopy(c, tbz[:, :, par * 64:par * 64 + 64],
                          pbb[:, 160:288].rearrange("p (c w) -> p c w", c=2))
            c.wait("pe", t_hh, t_ztm, state.get("mag_evac"))
            for tch in range(2):
                for gch in range(8):
                    t_mg = mm(c, pb[0][:, tch * 64:(tch + 1) * 64],
                              hid_bf[:, gch * CW + tch * 128:gch * CW + (tch + 1) * 128],
                              ww2blk[:, gch * 64:(gch + 1) * 64], start=(gch == 0),
                              stop=(gch == 7), inc=(gch == 7 and tch == 1))
            t_mge = act(c, magsbP[par][:], pb[0][:, 0:128], AF.Identity,
                        w=[t_mg, fst.get(("mag_read", par))])
            state["mag_evac"] = t_mge
            state["magmm_guard"] = t_mg
            fst[("E", col)] = t_mge

        def fD(col):
            off = col * CW
            par = col % 2
            gtm = gtmP[par]
            rowsE = rowsEP[par]
            t_mge = fst.pop(("E", col))
            for tch in range(2):
                vmax8(c, mxcB[:, tch * 8:tch * 8 + 8],
                      gtm[:, tch * 80 + 64:tch * 80 + 72])
                vmaxidx(c, idxu[:, tch * 8:tch * 8 + 8],
                        mxcB[:, tch * 8:tch * 8 + 8],
                        gtm[:, tch * 80 + 64:tch * 80 + 72])
            vcopy(c, idxfB[:, 0:16], idxu[:, 0:16])
            cidxb = idxfB[:, 0:16].rearrange("p (c w) -> p c w", c=2)[:, :, 0:1]
            tt(c, mskp, clus64[:].unsqueeze(1).broadcast_to((128, 2, 64)),
               cidxb.broadcast_to((128, 2, 64)), AluOpType.is_equal)
            g64p = gtm[:, 0:160].rearrange("p (c w) -> p c w", c=2)[:, :, 0:64]
            stt(c, mskp, mskp, 1000.0, g64p, MUL, ADD)
            for tch in range(2):
                vmax8(c, mxtF[:, tch * 8:tch * 8 + 8],
                      mskg[:, tch * 64:(tch + 1) * 64])
                vmaxidx(c, idxu[:, 16 + tch * 8:24 + tch * 8],
                        mxtF[:, tch * 8:tch * 8 + 8],
                        mskg[:, tch * 64:(tch + 1) * 64])
            vcopy(c, idxfB[:, 16:32], idxu[:, 16:32])
            tidxb = idxfB[:, 16:32].rearrange("p (c w) -> p c w", c=2)[:, :, 0:1]
            tt(c, ohv3, iota64[:].unsqueeze(1).broadcast_to((128, 2, 64)),
               tidxb.broadcast_to((128, 2, 64)), AluOpType.is_equal)
            tt(c, pr4, tbl4, ohv3.unsqueeze(2).broadcast_to((128, 2, 5, 64)), MUL)
            vred(c, sels.unsqueeze(3), pr4, ADD)
            zselv = sels[:, :, par:par + 1]
            tt(c, pr2f, ohv3,
               magsbP[par][:].rearrange("p (c w) -> p c w", c=2), MUL, w=[t_mge])
            fst[("mag_read", par)] = vred(c, q(6), pr2f, ADD)
            tt(c, q(1), q(6), sels[:, :, 4:5], ADD)
            ts(c, rowsP[:, 2:4], rowsP[:, 2:4], knotsB[:, 2:3], MUL)
            vcopy(c, rowsP[:, 0:2], rowsE[:, 4:6], w=[state.get("rp_guard")])
            stt(c, q(3), q(1), float(2.0 / D), zselv, MUL, MUL)
            tt(c, q(0), q(0), q(3), ADD)
            tt(c, q(3), q(1), q(1), MUL)
            tt(c, q(3), q(3), sels[:, :, 3:4], MUL)
            tt(c, q(0), q(0), q(3), ADD)
            tt(c, q(2), q(1), sels[:, :, 2:3], MUL)
            tt(c, q(3), q(2), q(2), MUL)
            t_varp = tt(c, q(0), q(0), q(3), SUB)
            act(c, rowsP[:, 8:10], rowsP[:, 0:2], AF.Ln, bias=EPS, w=[t_varp])
            t_rp = act(c, rowsP[:, 8:10], rowsP[:, 8:10], AF.Exp, scale=f32(-0.5))
            state["rp_guard"] = t_rp
            tt(c, rowsP[:, 12:14], rowsE[:, 0:2], rowsP[:, 8:10], MUL, w=[t_rp])
            tt(c, rowsP[:, 14:16], rowsE[:, 2:4], rowsP[:, 4:6], ADD)
            tt(c, rowsP[:, 14:16], rowsP[:, 14:16], rowsP[:, 8:10], MUL)
            t_ptm = tt(c, ptm[:].rearrange("p (c w) -> p c w", c=2), ohv3,
                       qb(4, 64), MUL, w=[state.get("ptm_guard")])
            t_dr = tt(c, rr3, identb, qb(6, 128), MUL)
            t_dm2 = tt(c, rm3, identb, qb(7, 128), MUL)
            c.wait("pe", t_dr, t_dm2, state.get("bc_guard"), state["stats_dve"])
            mm(c, pb[5][:, CW:2 * CW], onesf[:], rhsd_r[:, :], start=True, stop=True)
            t_bc2 = mm(c, pb[6][:, CW:2 * CW], onesf[:], rhsd_m[:, :], start=True,
                       stop=True, inc=True)
            c.wait("pe", t_ptm)
            for tch in range(2):
                transp(c, pbb[0:64, 288 + tch * 128:288 + (tch + 1) * 128],
                       ptm[:, tch * 64:(tch + 1) * 64], ident_bf[:], inc=(tch == 1))
            t_pt = c.tok("pe")
            state["ptm_guard"] = t_pt
            vcopy(c, pT_bf[:, 0:128], pbb[0:64, 288:288 + 128],
                  w=[t_pt, state.get("pT_guard")])
            t_pte = vcopy(c, pT_bf[:, 128:256], pbb[0:64, 416:416 + 128])
            state["pbb_guard2"] = t_pte
            vcopy(c, slot(4), pb[5][:, CW:2 * CW], w=[t_bc2])
            t_bce = vcopy(c, slot(6), pb[6][:, CW:2 * CW])
            state["bc_guard"] = t_bce
            c.wait("pe", t_pte)
            t3_hist = []
            for do in range(NP):
                tmp = rhsd_r if do % 2 == 0 else rhsd_m
                fb = pb[3] if do % 2 == 0 else pb[1]
                if do >= 2:
                    c.wait("pe", t3_hist[do - 2])
                elif state.get("f_guard") is not None:
                    c.wait("pe", state["f_guard"])
                t_f = mm(c, fb[:, 0:CW], wdirs[:, do * 128:(do + 1) * 128],
                         pT_bf[:], start=True, stop=True, inc=True)
                tt(c, tmp[:, :], Rcol(do, off), slot(4), MUL,
                   w=[t_bce] if do == 0 else ())
                tt(c, tmp[:, :], tmp[:, :], slot(6), SUB)
                t3 = tt(c, tmp[:, :], tmp[:, :], fb[:, 0:CW], ADD, w=[t_f])
                t3_hist.append(t3)
                ts(c, tmp[:, :], tmp[:, :], gamma[:, do:do + 1], MUL,
                   beta_[:, do:do + 1], ADD)
                tt(c, Rcol(do, off), Rcol(do, off), tmp[:, :], ADD)
            state["f_guard"] = t3_hist[-1]
            state["pT_guard"] = t_f

        fEs(0)
        fEa(0)
        for i in range(fcols):
            if i + 1 < fcols:
                fEs(i + 1)
            fEb(i)
            if i + 1 < fcols:
                fEa(i + 1)
            if i >= 1:
                fD(i - 1)
        fD(fcols - 1)
        c.barrier()

    def sublayer_B():
        t_wq = c.dma("pool", qbs[:, :], qTd[:, :])
        t_wcb = c.dma("pool", cbs[:, :], combd[:, :])
        t_st = None
        for col in range(NCOL):
            off = col * CW
            col_stats(lambda p: Rcol(p, off), 0, 1, extra=[t_st])
            rstd_into(1, 2)
            tt(c, slot(5), slot(0), slot(2), MUL)
            xv = colview(xn_bf)
            tt(c, xv, Rview(off), bcast(slot(2)), MUL)
            t_z = tt(c, xv, xv, bcast(slot(5)), SUB)
            t_st = c.dma("sp", zscr[:, off:off + CW].rearrange("(k p) t -> p k t", p=128),
                         xn_bf[:].rearrange("p (k t) -> p k t", k=NP), waits=[t_z])
            c.wait("dve", t_st)
        c.barrier()
        NTCH = T // 128
        gmil = []
        t_zl = None
        for fh in range(2):
            for tch in range(NTCH):
                gi = fh * NTCH + tch
                buf = gi % 2
                zt = ztT[:, buf * 1024:(buf + 1) * 1024]
                tw = [t_zl] if t_zl is not None else []
                if gi >= 2:
                    tw.append(gmil[gi - 2])
                t_zl = c.dma("sp", zt, zscr[:, tch * 128:(tch + 1) * 128],
                             transpose=True, waits=tw)
                c.wait("pe", t_zl)
                for q in range(4):
                    t_gq = mm(c, pb[q][:, fh * 256:fh * 256 + 256],
                              zt[:, q * 256 + fh * 128:q * 256 + fh * 128 + 128],
                              zt[:, q * 256:(q + 1) * 256],
                              start=(tch == 0), stop=(tch == NTCH - 1),
                              inc=(q == 3))
                gmil.append(t_gq)
            t_gr = c.tok("pe")
            for q in range(4):
                vcopy(c, gram[:, (q * 2 + fh) * 256:(q * 2 + fh) * 256 + 256],
                      pb[q][:, fh * 256:fh * 256 + 256], w=[t_gr])
        t_gf = vcopy(c, gram_bf, gram)
        c.barrier()
        for a in range(4):
            t_wqa = c.dma("sp", wqa, qbs[:, a * 2048:(a + 1) * 2048],
                          waits=[("pe", c.cnt["pe"]), t_wq])
            t_wca = c.dma("sp", wcba, cbs[:, a * 2048:(a + 1) * 2048],
                          waits=[t_wcb, t_gf, ("pe", c.cnt["pe"])])
            c.wait("pe", t_wqa, t_wca)
            t1hist = []
            for gi, (q, gch) in enumerate([(q, g) for q in range(4) for g in range(2)]):
                if gi >= 2:
                    c.wait("pe", t1hist[gi - 2])
                for fh in range(2):
                    t_t1 = mm(c, pb[4 + gi % 2][:, 0:256],
                              gram_bf[:, (q * 2 + fh) * 256 + gch * 128:
                                      (q * 2 + fh) * 256 + gch * 128 + 128],
                              wqa[:, (q * 2 + fh) * 256:
                                  (q * 2 + fh) * 256 + 256],
                              start=(fh == 0), stop=(fh == 1),
                              inc=(fh == 1))
                t1hist.append(act(c, t1T[:, (q * 2 + gch) * 256:(q * 2 + gch) * 256 + 256],
                                  pb[4 + gi % 2][:, 0:256], AF.Identity, w=[t_t1]))
            t_t1e = c.tok("act")
            c.wait("pe", t_t1e)
            for ech in range(2):
                if ech == 1:
                    c.wait("pe", c.tok("act"))
                for k in range(8):
                    t_M = mm(c, pb[6][:, ech * 256:ech * 256 + 256],
                             t1T[:, k * 256 + ech * 128:k * 256 + ech * 128 + 128],
                             wcba[:, k * 256:(k + 1) * 256],
                             start=(k == 0), stop=(k == 7), inc=(k == 7))
                act(c, Msb[:, (a * 2 + ech) * 256:(a * 2 + ech) * 256 + 256],
                    pb[6][:, ech * 256:ech * 256 + 256], AF.Identity, w=[t_M])
            c.wait("pe", c.tok("act"))
        c.barrier()
        for col in range(NCOL // 2):
            off = col * CW
            t_zl2 = c.dma("sp", xn_bf[:].rearrange("p (k t) -> p k t", k=NP),
                          zscr[:, off:off + CW].rearrange("(k p) t -> p k t", p=128))
            c.wait("pe", t_zl2)
            t_ap_hist = []
            for do in range(NP):
                a, gh = do // 2, do % 2
                if do >= 2:
                    c.wait("pe", t_ap_hist[do - 2])
                for ech in range(2):
                    t_ap = mm(c, pb[do % 2][:, 0:CW],
                              Msb[:, (a * 2 + ech) * 256 + gh * 128:
                                  (a * 2 + ech) * 256 + gh * 128 + 128],
                              xn_bf[:, (a * 2 + ech) * CW:(a * 2 + ech + 1) * CW],
                              start=(ech == 0), stop=(ech == 1), inc=(ech == 1))
                t_r = tt(c, Rcol(do, off), Rcol(do, off), pb[do % 2][:, 0:CW], ADD,
                         w=[t_ap])
                t_ap_hist.append(t_r)
            c.barrier()

    order = [p for p in ("A", "F1", "B", "F2") if p in parts]
    did_A = False
    did_Fpro = False
    for part in order:
        if part == "A":
            sublayer_A()
        elif part == "B":
            sublayer_B()
        elif part in ("F1", "F2"):
            if not did_Fpro:
                prologue_F()
                did_Fpro = True
            sublayer_F(part, fcols=NCOL // 2 if part == "F2" else None)

    c.barrier()
    outs = []
    for p in range(NP):
        outs.append(c.dma("sp", Rout[p * 128:(p + 1) * 128, :],
                          R[:, p * T:p * T + T // 2]))
    c.wait("sp", *outs)
    c.play()
    es.close()
    return nc



def _host_prep(inputs):
    ap = {}
    g = lambda k: np.ascontiguousarray(np.asarray(inputs[k], dtype=np.float32))
    ones = np.ones((128, 128), np.float32)
    ident = np.eye(128, dtype=np.float32)
    smalls = np.zeros((16,), np.float32)
    smalls[0:8] = g("L0_ffn_knot_values")
    smalls[8] = g("L0_ffn_temperature")[0]
    smalls[9] = g("L0_ffn_output_scale")[0]
    smalls[10:12] = g("L0_ffn_compress_b2")
    iota64 = np.broadcast_to(np.arange(64, dtype=np.float32), (128, 64))
    clus64 = np.broadcast_to((np.arange(64) // 8).astype(np.float32), (128, 64))
    consts = np.concatenate([ones, ident, np.broadcast_to(smalls, (128, 16)),
                             iota64, clus64], axis=1)
    ap["consts"] = np.ascontiguousarray(consts.astype(np.float32))
    ap["w_inT"] = np.ascontiguousarray(g("L0_attn_a_w_in").T)
    ap["w_outT"] = np.ascontiguousarray(g("L0_attn_a_w_out").T)

    def group_w(wT, grpcols):
        K8, M = wT.shape[0] // 128, wT.shape[1]
        ngrp = M // grpcols
        out = wT.reshape(K8, 128, ngrp, grpcols).transpose(1, 2, 0, 3)
        return np.ascontiguousarray(out.reshape(128, ngrp * K8 * grpcols))

    ap["w_gateT"] = group_w(g("L0_attn_a_w_gate").T, 256)
    ap["w1T"] = group_w(g("L0_attn_a_w_mlp1").T, 256)
    ap["w2T"] = group_w(g("L0_attn_a_w_mlp2").T, 128)
    vecsA = np.zeros((128, 48), np.float32)
    for i, k in enumerate(["ln1_g", "ln1_b", "ln2_g", "ln2_b", "b_gate", "b_mlp2"]):
        vecsA[:, 8 * i:8 * (i + 1)] = g(f"L0_attn_a_{k}").reshape(8, 128).T
    ap["vecsA"] = vecsA
    ap["bmlp1"] = np.ascontiguousarray(g("L0_attn_a_b_mlp1").reshape(32, 128).T)
    ap["eig"] = np.ascontiguousarray(
        np.stack([g("L0_attn_a_log_decay"), g("L0_attn_a_frequency")], axis=1))
    cpl = np.zeros((64, 64), np.float32)
    cp = g("L0_attn_a_coupling")
    for h in range(4):
        cpl[h * 16:(h + 1) * 16, h * 16:(h + 1) * 16] = cp[h].T
    ap["couple"] = cpl
    ap["sigsT_raw"] = np.ascontiguousarray(g("L0_ffn_signatures_raw").T)
    ap["dirsT"] = np.ascontiguousarray(g("L0_ffn_directions").T)
    ap["dirsL"] = g("L0_ffn_directions")
    ap["cw1"] = g("L0_ffn_compress_w1")
    cw2 = g("L0_ffn_compress_w2")
    ap["cw2"] = np.ascontiguousarray(cw2.reshape(2, 128, 2).transpose(1, 0, 2).reshape(128, 4))
    ap["w1f"] = np.ascontiguousarray(g("L0_ffn_spline_w1").transpose(1, 0, 2).reshape(2, 1024))
    w2b = np.zeros((1024, 64), np.float32)
    sw2 = g("L0_ffn_spline_w2")
    for t in range(64):
        w2b[t * 16:(t + 1) * 16, t] = sw2[t, :, 0]
    ap["w2blk"] = w2b
    vecsF = np.zeros((128, 20), np.float32)
    vecsF[:, 0:8] = g("L0_ffn_norm_gamma").reshape(8, 128).T
    vecsF[:, 8:16] = g("L0_ffn_norm_beta").reshape(8, 128).T
    vecsF[:, 16:18] = g("L0_ffn_compress_b1").reshape(2, 128).T
    vecsF[0:2, 18] = g("L0_ffn_compress_b2")
    ap["vecsF"] = vecsF
    ap["b2tmd"] = np.ascontiguousarray(
        np.broadcast_to(g("L0_ffn_spline_b2")[:, 0], (128, 64)).astype(np.float32))
    qs = g("L0_attn_b_queries")
    qT = np.zeros((128, 32 * 256), np.float32)
    for a in range(4):
        for q in range(4):
            blk = qs[a][:, q * 256:(q + 1) * 256].T
            for fch in range(2):
                idx = a * 8 + q * 2 + fch
                qT[:, idx * 256:(idx + 1) * 256] = blk[fch * 128:(fch + 1) * 128, :]
    ap["qTd"] = qT
    cb = g("L0_attn_b_combiners")
    cbd = np.zeros((128, 32 * 256), np.float32)
    for a in range(4):
        for k in range(8):
            idx = a * 8 + k
            cbd[:, idx * 256:(idx + 1) * 256] = cb[a][k * 128:(k + 1) * 128, :]
    ap["combd"] = cbd
    ap["b1fd"] = np.ascontiguousarray(g("L0_ffn_spline_b1").reshape(64, 16).reshape(8, 8, 16).transpose(0, 2, 1).reshape(128, 8) if False else g("L0_ffn_spline_b1").reshape(1024).reshape(8, 128).T)
    return ap


def _mk_core_inputs(ap, Xb, half):
    m = dict(ap)
    if half == 0:
        m["Rin"] = np.ascontiguousarray(Xb.T)
        xprev = Xb[T - 1]
        prevmask, midmask = 0.0, 1.0
    else:
        m["Rin"] = np.ascontiguousarray(
            np.concatenate([Xb[T // 2:], Xb[:T // 2]], axis=0).T)
        xprev = Xb[T // 2 - 1]
        prevmask, midmask = 1.0, 0.0
    halo = np.zeros((128, 10), np.float32)
    halo[:, 0:8] = xprev.reshape(8, 128).T
    halo[:, 8] = prevmask
    halo[:, 9] = midmask
    m["halo"] = halo
    return m


def kernel(**inputs):
    ap = _host_prep(inputs)
    X = np.asarray(inputs["L0_attn_a_x"], dtype=np.float32)
    nc = build_program(("A", "F1", "B", "F2"))
    in_maps = []
    for core in range(8):
        in_maps.append(_mk_core_inputs(ap, X[core // 2], core % 2))
    res = run_bass_kernel_spmd(nc, in_maps, core_ids=list(range(8)))
    out = np.zeros((4, T, D), np.float32)
    for b in range(4):
        out[b, 0:T // 2] = res.results[2 * b]["Rout"].T
        out[b, T // 2:] = res.results[2 * b + 1]["Rout"].T
    return out


# version 19
# speedup vs baseline: 1.0723x; 1.0723x over previous
import numpy as np
from contextlib import ExitStack
import concourse.bass as bass
import concourse.mybir as mybir
from concourse.alu_op_type import AluOpType
from concourse.bass_utils import run_bass_kernel_spmd

F32 = mybir.dt.float32
BF16 = mybir.dt.bfloat16
AF = mybir.ActivationFunctionType
AX = mybir.AxisListType
f32 = np.float32
ADD, MUL, SUB = AluOpType.add, AluOpType.mult, AluOpType.subtract
MAXO, MINO, BYP = AluOpType.max, AluOpType.min, AluOpType.bypass

D = 1024
T = 4096
NP = 8
CW = 256
NCOL = T // CW
CHUNK = 64
EPS = f32(1e-5)

ENGINES = ("pe", "dve", "act", "sp", "pool")


NLANE = 4


class Ctx:
    def __init__(self, nc):
        self.nc = nc
        self.es = ExitStack()
        self.q = {e: [] for e in ENGINES}
        self.cnt = {}
        self.sem = {}
        names = list(ENGINES)
        for qn in ("spq", "poolq", "actq"):
            for l in range(NLANE):
                names.append(f"{qn}{l}")
        for e in names:
            self.sem[e] = self.es.enter_context(nc.semaphore(f"s_{e}"))
            self.cnt[e] = 0
        self.lane_rr = {"spq": 0, "poolq": 0, "actq": 0}
        self.last_wait = {e: {} for e in ENGINES}

    def tok(self, e):
        return (e, self.cnt[e])

    def _waits(self, e, waits):
        out = []
        lw = self.last_wait[e]
        for w in waits:
            if w is None:
                continue
            s, v = w
            if s == e or v <= 0:
                continue
            if lw.get(s, 0) >= v:
                continue
            lw[s] = v
            out.append((s, v))
        return out

    def emit(self, e, fn, waits=(), inc=None):
        import traceback as _tb
        fn._src = "".join(_tb.format_stack(limit=6)[:-2])
        wl = self._waits(e, waits)
        serial = e in ("dve", "act")
        do_inc = serial if inc is None else inc
        selfwait = self.cnt[e] if (serial and self.cnt[e] > 0) else None
        if do_inc:
            self.cnt[e] += 1
        n = self.cnt[e]
        self.q[e].append((wl, fn, n if do_inc else None, selfwait))
        return (e, n)

    def dma(self, e, out, in_, waits=(), **kw):
        q = {"sp": "spq", "pool": "poolq", "act": "actq"}[e]
        lane = self.lane_rr[q]
        self.lane_rr[q] = (lane + 1) % NLANE
        ln = f"{q}{lane}"
        wl = self._waits(e, list(waits) + [(ln, self.cnt[ln])])
        self.cnt[ln] += 16
        n = self.cnt[ln]
        sem = self.sem[ln]

        def fn(eng, out=out, in_=in_, kw=kw):
            return eng.dma_start(out=out, in_=in_, **kw)

        self.q[e].append((wl, fn, ("dma", sem), None))
        return (ln, n)

    def wait(self, e, *toks):
        wl = self._waits(e, [t for t in toks if t is not None])
        if wl:
            self.q[e].append((wl, None, None, None))

    def barrier(self):
        toks = [(s_, self.cnt[s_]) for s_ in self.sem if s_ not in ("sp", "pool")]
        for e in ENGINES:
            self.wait(e, *toks)

    def play(self):
        nc = self.nc
        sems = self.sem

        def mk(e):
            queue = self.q[e]

            def body(eng):
                for wl, fn, inc, selfwait in queue:
                    if selfwait:
                        eng.wait_ge(sems[e], selfwait)
                    for (s, v) in wl:
                        eng.wait_ge(sems[s], v)
                    if fn is None:
                        continue
                    try:
                        inst = fn(eng)
                    except Exception:
                        import traceback
                        print("FAILING OP on", e, "emitted at:")
                        print(getattr(fn, "_src", "?"))
                        raise
                    if inst is None:
                        continue
                    if isinstance(inc, tuple):
                        inst.then_inc(inc[1], 16)
                    elif inc is not None:
                        inst.then_inc(sems[e], 1)
            return body

        with nc.Block() as block:
            block.tensor(mk("pe"))
            block.vector(mk("dve"))
            block.scalar(mk("act"))
            block.sync(mk("sp"))
            block.gpsimd(mk("pool"))
        self.es.close()


def tt(c, out, a, b, op, w=()):
    return c.emit("dve", lambda v: v.tensor_tensor(out=out, in0=a, in1=b, op=op), w)


def _sc(x):
    if isinstance(x, (np.floating, np.integer)):
        return float(x)
    return x


def ts(c, out, a, s1, op0, s2=None, op1=None, w=()):
    kw = {}
    if op1 is not None:
        kw["op1"] = op1
    s1 = _sc(s1); s2 = _sc(s2)
    return c.emit("dve", lambda v: v.tensor_scalar(out=out, in0=a, scalar1=s1,
                                                   scalar2=s2, op0=op0, **kw), w)


def stt(c, out, in0, sc, in1, op0, op1, w=()):
    sc = _sc(sc)
    return c.emit("dve", lambda v: v.scalar_tensor_tensor(
        out=out, in0=in0, scalar=sc, in1=in1, op0=op0, op1=op1), w)


def vcopy(c, out, a, w=()):
    return c.emit("dve", lambda v: v.tensor_copy(out=out, in_=a), w)


def vmemset(c, ap, val, w=()):
    return c.emit("dve", lambda v: v.memset(ap, val), w)


def vred(c, out, a, op, w=()):
    return c.emit("dve", lambda v: v.tensor_reduce(out=out, in_=a, axis=AX.X, op=op), w)


def vmax8(c, out, a, w=()):
    return c.emit("dve", lambda v: v.max(out=out, in_=a), w)


def vmaxidx(c, out, mx, vals, w=()):
    return c.emit("dve", lambda v: v.max_index(out=out, in_max=mx, in_values=vals), w)


def act(c, out, a, func, bias=None, scale=None, w=()):
    kw = {}
    if bias is not None:
        kw["bias"] = np.float32(bias) if isinstance(bias, (float, np.floating)) else bias
    if scale is not None:
        kw["scale"] = np.float32(scale) if isinstance(scale, (float, np.floating)) else scale
    return c.emit("act", lambda s: s.activation(out=out, in_=a, func=func, **kw), w)


def mm(c, out, lhsT, rhs, start, stop, w=(), inc=False):
    return c.emit("pe", lambda t: t.matmul(out, lhsT, rhs, start=start, stop=stop),
                  w, inc=inc)


def transp(c, out, in_, ident, w=(), inc=False):
    return c.emit("pe", lambda t: t.transpose(out, in_, ident), w, inc=inc)


def bcast(ap_, np_=NP):
    return ap_.unsqueeze(1).broadcast_to((ap_.shape[0], np_, ap_.shape[1]))


def bcast_t(ap_, n):
    return ap_.unsqueeze(2).broadcast_to((ap_.shape[0], ap_.shape[1], n))



def build_program(parts):
    nc = bass.Bass("TRN2", target_bir_lowering=False)
    es = ExitStack()
    c = Ctx(nc)

    def dram(name, shape, dt=F32, kind="ExternalInput"):
        return nc.dram_tensor(name, shape, dt, kind=kind).ap()

    def sb(name, shape, dt=F32):
        return es.enter_context(nc.sbuf_tensor(name, shape, dt))

    def psum(name, shape, dt=F32):
        return es.enter_context(nc.psum_tensor(name, shape, dt))

    need_A = "A" in parts
    need_F = ("F1" in parts) or ("F2" in parts)
    need_B = "B" in parts

    Rin = dram("Rin", [NP * 128, T])
    Rout = dram("Rout", [NP * 128, T // 2], kind="ExternalOutput")
    consts = dram("consts", [128, 400])

    if need_B:
        qTd = dram("qTd", [128, 32 * 256])
        combd = dram("combd", [128, 32 * 256])
        zscr = dram("zscr", [D, T], BF16, kind="Internal")
        qbs = dram("qbs", [128, 32 * 256], BF16, kind="Internal")
        cbs = dram("cbs", [128, 32 * 256], BF16, kind="Internal")
    if need_F:
        sigsT_raw = dram("sigsT_raw", [D, 64])
        dirsT = dram("dirsT", [D, 64])
        dirsL = dram("dirsL", [64, D])
        cw1 = dram("cw1", [D, 256])
        cw2 = dram("cw2", [128, 4])
        w1f = dram("w1f", [2, 1024])
        w2blk = dram("w2blk", [1024, 64])
        vecsF = dram("vecsF", [128, 20])
        b2tmd = dram("b2tmd", [128, 64])
        b1fd = dram("b1fd", [128, 8])
    if need_A:
        w_inT = dram("w_inT", [D, 128])
        w_gateT = dram("w_gateT", [128, 8192])
        w_outT = dram("w_outT", [128, D])
        w1T = dram("w1T", [128, 32768])
        w2T = dram("w2T", [128, 32768])
        vecsA = dram("vecsA", [128, 48])
        bmlp1 = dram("bmlp1", [128, 32])
        eig = dram("eig", [64, 2])
        couple = dram("couple", [64, 64])
        halo = dram("halo", [128, 10])
        wgs = dram("wgs", [128, 8192], BF16, kind="Internal")
        w1s = dram("w1s", [128, 32768], BF16, kind="Internal")
        w2s = dram("w2s", [128, 32768], BF16, kind="Internal")

    R = sb("R", [128, NP * T])
    ones_bf = sb("ones_bf", [128, 128], BF16)
    onesf = sb("onesf", [128, 128])
    ident_bf = sb("ident_bf", [128, 128], BF16)
    smallsB = sb("smallsB", [128, 12])
    xn_bf = sb("xn_bf", [128, NP * CW], BF16)
    if need_A or need_F:
        u32 = sb("u32", [128, NP * CW])
    rbf = xn_bf[:, 0:NP * CW]
    sqbf = sb("sqbf", [128, NP * CW], BF16)
    bc = sb("bc", [128, 8 * CW])
    hid_bf = sb("hid_bf", [128, 32 * CW], BF16)
    wbig = sb("wbig", [128, 2 * 2048], BF16)
    wbig2 = sb("wbig2", [128, 3 * 1024], BF16)
    if need_B:
        gram = hid_bf[:, 0:4096].bitcast(F32)
        gram_bf = hid_bf[:, 4096:6144]
        t1T = hid_bf[:, 6144:8192]
        Msb = wbig[:, 0:2048]
        ztT = wbig[:, 2048:4096]
        wqa = wbig2[:, 0:2048]
        wcba = hid_bf[:, 0:2048]
    if need_A or need_F:
        x2n_bf = sb("x2n_bf", [128, NP * CW], BF16)
        gate_bf = sb("gate_bf", [128, NP * CW], BF16)
    if need_A:
        ccb = sb("ccb", [128, CW], BF16)
        carry = sb("carry", [64, 2])
        lam = sb("lam", [64, 20])
        lamt = sb("lamt", [64, 4])
        vA = sb("vA", [128, 48])
        bm1 = sb("bm1", [128, 32])
        eig_sb = sb("eig_sb", [64, 2])
        couple_sb = sb("couple_sb", [64, 64])
        winsb = sb("winsb", [128, 8 * 128], BF16)
        halo_sb = sb("halo_sb", [128, 10])
        woutsb = sb("woutsb", [128, 8 * 128], BF16)

    if need_F:
        wsc = sb("wsc", [128, 8 * 72], BF16)
        wdT = sb("wdT", [128, 8 * 64], BF16)
        wdirs = sb("wdirs", [64, D], BF16)
        wc1 = sb("wc1", [128, 8 * 256], BF16)
        wc2 = sb("wc2", [128, 4], BF16)
        ww1f = sb("ww1f", [2, 1024], BF16)
        ww2blk = sb("ww2blk", [128, 8 * 64], BF16)
        vF = sb("vF", [128, 20])
        b1f_sb = sb("b1f_sb", [128, 8])
        knotsB = sb("knotsB", [128, 4])
        dB = sb("dB", [128, 7])
        calF = sb("calF", [128, 8])
        tblp = x2n_bf[:, 0:640]
        prodp = x2n_bf[:, 640:1280]
        extrF = x2n_bf[:, 640:1152].bitcast(F32)
        rhsd_r = x2n_bf[:, 1280:1792].bitcast(F32)
        rowsP = gate_bf[:, 0:64].bitcast(F32)
        mxcB = gate_bf[:, 64:96]
        idxfB = gate_bf[:, 96:160].bitcast(F32)
        mxtF = gate_bf[:, 160:224].bitcast(F32)
        mskg = gate_bf[:, 224:480].bitcast(F32)
        magsbP = [gate_bf[:, 480:736].bitcast(F32),
                  gate_bf[:, 736:992].bitcast(F32)]
        rowsEP = [gate_bf[:, 992:1008].bitcast(F32),
                  gate_bf[:, 1008:1024].bitcast(F32)]
        egl = hid_bf[:, 4096:4608]
        xgl = hid_bf[:, 4608:5120]
        rhsd_m = gate_bf[:, 1536:2048].bitcast(F32)
        b2tm = gate_bf[:, 1184:1248]
        drows = gate_bf[0:64, 1248:1250]
        iota64 = gate_bf[:, 1252:1316]
        clus64 = gate_bf[:, 1316:1380]
        idxu = gate_bf[:, 1380:1444].bitcast(mybir.dt.uint32)
        uu = hid_bf[:, 2048:2560].bitcast(F32)[0:72, :]
        gts = hid_bf[:, 2560:3072].bitcast(F32)[0:72, :]
        gts_bf = hid_bf[0:72, 3072:3328]
        zd_bf = hid_bf[0:64, 3328:3584]
        h1_bf = hid_bf[:, 3584:4096]
        gtmP = [hid_bf[:, 5120:5280], gate_bf[:, 1024:1184]]
        ptm = hid_bf[:, 5408:5536]
        pT_bf = hid_bf[0:64, 5536:5792]
        oh = hid_bf[:, 5792:6304].bitcast(F32)
        comp_bf = hid_bf[0:2, 6304:6560]

    pb = [psum(f"pb{i}", [128, 512]) for i in range(7)]
    pbb = psum("pbb", [128, 1024], BF16)

    t_ones = c.dma("pool", ones_bf[:], consts[:, 0:128])
    t_onesf = c.dma("pool", onesf[:], consts[:, 0:128])
    t_id = c.dma("pool", ident_bf[:], consts[:, 128:256])
    t_sm = c.dma("pool", smallsB[:], consts[:, 256:268])
    for p in range(NP):
        t_R = c.dma("sp", R[:, p * T:(p + 1) * T], Rin[p * 128:(p + 1) * 128, :])

    c.barrier()

    def slot(i):
        return bc[:, i * CW:(i + 1) * CW]

    if need_A:
        cc = bc[0:64, 0:2 * CW]
        cct = bc[0:64, 3 * CW:4 * CW]

    state = {"stats_pe": None, "stats_dve": None, "bc_guard": None}

    def col_stats_mm(src, extra=(), bs=None):
        b1, b2 = bs if bs else (pb[5][:, 0:CW], pb[6][:, 0:CW])
        for p in range(NP):
            act(c, sqbf[:, p * CW:(p + 1) * CW], src(p), AF.Square,
                w=([state["stats_pe"]] + list(extra)) if p == 0 else ())
        t_sq = c.tok("act")
        c.wait("pe", t_sq, t_onesf, state["stats_dve"])
        for p in range(NP):
            mm(c, b1, onesf[:], src(p),
               start=(p == 0), stop=(p == NP - 1))
        last = None
        for p in range(NP):
            last = mm(c, b2, ones_bf[:], sqbf[:, p * CW:(p + 1) * CW],
                      start=(p == 0), stop=(p == NP - 1), inc=(p == NP - 1))
        state["stats_pe"] = last
        return last

    def col_stats_fin(mslot, vslot, last, bs=None):
        b1, b2 = bs if bs else (pb[5][:, 0:CW], pb[6][:, 0:CW])
        ts(c, slot(mslot), b1, f32(1.0 / D), MUL, w=[last])
        state["stats_dve"] = ts(c, slot(vslot), b2, f32(1.0 / D), MUL)
        tt(c, slot(7), slot(mslot), slot(mslot), MUL)
        return tt(c, slot(vslot), slot(vslot), slot(7), SUB)

    def col_stats(src, mslot, vslot, extra=()):
        last = col_stats_mm(src, extra)
        return col_stats_fin(mslot, vslot, last)

    def rstd_into(vslot, oslot):
        act(c, slot(oslot), slot(vslot), AF.Ln, bias=EPS, w=[c.tok("dve")])
        a = act(c, slot(oslot), slot(oslot), AF.Exp, scale=f32(-0.5))
        c.wait("dve", a)
        return a

    def Rcol(p, off, n=CW):
        return R[:, p * T + off: p * T + off + n]

    def Rview(off, n=CW):
        return R[:].rearrange("q (p t) -> q p t", p=NP)[:, :, off:off + n]

    def colview(tensor, npt=NP, n=CW):
        return tensor[:].rearrange("q (p t) -> q p t", p=npt)

    def sublayer_A():
        t_vA = c.dma("pool", vA[:], vecsA[:, :])
        t_bm1 = c.dma("pool", bm1[:], bmlp1[:, :])
        t_eig = c.dma("pool", eig_sb[:], eig[:, :])
        t_cp = c.dma("pool", couple_sb[:], couple[:, :])
        t_win = c.dma("pool", winsb[:].rearrange("p (k m) -> p k m", k=8),
                      w_inT[:, :].rearrange("(k p) m -> p k m", p=128))
        t_wout = c.dma("pool", woutsb[:], w_outT[:, :])
        t_wg = c.dma("pool", wgs[:, :], w_gateT[:, :])
        t_w1 = c.dma("pool", w1s[:, :], w1T[:, :])
        t_w2 = c.dma("pool", w2s[:, :], w2T[:, :])
        g1 = vA[:, 0:8]; b1 = vA[:, 8:16]; g2 = vA[:, 16:24]; b2 = vA[:, 24:32]
        bgate = vA[:, 32:40]; bmlp2v = vA[:, 40:48]

        ld_c = eig_sb[:, 0:1]; fq = eig_sb[:, 1:2]
        mag = lamt[:, 0:1]; fr = lamt[:, 1:2]; q0 = lamt[:, 2:3]; q1 = lamt[:, 3:4]
        act(c, mag, ld_c, AF.Sigmoid, w=[t_eig])
        vcopy(c, fr, fq, w=[t_eig])
        for _ in range(6):
            act(c, q0, fr, AF.Abs, w=[c.tok("dve")])
            act(c, q0, q0, AF.Sign, bias=f32(-np.pi))
            ts(c, q0, q0, f32(0.0), MAXO, w=[c.tok("act")])
            act(c, q1, fr, AF.Sign, w=[c.tok("dve")])
            tt(c, q0, q0, q1, MUL, w=[c.tok("act")])
            stt(c, fr, q0, f32(-2.0 * np.pi), fr, MUL, ADD)
        act(c, lam[:, 16:17], fr, AF.Sin, w=[c.tok("dve")])
        ts(c, q0, fr, f32(-1.0), MUL, f32(np.pi / 2), ADD, w=[c.tok("act")])
        act(c, q1, q0, AF.Sign, bias=f32(-np.pi), w=[c.tok("dve")])
        ts(c, q1, q1, f32(0.0), MAXO, w=[c.tok("act")])
        stt(c, q0, q1, f32(-2.0 * np.pi), q0, MUL, ADD)
        act(c, lam[:, 17:18], q0, AF.Sin, w=[c.tok("dve")])
        tt(c, lam[:, 0:1], mag, lam[:, 17:18], MUL, w=[c.tok("act")])
        tt(c, lam[:, 1:2], mag, lam[:, 16:17], MUL)
        for i in range(1, 6):
            pr = lam[:, 2 * i - 2:2 * i - 1]; pi_ = lam[:, 2 * i - 1:2 * i]
            qr = lam[:, 2 * i:2 * i + 1]; qi = lam[:, 2 * i + 1:2 * i + 2]
            tt(c, q0, pr, pr, MUL)
            tt(c, q1, pi_, pi_, MUL)
            tt(c, qr, q0, q1, SUB)
            tt(c, q0, pr, pi_, MUL)
            ts(c, qi, q0, f32(2.0), MUL)
        for i in range(6):
            ts(c, lam[:, 12 + i:13 + i], lam[:, 2 * i + 1:2 * i + 2], f32(-1.0), MUL)
        t_lam = c.tok("dve")
        t_halo = c.dma("pool", halo_sb[:], halo[:, :])
        hb = slot(6)
        act(c, hb[:, 16:24], halo_sb[:, 0:8], AF.Square, w=[t_halo])
        t_hsq = c.tok("act")
        c.wait("pe", t_halo, t_hsq, t_onesf)
        for p in range(NP):
            mm(c, pb[2][:, 0:1], onesf[:], halo_sb[:, p:p + 1],
               start=(p == 0), stop=(p == NP - 1))
        for p in range(NP):
            t_hs2 = mm(c, pb[2][:, 1:2], onesf[:], hb[:, 16 + p:17 + p],
                       start=(p == 0), stop=(p == NP - 1), inc=(p == NP - 1))
        ts(c, hb[:, 0:1], pb[2][:, 0:1], f32(1.0 / D), MUL, w=[t_hs2])
        ts(c, hb[:, 1:2], pb[2][:, 1:2], f32(1.0 / D), MUL)
        tt(c, hb[:, 2:3], hb[:, 0:1], hb[:, 0:1], MUL)
        tt(c, hb[:, 1:2], hb[:, 1:2], hb[:, 2:3], SUB)
        act(c, hb[:, 2:3], hb[:, 1:2], AF.Ln, bias=EPS, w=[c.tok("dve")])
        t_ha = act(c, hb[:, 2:3], hb[:, 2:3], AF.Exp, scale=f32(-0.5))
        tt(c, hb[:, 3:4], hb[:, 2:3], hb[:, 2:3], MUL, w=[t_ha])
        tt(c, hb[:, 4:5], hb[:, 1:2], hb[:, 3:4], MUL)
        act(c, hb[:, 5:6], hb[:, 4:5], AF.Ln, bias=EPS, w=[c.tok("dve")])
        t_ha2 = act(c, hb[:, 5:6], hb[:, 5:6], AF.Exp, scale=f32(-0.5))
        tt(c, hb[:, 6:7], hb[:, 2:3], hb[:, 5:6], MUL, w=[t_ha2])
        ts(c, hb[:, 8:16], halo_sb[:, 0:8], hb[:, 0:1], AluOpType.subtract)
        ts(c, hb[:, 8:16], hb[:, 8:16], hb[:, 6:7], MUL)
        tt(c, hb[:, 8:16], hb[:, 8:16], vA[:, 0:8], MUL, w=[t_vA])
        tt(c, hb[:, 8:16], hb[:, 8:16], vA[:, 8:16], ADD)
        t_xnp = vcopy(c, ccb[:, 0:8], hb[:, 8:16])
        c.wait("pe", t_xnp, t_win)
        for k in range(NP):
            mm(c, pb[2][0:64, 2:3], winsb[:, k * 128:k * 128 + 64],
               ccb[:, k:k + 1], start=(k == 0), stop=(k == NP - 1))
        for k in range(NP):
            t_bp = mm(c, pb[2][0:64, 3:4], winsb[:, k * 128 + 64:(k + 1) * 128],
                      ccb[:, k:k + 1], start=(k == 0), stop=(k == NP - 1),
                      inc=(k == NP - 1))
        ts(c, carry[:, 0:1], pb[2][0:64, 2:3], halo_sb[0:64, 8:9], MUL, w=[t_bp])
        t_carry = ts(c, carry[:, 1:2], pb[2][0:64, 3:4], halo_sb[0:64, 8:9], MUL)
        c.wait("pe", t_carry)

        c.wait("pe", t_win, t_wout, t_cp, t_R)
        c.wait("dve", t_vA, t_bm1, t_lam, t_carry)
        c.wait("act", t_vA, t_bm1, t_R)
        prevA = {"gate_pe": None, "x1_dve": None, "ccb_act": None,
                 "mlp2_dve": None, "m1_last": None, "u32_act": None,
                 "gate_act": None, "stats_dve_g": None}
        colstate = {}

        def slab1(col):
            off = col * CW
            col_stats(lambda p: Rcol(p, off), 0, 1,
                      extra=[prevA["gate_pe"]])
            rstd_into(1, 2)
            tt(c, slot(7), slot(2), slot(2), MUL)
            tt(c, slot(3), slot(1), slot(7), MUL)
            rstd_into(3, 3)
            tt(c, slot(4), slot(2), slot(3), MUL)
            tt(c, slot(5), slot(0), slot(4), MUL)
            tt(c, slot(6), slot(0), slot(2), MUL)
            prevA["stats_dve_g"] = state["stats_dve"]
            xv = colview(xn_bf)
            tt(c, xv, Rview(off), bcast(slot(4)), MUL, w=[prevA["gate_pe"]])
            t_xnp = tt(c, xv, xv, bcast(slot(5)), SUB)
            for p in range(NP):
                t_xn = act(c, xn_bf[:, p * CW:(p + 1) * CW], xn_bf[:, p * CW:(p + 1) * CW],
                           AF.Identity, scale=g1[:, p:p + 1], bias=b1[:, p:p + 1],
                           w=[t_xnp])
            uv = colview(u32)
            tt(c, uv, Rview(off), bcast(slot(2)), MUL, w=[prevA["u32_act"]])
            t_y = tt(c, uv, uv, bcast(slot(6)), SUB)
            c.wait("pe", t_xn, prevA["x1_dve"])
            for k in range(NP):
                mm(c, pb[2][0:64, 0:CW], winsb[:, k * 128:k * 128 + 64],
                   xn_bf[:, k * CW:(k + 1) * CW], start=(k == 0),
                   stop=(k == NP - 1))
            for k in range(NP):
                t_beta = mm(c, pb[2][0:64, CW:2 * CW], winsb[:, k * 128 + 64:(k + 1) * 128],
                            xn_bf[:, k * CW:(k + 1) * CW], start=(k == 0),
                            stop=(k == NP - 1), inc=(k == NP - 1))
            t_c0 = vcopy(c, cc[:, :], pb[2][0:64, 0:2 * CW], w=[t_beta])
            cr = cc[:, 0:CW]; ci = cc[:, CW:2 * CW]
            tmp = cct[:, 0:CW]
            d, i = 1, 0
            while d < CHUNK:
                pr = lam[:, 2 * i:2 * i + 1]
                pi_ = lam[:, 2 * i + 1:2 * i + 2]
                nli = lam[:, 12 + i:13 + i]
                crv = cr.rearrange("k (n s) -> k n s", s=CHUNK)
                civ = ci.rearrange("k (n s) -> k n s", s=CHUNK)
                tmv = tmp.rearrange("k (n s) -> k n s", s=CHUNK)
                dst_r = crv[:, :, 0:CHUNK - d]; src_r = crv[:, :, d:CHUNK]
                dst_i = civ[:, :, 0:CHUNK - d]; src_i = civ[:, :, d:CHUNK]
                tm = tmv[:, :, 0:CHUNK - d]
                vcopy(c, tm, src_r)
                stt(c, dst_r, src_r, pr, dst_r, MUL, ADD)
                stt(c, dst_r, src_i, nli, dst_r, MUL, ADD)
                stt(c, dst_i, src_i, pr, dst_i, MUL, ADD)
                stt(c, dst_i, tm, pi_, dst_i, MUL, ADD)
                d *= 2
                i += 1
            if col == NCOL // 2:
                ts(c, carry[:], carry[:], halo_sb[0:64, 9:10], MUL)
            nch = CW // CHUNK
            for half in range(2):
                cch = cc[:, half * CW:(half + 1) * CW]
                ccv2 = cch.rearrange("k (n s) -> k n s", s=CHUNK)
                bv = pb[2][0:64, half * CW:(half + 1) * CW].rearrange(
                    "k (n s) -> k n s", s=CHUNK)
                tt(c, ccv2[:, 1:nch, 0:1], ccv2[:, 1:nch, 0:1],
                   bv[:, 0:nch - 1, CHUNK - 1:CHUNK], ADD)
                tt(c, cch[:, 0:1], cch[:, 0:1], carry[:, half:half + 1], ADD)
                vcopy(c, carry[:, half:half + 1],
                      pb[2][0:64, (half + 1) * CW - 1:(half + 1) * CW])
            colstate[col] = [None, t_y, None, None, None, ([], []), c.tok("dve")]

        def slab1b(col):
            c.wait("pe", c.tok("dve"), prevA["ccb_act"])
            mm(c, pb[2][0:64, 0:CW], couple_sb[:], cc[:, 0:CW], start=True, stop=True)
            t_cpl = mm(c, pb[2][64:128, 0:CW], couple_sb[:], cc[:, CW:2 * CW],
                       start=True, stop=True, inc=True)
            act(c, ccb[0:64, :], pb[2][0:64, 0:CW], AF.Identity, w=[t_cpl])
            t_ccb = act(c, ccb[64:128, :], pb[2][64:128, 0:CW], AF.Identity)
            prevA["ccb_act"] = t_ccb
            colstate[col][0] = t_ccb

        def gate_step(col, do):
            gst = colstate[col][5]
            gate_mil, t_gb_hist = gst
            grp, ds = do // 2, do % 2
            if ds == 0:
                wslice = wbig[:, grp % 2 * 2048:(grp % 2 + 1) * 2048]
                t_wgl = c.dma("sp", wslice,
                              wgs[:, grp * 2048:(grp + 1) * 2048],
                              waits=[gate_mil[grp - 2], t_wg] if grp >= 2
                                    else [t_wg, prevA["m1_last"]])
                c.wait("pe", t_wgl)
            wslice = wbig[:, grp % 2 * 2048:(grp % 2 + 1) * 2048]
            gb_bank = pb[3] if do % 2 == 0 else pb[5]
            if do >= 2:
                c.wait("pe", t_gb_hist[do - 2])
            elif do == 0:
                c.wait("pe", prevA["gate_act"], prevA["stats_dve_g"])
            for k in range(NP):
                t_g = mm(c, gb_bank[:, 0:CW],
                         wslice[:, (k * 256 + ds * 128):(k * 256 + ds * 128 + 128)],
                         xn_bf[:, k * CW:(k + 1) * CW], start=(k == 0),
                         stop=(k == NP - 1), inc=(k == NP - 1))
            if ds == 1:
                gate_mil.append(t_g)
            t_gb = act(c, gate_bf[:, do * CW:(do + 1) * CW], gb_bank[:, 0:CW],
                       AF.Sigmoid, bias=bgate[:, do:do + 1],
                       w=[t_g, colstate[col][6]])
            t_gb_hist.append(t_gb)

        def slab2a(col):
            off = col * CW
            st = colstate[col]
            t_ccb, t_y = st[0], st[1]
            gate_mil, t_gb_hist = st[5]
            c.wait("pe", t_ccb, t_y)
            c.wait("pe", t_gb_hist[-2])
            t_h_hist = []
            t_hx1_hist = []
            for do in range(NP):
                hb = pb[2] if do % 2 == 0 else pb[4]
                if do >= 2:
                    c.wait("pe", t_hx1_hist[do - 2])
                t_h = mm(c, hb[:, 0:CW], woutsb[:, do * 128:(do + 1) * 128],
                         ccb[:], start=True, stop=True, inc=True)
                t_h_hist.append(t_h)
                tt(c, slot(7), gate_bf[:, do * CW:(do + 1) * CW], hb[:, 0:CW], MUL,
                   w=[t_gb_hist[do], t_h])
                t_hx1 = tt(c, u32[:, do * CW:(do + 1) * CW],
                           u32[:, do * CW:(do + 1) * CW], slot(7), ADD)
                t_hx1_hist.append(t_hx1)
            gate_mil.append(t_h_hist[-1])
            tt(c, Rview(off), Rview(off), colview(u32), ADD, w=[t_hx1])
            col_stats(lambda p: u32[:, p * CW:(p + 1) * CW], 0, 1,
                      extra=[c.tok("dve")])
            rstd_into(1, 2)
            tt(c, slot(5), slot(0), slot(2), MUL)
            uv2 = colview(u32)
            tt(c, uv2, uv2, bcast(slot(2)), MUL)
            t_x2p = tt(c, uv2, uv2, bcast(slot(5)), SUB)
            for p in range(NP):
                t_x2n = act(c, x2n_bf[:, p * CW:(p + 1) * CW], u32[:, p * CW:(p + 1) * CW],
                            AF.Identity, scale=g2[:, p:p + 1], bias=b2[:, p:p + 1],
                            w=[t_x2p])
            prevA["gate_pe"] = gate_mil[-1]
            prevA["gate_act"] = t_gb_hist[-1]
            prevA["x1_dve"] = t_hx1
            prevA["u32_act"] = c.tok("act")
            st[2] = t_x2n
            st[3] = gate_mil
            st[4] = t_hx1

        def slab2b1(col):
            off = col * CW
            st = colstate[col]
            t_x2n, gate_mil = st[2], st[3]
            c.wait("pe", t_x2n, t_w1)
            t_sil = c.tok("act")
            m1_mil = []
            for grp in range(16):
                wslice = wbig[:, grp % 2 * 2048:(grp % 2 + 1) * 2048]
                t_wl = c.dma("sp", wslice,
                             w1s[:, grp * 2048:(grp + 1) * 2048],
                             waits=[m1_mil[2 * (grp - 2) + 1], t_w1] if grp >= 2
                                   else [t_w1, gate_mil[-1]])
                c.wait("pe", t_wl)
                for hs in range(2):
                    h = grp * 2 + hs
                    mb = (pb[0], pb[1], pb[4])[h % 3]
                    if h >= 3:
                        c.wait("pe", t_sil_hist[h - 3])
                    for k in range(NP):
                        t_m1 = mm(c, mb[:, 0:CW],
                                  wslice[:, (k * 256 + hs * 128):(k * 256 + hs * 128 + 128)],
                                  x2n_bf[:, k * CW:(k + 1) * CW], start=(k == 0),
                                  stop=(k == NP - 1), inc=(k == NP - 1))
                    m1_mil.append(t_m1)
                    t_sil = act(c, hid_bf[:, h * CW:(h + 1) * CW], mb[:, 0:CW],
                                AF.Silu, bias=bm1[:, h:h + 1], w=[t_m1])
                    if h == 0:
                        t_sil_hist = []
                    t_sil_hist.append(t_sil)
            colstate[col][2] = (t_sil, gate_mil, m1_mil)

        def slab2b2(col):
            off = col * CW
            t_sil, gate_mil, m1_mil = colstate[col][2]
            colstate.pop(col)
            prevA["m1_last"] = m1_mil[-1]
            c.wait("pe", t_sil, t_w2)
            t_radd = None
            m2_mil = []
            for do in range(NP):
                buf = do % 2
                if do >= 2:
                    c.wait("pe", t_radd_hist[do - 2])
                for q in range(4):
                    qi = do * 4 + q
                    wslice = wbig2[:, (qi % 3) * 1024:(qi % 3 + 1) * 1024]
                    t_wl2 = c.dma("sp", wslice,
                                  w2s[:, qi * 1024:(qi + 1) * 1024],
                                  waits=[m2_mil[qi - 3], t_w2] if qi >= 3
                                        else [t_w2, t_sil, gate_mil[-1]])
                    c.wait("pe", t_wl2)
                    for k8 in range(8):
                        k = q * 8 + k8
                        t_m2 = mm(c, pb[0 + buf][:, 0:CW], wslice[:, k8 * 128:(k8 + 1) * 128],
                                  hid_bf[:, k * CW:(k + 1) * CW], start=(k == 0),
                                  stop=(k == 31), inc=(k8 == 7))
                    m2_mil.append(t_m2)
                t_radd = stt(c, Rcol(do, off), pb[0 + buf][:, 0:CW], bmlp2v[:, do:do + 1],
                             Rcol(do, off), ADD, ADD, w=[t_m2])
                if do == 0:
                    t_radd_hist = []
                t_radd_hist.append(t_radd)
                if col + 1 in colstate:
                    gate_step(col + 1, do)
            prevA["mlp2_dve"] = t_radd

        slab1(0)
        slab1b(0)
        for do in range(NP):
            gate_step(0, do)
        for col in range(NCOL):
            slab2a(col)
            if col + 1 < NCOL:
                slab1(col + 1)
            slab2b1(col)
            if col + 1 < NCOL:
                slab1b(col + 1)
            slab2b2(col)
        c.barrier()

    def prologue_F():
        t_vF = c.dma("pool", vF[:], vecsF[:, :])
        t_b1f = c.dma("pool", b1f_sb[:], b1fd[:, :])
        t_wsc0 = c.dma("pool", wdT[:].rearrange("p (k m) -> p k m", k=8),
                       dirsT[:, :].rearrange("(k p) m -> p k m", p=128))
        t_wdir = c.dma("pool", wdirs[:], dirsL[:, :])
        ts(c, wdirs[:], wdirs[:], smallsB[0:64, 9:10], MUL, w=[t_wdir, t_sm])
        t_wc1 = c.dma("pool", wc1[:].rearrange("p (k m) -> p k m", k=8),
                      cw1[:, :].rearrange("(k p) m -> p k m", p=128))
        t_wc2 = c.dma("pool", wc2[:], cw2[:, :])
        t_w1f = c.dma("pool", ww1f[:], w1f[:, :])
        t_w2b = c.dma("pool", ww2blk[:].rearrange("p (k m) -> p k m", k=8),
                      w2blk[:, :].rearrange("(k p) m -> p k m", p=128))
        t_io = c.dma("pool", iota64[:], consts[:, 272:336])
        t_cl = c.dma("pool", clus64[:], consts[:, 336:400])
        t_b2 = c.dma("pool", b2tm[:], b2tmd[:, :])
        sraw = hid_bf[:, 0:1024].bitcast(F32)
        qa = hid_bf[:, 1024:2048].bitcast(F32)
        t_sr = c.dma("pool", sraw[:].rearrange("p (k m) -> p k m", k=8),
                     sigsT_raw[:, :].rearrange("(k p) m -> p k m", p=128))
        c.wait("act", t_sr)
        act(c, qa[:], sraw[:], AF.Sign, bias=f32(-0.3))
        t_q1 = c.tok("act")
        ts(c, qa[:], qa[:], 0.0, MAXO, w=[t_q1])
        act(c, sraw[:], sraw[:], AF.Sign, bias=f32(-0.3), scale=f32(-1.0),
            w=[c.tok("dve")])
        t_q2 = c.tok("act")
        ts(c, sraw[:], sraw[:], 0.0, MAXO, w=[t_q2])
        wscv = wsc[:].rearrange("p (k m) -> p k m", m=72)
        qv = qa[:].rearrange("p (k m) -> p k m", m=64)
        sv_ = sraw[:].rearrange("p (k m) -> p k m", m=64)
        tt(c, qv, qv, sv_, SUB)
        t_q = tt(c, wscv[:, :, 0:64], qv, qv, MINO)
        qg = qa[:].rearrange("p (k cl t) -> p k cl t", k=8, cl=8)
        red = hid_bf[:, 2048:2176].bitcast(F32)
        rv = red[:].rearrange("p (k cl) -> p k cl", k=8)
        vred(c, rv, qg, ADD)
        t_red = ts(c, red[:], red[:], 0.125, MUL)
        t_cs = act(c, red[:], red[:], AF.Sign, w=[t_red])
        tt(c, wscv[:, :, 64:72], rv, rv, MINO, w=[t_cs])
        c.wait("dve", t_sm)
        vcopy(c, knotsB[:, 0:1], smallsB[:, 0:1])
        ts(c, knotsB[:, 3:4], smallsB[:, 8:9], 1e-6, ADD)
        c.emit("dve", lambda v: v.reciprocal(out=knotsB[:, 1:2], in_=knotsB[:, 3:4]))
        vcopy(c, knotsB[:, 2:3], smallsB[:, 9:10])
        tt(c, dB[:], smallsB[:, 1:8], smallsB[:, 0:7], SUB)
        tt(c, calF[:, 0:6], dB[:, 0:6], dB[:, 1:7], SUB)
        vcopy(c, calF[:, 6:7], dB[:, 6:7])
        ts(c, calF[:, 0:7], calF[:, 0:7], 7.0, MUL)
        rowsb = hid_bf[:, 4096:5120].bitcast(F32)[0:1, :]
        c.wait("pe", t_wsc0, t_id, c.tok("dve"), c.tok("act"))
        for k in range(NP):
            t_dm = mm(c, pb[0][0:64, 0:1], wdT[:, k * 64:(k + 1) * 64],
                      ones_bf[:, 0:1], start=(k == 0), stop=(k == NP - 1),
                      inc=(k == NP - 1))
        t_sq2 = c.dma("pool", qa[:, 0:512].rearrange("p (k m) -> p k m", k=8),
                      dirsT[:, :].rearrange("(k p) m -> p k m", p=128),
                      waits=[c.tok("dve"), c.tok("act")])
        t_sqa = act(c, sqbf[:, 0:512], qa[:, 0:512], AF.Square, w=[t_sq2])
        c.wait("pe", t_sqa)
        for k in range(NP):
            t_ds = mm(c, pb[1][0:64, 0:1], sqbf[:, k * 64:(k + 1) * 64],
                      ones_bf[:, 0:1], start=(k == 0), stop=(k == NP - 1),
                      inc=(k == NP - 1))
        ts(c, drows[:, 0:1], pb[0][0:64, 0:1], f32(1.0 / D), MUL, w=[t_dm, t_ds])
        ts(c, drows[:, 1:2], pb[1][0:64, 0:1], f32(1.0 / D), MUL)
        c.wait("pe", c.tok("dve"))
        transp(c, pbb[0:1, 544:608], drows[:, 0:1], ident_bf[0:64, 0:64], inc=True)
        t_tr = c.tok("pe")
        vcopy(c, rowsb[:, 0:64], pbb[0:1, 544:608], w=[("pe", t_tr[1])])
        transp(c, pbb[0:1, 608:672], drows[:, 1:2], ident_bf[0:64, 0:64], inc=True,
               w=[c.tok("dve")])
        t_tr2 = c.tok("pe")
        vcopy(c, rowsb[:, 64:128], pbb[0:1, 608:672], w=[("pe", t_tr2[1])])
        c.wait("pe", c.tok("dve"))
        mm(c, pb[3][:, 0:64], onesf[0:1, :], rowsb[0:1, 0:64], start=True, stop=True)
        t_bb = mm(c, pb[4][:, 0:64], onesf[0:1, :], rowsb[0:1, 64:128], start=True,
                  stop=True, inc=True)
        tb2 = tblp.rearrange("p (c w) -> p c w", c=2)
        vcopy(c, tb2[:, :, 128:192],
              pb[3][:, 0:64].unsqueeze(1).broadcast_to((128, 2, 64)), w=[t_bb])
        vcopy(c, tb2[:, :, 192:256],
              pb[4][:, 0:64].unsqueeze(1).broadcast_to((128, 2, 64)))
        c.wait("dve", t_b2)
        vcopy(c, tb2[:, :, 256:320],
              b2tm[:].unsqueeze(1).broadcast_to((128, 2, 64)))
        c.barrier()

    def sublayer_F(tag, fcols=None):
        gamma = vF[:, 0:8]; beta_ = vF[:, 8:16]; cb1 = vF[:, 16:18]
        cb2c = vF[0:2, 18:19]
        if fcols is None:
            fcols = NCOL
        tbz = tblp.rearrange("p (c w) -> p c w", c=2)
        tbl4 = tblp.rearrange("p (c t w) -> p c t w", c=2, t=5)
        pr4 = prodp.rearrange("p (c t w) -> p c t w", c=2, t=5)
        sels = rowsP[:, 16:26].rearrange("p (c t) -> p c t", c=2)

        def q(j):
            return rowsP[:, 2 * j:2 * j + 2].unsqueeze(2)

        def qb(j, n):
            return q(j).broadcast_to((128, 2, n))
        mskp = mskg.rearrange("p (c w) -> p c w", c=2)
        ohp = oh[:, 0:128]
        ohv3 = ohp.rearrange("p (c w) -> p c w", c=2)
        pr2f = oh[:, 128:256].rearrange("p (c w) -> p c w", c=2)
        identb = ident_bf[:].unsqueeze(1).broadcast_to((128, 2, 128))
        rr3 = rhsd_r.rearrange("p (c w) -> p c w", c=2)
        rm3 = rhsd_m.rearrange("p (c w) -> p c w", c=2)
        ex3 = extrF.rearrange("p (c w) -> p c w", c=2)
        fst = {}

        fstats_bs = (pb[1][:, CW:2 * CW], pb[3][:, CW:2 * CW])

        def fEs(col):
            off = col * CW
            c.wait("pe", state.get("bc_guard"), state.get("f_guard"))
            fst[("S", col)] = col_stats_mm(lambda p: Rcol(p, off), bs=fstats_bs)

        def fEa(col):
            off = col * CW
            col_stats_fin(0, 1, fst.pop(("S", col)), bs=fstats_bs)
            rstd_into(1, 2)
            tt(c, slot(5), slot(0), slot(2), MUL)
            tt(c, slot(7), slot(2), slot(2), MUL)
            tt(c, slot(3), slot(1), slot(7), MUL)
            uv = colview(u32)
            tt(c, uv, Rview(off), bcast(slot(2)), MUL)
            tt(c, uv, uv, bcast(slot(5)), SUB)
            t_z = vcopy(c, colview(xn_bf), uv, w=[state.get("xn_guard")])
            c.wait("pe", t_z, state.get("pb2_guard"))
            for k in range(NP):
                mm(c, pb[2][0:72, 0:CW], wsc[:, k * 72:(k + 1) * 72],
                   xn_bf[:, k * CW:(k + 1) * CW], start=(k == 0), stop=(k == NP - 1))
            for k in range(NP):
                t_zd = mm(c, pb[2][0:64, CW:2 * CW], wdT[:, k * 64:(k + 1) * 64],
                          xn_bf[:, k * CW:(k + 1) * CW], start=(k == 0),
                          stop=(k == NP - 1), inc=(k == NP - 1))
            c.wait("pe", state.get("pb4_guard"))
            for half in range(2):
                for k in range(NP):
                    t_c1 = mm(c, pb[4][:, half * CW:(half + 1) * CW],
                              wc1[:, k * 256 + half * 128:k * 256 + half * 128 + 128],
                              xn_bf[:, k * CW:(k + 1) * CW], start=(k == 0),
                              stop=(k == NP - 1), inc=(k == NP - 1 and half == 1))
            state["xn_guard"] = t_c1
            for half in range(2):
                t_h1 = act(c, h1_bf[:, half * CW:(half + 1) * CW],
                           pb[4][:, half * CW:(half + 1) * CW],
                           AF.Gelu, bias=cb1[:, half:half + 1],
                           w=[t_c1, state.get("c2_guard")] if half == 0 else [t_c1])
            state["pb4_guard"] = t_h1
            t_zde = act(c, zd_bf[:], pb[2][0:64, CW:2 * CW], AF.Identity,
                        w=[t_zd, state.get("tp_guard")])
            t_sg = act(c, uu[:], pb[2][0:72, 0:CW], AF.Sigmoid,
                       scale=knotsB[0:72, 1:2], w=[t_zd, state.get("uu_guard")])
            state["pb2_guard"] = t_sg
            fst[("Ea", col)] = (t_sg, t_h1, t_zde)

        def fEb(col):
            par = col % 2
            gtm = gtmP[par]
            rowsE = rowsEP[par]
            t_sg, t_h1, t_zde = fst.pop(("Ea", col))
            for j, sl in enumerate((2, 5, 3)):
                tt(c, ex3, identb, slot(sl).rearrange("p (c w) -> p c w", c=2), MUL)
                vred(c, rowsE[:, 2 * j:2 * j + 2].unsqueeze(2), ex3, ADD)
            t_cal = ts(c, gts[:], uu[:], calF[0:72, 6:7], MUL, knotsB[0:72, 0:1],
                       ADD, w=[t_sg])
            for m in range(1, 7):
                ts(c, slot(7)[0:72, :], uu[:], float(m / 7.0), MINO)
                t_cal = stt(c, gts[:], slot(7)[0:72, :], calF[0:72, m - 1:m], gts[:],
                            MUL, ADD)
            state["uu_guard"] = t_cal
            t_gts = vcopy(c, gts_bf[:], gts[:], w=[state.get("tp_guard")])
            c.wait("pe", t_h1)
            for k2 in range(2):
                t_c2 = mm(c, pb[5][0:2, 0:CW], wc2[:, k2 * 2:(k2 + 1) * 2],
                          h1_bf[:, k2 * CW:(k2 + 1) * CW], start=(k2 == 0),
                          stop=(k2 == 1), inc=(k2 == 1))
            t_cp2 = act(c, comp_bf[:], pb[5][0:2, 0:CW], AF.Tanh,
                        bias=cb2c, w=[t_c2, state.get("hh_guard")])
            state["c2_guard"] = t_c2
            c.wait("pe", t_cp2, t_gts, state.get("bc_guard"))
            t_hh = None
            t_hh_hist = []
            for gch in range(8):
                if gch >= 2:
                    c.wait("pe", t_hh_hist[gch - 2])
                t_hmm = mm(c, pb[5 + gch % 2][:, 0:CW], ww1f[:, gch * 128:(gch + 1) * 128],
                           comp_bf[:], start=True, stop=True, inc=True)
                t_hh = act(c, hid_bf[:, gch * CW:(gch + 1) * CW], pb[5 + gch % 2][:, 0:CW],
                           AF.Relu, bias=b1f_sb[:, gch:gch + 1],
                           w=[t_hmm, state.get("magmm_guard")] if gch == 0
                             else [t_hmm])
                t_hh_hist.append(t_hh)
                if gch == 7:
                    state["hh_guard"] = t_hmm
            c.wait("pe", t_gts, t_zde, state.get("pbb_guard2"))
            for tch in range(2):
                transp(c, pbb[:, tch * 80:tch * 80 + 72],
                       gts_bf[:, tch * 128:(tch + 1) * 128], ident_bf[0:72, 0:72])
                transp(c, pbb[:, 160 + tch * 64:160 + (tch + 1) * 64],
                       zd_bf[:, tch * 128:(tch + 1) * 128], ident_bf[0:64, 0:64],
                       inc=(tch == 1))
            t_tp = c.tok("pe")
            state["tp_guard"] = t_tp
            vcopy(c, gtm[:, 0:72], pbb[:, 0:72], w=[t_tp])
            vcopy(c, gtm[:, 80:152], pbb[:, 80:152])
            t_ztm = vcopy(c, tbz[:, :, par * 64:par * 64 + 64],
                          pbb[:, 160:288].rearrange("p (c w) -> p c w", c=2))
            c.wait("pe", t_hh, t_ztm, state.get("mag_evac"))
            for tch in range(2):
                for gch in range(8):
                    t_mg = mm(c, pb[0][:, tch * 64:(tch + 1) * 64],
                              hid_bf[:, gch * CW + tch * 128:gch * CW + (tch + 1) * 128],
                              ww2blk[:, gch * 64:(gch + 1) * 64], start=(gch == 0),
                              stop=(gch == 7), inc=(gch == 7 and tch == 1))
            t_mge = act(c, magsbP[par][:], pb[0][:, 0:128], AF.Identity,
                        w=[t_mg, fst.get(("mag_read", par))])
            state["mag_evac"] = t_mge
            state["magmm_guard"] = t_mg
            fst[("E", col)] = t_mge

        def fD(col):
            off = col * CW
            par = col % 2
            gtm = gtmP[par]
            rowsE = rowsEP[par]
            t_mge = fst.pop(("E", col))
            for tch in range(2):
                vmax8(c, mxcB[:, tch * 8:tch * 8 + 8],
                      gtm[:, tch * 80 + 64:tch * 80 + 72])
                vmaxidx(c, idxu[:, tch * 8:tch * 8 + 8],
                        mxcB[:, tch * 8:tch * 8 + 8],
                        gtm[:, tch * 80 + 64:tch * 80 + 72])
            vcopy(c, idxfB[:, 0:16], idxu[:, 0:16])
            cidxb = idxfB[:, 0:16].rearrange("p (c w) -> p c w", c=2)[:, :, 0:1]
            tt(c, mskp, clus64[:].unsqueeze(1).broadcast_to((128, 2, 64)),
               cidxb.broadcast_to((128, 2, 64)), AluOpType.is_equal)
            g64p = gtm[:, 0:160].rearrange("p (c w) -> p c w", c=2)[:, :, 0:64]
            stt(c, mskp, mskp, 1000.0, g64p, MUL, ADD)
            for tch in range(2):
                vmax8(c, mxtF[:, tch * 8:tch * 8 + 8],
                      mskg[:, tch * 64:(tch + 1) * 64])
                vmaxidx(c, idxu[:, 16 + tch * 8:24 + tch * 8],
                        mxtF[:, tch * 8:tch * 8 + 8],
                        mskg[:, tch * 64:(tch + 1) * 64])
            vcopy(c, idxfB[:, 16:32], idxu[:, 16:32])
            tidxb = idxfB[:, 16:32].rearrange("p (c w) -> p c w", c=2)[:, :, 0:1]
            tt(c, ohv3, iota64[:].unsqueeze(1).broadcast_to((128, 2, 64)),
               tidxb.broadcast_to((128, 2, 64)), AluOpType.is_equal)
            tt(c, pr4, tbl4, ohv3.unsqueeze(2).broadcast_to((128, 2, 5, 64)), MUL)
            vred(c, sels.unsqueeze(3), pr4, ADD)
            zselv = sels[:, :, par:par + 1]
            tt(c, pr2f, ohv3,
               magsbP[par][:].rearrange("p (c w) -> p c w", c=2), MUL, w=[t_mge])
            fst[("mag_read", par)] = vred(c, q(6), pr2f, ADD)
            tt(c, q(1), q(6), sels[:, :, 4:5], ADD)
            ts(c, rowsP[:, 2:4], rowsP[:, 2:4], knotsB[:, 2:3], MUL)
            vcopy(c, rowsP[:, 0:2], rowsE[:, 4:6], w=[state.get("rp_guard")])
            stt(c, q(3), q(1), float(2.0 / D), zselv, MUL, MUL)
            tt(c, q(0), q(0), q(3), ADD)
            tt(c, q(3), q(1), q(1), MUL)
            tt(c, q(3), q(3), sels[:, :, 3:4], MUL)
            tt(c, q(0), q(0), q(3), ADD)
            tt(c, q(2), q(1), sels[:, :, 2:3], MUL)
            tt(c, q(3), q(2), q(2), MUL)
            t_varp = tt(c, q(0), q(0), q(3), SUB)
            act(c, rowsP[:, 8:10], rowsP[:, 0:2], AF.Ln, bias=EPS, w=[t_varp])
            t_rp = act(c, rowsP[:, 8:10], rowsP[:, 8:10], AF.Exp, scale=f32(-0.5))
            state["rp_guard"] = t_rp
            tt(c, rowsP[:, 12:14], rowsE[:, 0:2], rowsP[:, 8:10], MUL, w=[t_rp])
            tt(c, rowsP[:, 14:16], rowsE[:, 2:4], rowsP[:, 4:6], ADD)
            tt(c, rowsP[:, 14:16], rowsP[:, 14:16], rowsP[:, 8:10], MUL)
            t_ptm = tt(c, ptm[:].rearrange("p (c w) -> p c w", c=2), ohv3,
                       qb(4, 64), MUL, w=[state.get("ptm_guard")])
            t_dr = tt(c, rr3, identb, qb(6, 128), MUL)
            t_dm2 = tt(c, rm3, identb, qb(7, 128), MUL)
            c.wait("pe", t_dr, t_dm2, state.get("bc_guard"), state["stats_dve"])
            mm(c, pb[5][:, CW:2 * CW], onesf[:], rhsd_r[:, :], start=True, stop=True)
            t_bc2 = mm(c, pb[6][:, CW:2 * CW], onesf[:], rhsd_m[:, :], start=True,
                       stop=True, inc=True)
            c.wait("pe", t_ptm)
            for tch in range(2):
                transp(c, pbb[0:64, 288 + tch * 128:288 + (tch + 1) * 128],
                       ptm[:, tch * 64:(tch + 1) * 64], ident_bf[:], inc=(tch == 1))
            t_pt = c.tok("pe")
            state["ptm_guard"] = t_pt
            vcopy(c, pT_bf[:, 0:128], pbb[0:64, 288:288 + 128],
                  w=[t_pt, state.get("pT_guard")])
            t_pte = vcopy(c, pT_bf[:, 128:256], pbb[0:64, 416:416 + 128])
            state["pbb_guard2"] = t_pte
            vcopy(c, slot(4), pb[5][:, CW:2 * CW], w=[t_bc2])
            t_bce = vcopy(c, slot(6), pb[6][:, CW:2 * CW])
            state["bc_guard"] = t_bce
            c.wait("pe", t_pte)
            t3_hist = []
            for do in range(NP):
                tmp = rhsd_r if do % 2 == 0 else rhsd_m
                fb = pb[3] if do % 2 == 0 else pb[1]
                if do >= 2:
                    c.wait("pe", t3_hist[do - 2])
                elif state.get("f_guard") is not None:
                    c.wait("pe", state["f_guard"])
                t_f = mm(c, fb[:, 0:CW], wdirs[:, do * 128:(do + 1) * 128],
                         pT_bf[:], start=True, stop=True, inc=True)
                tt(c, tmp[:, :], Rcol(do, off), slot(4), MUL,
                   w=[t_bce] if do == 0 else ())
                tt(c, tmp[:, :], tmp[:, :], slot(6), SUB)
                t3 = tt(c, tmp[:, :], tmp[:, :], fb[:, 0:CW], ADD, w=[t_f])
                t3_hist.append(t3)
                ts(c, tmp[:, :], tmp[:, :], gamma[:, do:do + 1], MUL,
                   beta_[:, do:do + 1], ADD)
                tt(c, Rcol(do, off), Rcol(do, off), tmp[:, :], ADD)
            state["f_guard"] = t3_hist[-1]
            state["pT_guard"] = t_f

        fEs(0)
        fEa(0)
        for i in range(fcols):
            if i + 1 < fcols:
                fEs(i + 1)
            fEb(i)
            if i + 1 < fcols:
                fEa(i + 1)
            if i >= 1:
                fD(i - 1)
        fD(fcols - 1)
        c.barrier()

    def sublayer_B():
        t_wq = c.dma("pool", qbs[:, :], qTd[:, :])
        t_wcb = c.dma("pool", cbs[:, :], combd[:, :])
        t_st = None
        for col in range(NCOL):
            off = col * CW
            col_stats(lambda p: Rcol(p, off), 0, 1, extra=[t_st])
            rstd_into(1, 2)
            tt(c, slot(5), slot(0), slot(2), MUL)
            xv = colview(xn_bf)
            tt(c, xv, Rview(off), bcast(slot(2)), MUL)
            t_z = tt(c, xv, xv, bcast(slot(5)), SUB)
            t_st = c.dma("sp", zscr[:, off:off + CW].rearrange("(k p) t -> p k t", p=128),
                         xn_bf[:].rearrange("p (k t) -> p k t", k=NP), waits=[t_z])
            c.wait("dve", t_st)
        c.barrier()
        NTCH = T // 128
        gmil = []
        t_zl = None
        for fh in range(2):
            for tch in range(NTCH):
                gi = fh * NTCH + tch
                buf = gi % 2
                zt = ztT[:, buf * 1024:(buf + 1) * 1024]
                tw = [t_zl] if t_zl is not None else []
                if gi >= 2:
                    tw.append(gmil[gi - 2])
                t_zl = c.dma("sp", zt, zscr[:, tch * 128:(tch + 1) * 128],
                             transpose=True, waits=tw)
                c.wait("pe", t_zl)
                for q in range(4):
                    t_gq = mm(c, pb[q][:, fh * 256:fh * 256 + 256],
                              zt[:, q * 256 + fh * 128:q * 256 + fh * 128 + 128],
                              zt[:, q * 256:(q + 1) * 256],
                              start=(tch == 0), stop=(tch == NTCH - 1),
                              inc=(q == 3))
                gmil.append(t_gq)
            t_gr = c.tok("pe")
            for q in range(4):
                vcopy(c, gram[:, (q * 2 + fh) * 256:(q * 2 + fh) * 256 + 256],
                      pb[q][:, fh * 256:fh * 256 + 256], w=[t_gr])
        t_gf = vcopy(c, gram_bf, gram)
        c.barrier()
        for a in range(4):
            t_wqa = c.dma("sp", wqa, qbs[:, a * 2048:(a + 1) * 2048],
                          waits=[("pe", c.cnt["pe"]), t_wq])
            t_wca = c.dma("sp", wcba, cbs[:, a * 2048:(a + 1) * 2048],
                          waits=[t_wcb, t_gf, ("pe", c.cnt["pe"])])
            c.wait("pe", t_wqa, t_wca)
            t1hist = []
            for gi, (q, gch) in enumerate([(q, g) for q in range(4) for g in range(2)]):
                if gi >= 2:
                    c.wait("pe", t1hist[gi - 2])
                for fh in range(2):
                    t_t1 = mm(c, pb[4 + gi % 2][:, 0:256],
                              gram_bf[:, (q * 2 + fh) * 256 + gch * 128:
                                      (q * 2 + fh) * 256 + gch * 128 + 128],
                              wqa[:, (q * 2 + fh) * 256:
                                  (q * 2 + fh) * 256 + 256],
                              start=(fh == 0), stop=(fh == 1),
                              inc=(fh == 1))
                t1hist.append(act(c, t1T[:, (q * 2 + gch) * 256:(q * 2 + gch) * 256 + 256],
                                  pb[4 + gi % 2][:, 0:256], AF.Identity, w=[t_t1]))
            t_t1e = c.tok("act")
            c.wait("pe", t_t1e)
            for ech in range(2):
                if ech == 1:
                    c.wait("pe", c.tok("act"))
                for k in range(8):
                    t_M = mm(c, pb[6][:, ech * 256:ech * 256 + 256],
                             t1T[:, k * 256 + ech * 128:k * 256 + ech * 128 + 128],
                             wcba[:, k * 256:(k + 1) * 256],
                             start=(k == 0), stop=(k == 7), inc=(k == 7))
                act(c, Msb[:, (a * 2 + ech) * 256:(a * 2 + ech) * 256 + 256],
                    pb[6][:, ech * 256:ech * 256 + 256], AF.Identity, w=[t_M])
            c.wait("pe", c.tok("act"))
        c.barrier()
        for col in range(NCOL // 2):
            off = col * CW
            t_zl2 = c.dma("sp", xn_bf[:].rearrange("p (k t) -> p k t", k=NP),
                          zscr[:, off:off + CW].rearrange("(k p) t -> p k t", p=128))
            c.wait("pe", t_zl2)
            t_ap_hist = []
            for do in range(NP):
                a, gh = do // 2, do % 2
                if do >= 2:
                    c.wait("pe", t_ap_hist[do - 2])
                for ech in range(2):
                    t_ap = mm(c, pb[do % 2][:, 0:CW],
                              Msb[:, (a * 2 + ech) * 256 + gh * 128:
                                  (a * 2 + ech) * 256 + gh * 128 + 128],
                              xn_bf[:, (a * 2 + ech) * CW:(a * 2 + ech + 1) * CW],
                              start=(ech == 0), stop=(ech == 1), inc=(ech == 1))
                t_r = tt(c, Rcol(do, off), Rcol(do, off), pb[do % 2][:, 0:CW], ADD,
                         w=[t_ap])
                t_ap_hist.append(t_r)
            c.barrier()

    order = [p for p in ("A", "F1", "B", "F2") if p in parts]
    did_A = False
    did_Fpro = False
    for part in order:
        if part == "A":
            sublayer_A()
        elif part == "B":
            sublayer_B()
        elif part in ("F1", "F2"):
            if not did_Fpro:
                prologue_F()
                did_Fpro = True
            sublayer_F(part, fcols=NCOL // 2 if part == "F2" else None)

    c.barrier()
    outs = []
    for p in range(NP):
        outs.append(c.dma("sp", Rout[p * 128:(p + 1) * 128, :],
                          R[:, p * T:p * T + T // 2]))
    c.wait("sp", *outs)
    c.play()
    es.close()
    return nc



def _host_prep(inputs):
    ap = {}
    g = lambda k: np.ascontiguousarray(np.asarray(inputs[k], dtype=np.float32))
    ones = np.ones((128, 128), np.float32)
    ident = np.eye(128, dtype=np.float32)
    smalls = np.zeros((16,), np.float32)
    smalls[0:8] = g("L0_ffn_knot_values")
    smalls[8] = g("L0_ffn_temperature")[0]
    smalls[9] = g("L0_ffn_output_scale")[0]
    smalls[10:12] = g("L0_ffn_compress_b2")
    iota64 = np.broadcast_to(np.arange(64, dtype=np.float32), (128, 64))
    clus64 = np.broadcast_to((np.arange(64) // 8).astype(np.float32), (128, 64))
    consts = np.concatenate([ones, ident, np.broadcast_to(smalls, (128, 16)),
                             iota64, clus64], axis=1)
    ap["consts"] = np.ascontiguousarray(consts.astype(np.float32))
    ap["w_inT"] = np.ascontiguousarray(g("L0_attn_a_w_in").T)
    ap["w_outT"] = np.ascontiguousarray(g("L0_attn_a_w_out").T)

    def group_w(wT, grpcols):
        K8, M = wT.shape[0] // 128, wT.shape[1]
        ngrp = M // grpcols
        out = wT.reshape(K8, 128, ngrp, grpcols).transpose(1, 2, 0, 3)
        return np.ascontiguousarray(out.reshape(128, ngrp * K8 * grpcols))

    ap["w_gateT"] = group_w(g("L0_attn_a_w_gate").T, 256)
    ap["w1T"] = group_w(g("L0_attn_a_w_mlp1").T, 256)
    ap["w2T"] = group_w(g("L0_attn_a_w_mlp2").T, 128)
    vecsA = np.zeros((128, 48), np.float32)
    for i, k in enumerate(["ln1_g", "ln1_b", "ln2_g", "ln2_b", "b_gate", "b_mlp2"]):
        vecsA[:, 8 * i:8 * (i + 1)] = g(f"L0_attn_a_{k}").reshape(8, 128).T
    ap["vecsA"] = vecsA
    ap["bmlp1"] = np.ascontiguousarray(g("L0_attn_a_b_mlp1").reshape(32, 128).T)
    ap["eig"] = np.ascontiguousarray(
        np.stack([g("L0_attn_a_log_decay"), g("L0_attn_a_frequency")], axis=1))
    cpl = np.zeros((64, 64), np.float32)
    cp = g("L0_attn_a_coupling")
    for h in range(4):
        cpl[h * 16:(h + 1) * 16, h * 16:(h + 1) * 16] = cp[h].T
    ap["couple"] = cpl
    ap["sigsT_raw"] = np.ascontiguousarray(g("L0_ffn_signatures_raw").T)
    ap["dirsT"] = np.ascontiguousarray(g("L0_ffn_directions").T)
    ap["dirsL"] = g("L0_ffn_directions")
    ap["cw1"] = g("L0_ffn_compress_w1")
    cw2 = g("L0_ffn_compress_w2")
    ap["cw2"] = np.ascontiguousarray(cw2.reshape(2, 128, 2).transpose(1, 0, 2).reshape(128, 4))
    ap["w1f"] = np.ascontiguousarray(g("L0_ffn_spline_w1").transpose(1, 0, 2).reshape(2, 1024))
    w2b = np.zeros((1024, 64), np.float32)
    sw2 = g("L0_ffn_spline_w2")
    for t in range(64):
        w2b[t * 16:(t + 1) * 16, t] = sw2[t, :, 0]
    ap["w2blk"] = w2b
    vecsF = np.zeros((128, 20), np.float32)
    vecsF[:, 0:8] = g("L0_ffn_norm_gamma").reshape(8, 128).T
    vecsF[:, 8:16] = g("L0_ffn_norm_beta").reshape(8, 128).T
    vecsF[:, 16:18] = g("L0_ffn_compress_b1").reshape(2, 128).T
    vecsF[0:2, 18] = g("L0_ffn_compress_b2")
    ap["vecsF"] = vecsF
    ap["b2tmd"] = np.ascontiguousarray(
        np.broadcast_to(g("L0_ffn_spline_b2")[:, 0], (128, 64)).astype(np.float32))
    qs = g("L0_attn_b_queries")
    qT = np.zeros((128, 32 * 256), np.float32)
    for a in range(4):
        for q in range(4):
            blk = qs[a][:, q * 256:(q + 1) * 256].T
            for fch in range(2):
                idx = a * 8 + q * 2 + fch
                qT[:, idx * 256:(idx + 1) * 256] = blk[fch * 128:(fch + 1) * 128, :]
    ap["qTd"] = qT
    cb = g("L0_attn_b_combiners")
    cbd = np.zeros((128, 32 * 256), np.float32)
    for a in range(4):
        for k in range(8):
            idx = a * 8 + k
            cbd[:, idx * 256:(idx + 1) * 256] = cb[a][k * 128:(k + 1) * 128, :]
    ap["combd"] = cbd
    ap["b1fd"] = np.ascontiguousarray(g("L0_ffn_spline_b1").reshape(64, 16).reshape(8, 8, 16).transpose(0, 2, 1).reshape(128, 8) if False else g("L0_ffn_spline_b1").reshape(1024).reshape(8, 128).T)
    return ap


def _mk_core_inputs(ap, Xb, half):
    m = dict(ap)
    if half == 0:
        m["Rin"] = np.ascontiguousarray(Xb.T)
        xprev = Xb[T - 1]
        prevmask, midmask = 0.0, 1.0
    else:
        m["Rin"] = np.ascontiguousarray(
            np.concatenate([Xb[T // 2:], Xb[:T // 2]], axis=0).T)
        xprev = Xb[T // 2 - 1]
        prevmask, midmask = 1.0, 0.0
    halo = np.zeros((128, 10), np.float32)
    halo[:, 0:8] = xprev.reshape(8, 128).T
    halo[:, 8] = prevmask
    halo[:, 9] = midmask
    m["halo"] = halo
    return m


def kernel(**inputs):
    ap = _host_prep(inputs)
    X = np.asarray(inputs["L0_attn_a_x"], dtype=np.float32)
    nc = build_program(("A", "F1", "B", "F2"))
    in_maps = []
    for core in range(8):
        in_maps.append(_mk_core_inputs(ap, X[core // 2], core % 2))
    res = run_bass_kernel_spmd(nc, in_maps, core_ids=list(range(8)))
    out = np.zeros((4, T, D), np.float32)
    for b in range(4):
        out[b, 0:T // 2] = res.results[2 * b]["Rout"].T
        out[b, T // 2:] = res.results[2 * b + 1]["Rout"].T
    return out


# version 20
# speedup vs baseline: 1.1656x; 1.0870x over previous
import numpy as np
from contextlib import ExitStack
import concourse.bass as bass
import concourse.mybir as mybir
from concourse.alu_op_type import AluOpType
from concourse.bass_utils import run_bass_kernel_spmd

F32 = mybir.dt.float32
BF16 = mybir.dt.bfloat16
AF = mybir.ActivationFunctionType
AX = mybir.AxisListType
f32 = np.float32
ADD, MUL, SUB = AluOpType.add, AluOpType.mult, AluOpType.subtract
MAXO, MINO, BYP = AluOpType.max, AluOpType.min, AluOpType.bypass

D = 1024
T = 4096
NP = 8
CW = 256
NCOL = T // CW
CHUNK = 64
EPS = f32(1e-5)

ENGINES = ("pe", "dve", "act", "sp", "pool")


NLANE = 4


class Ctx:
    def __init__(self, nc):
        self.nc = nc
        self.es = ExitStack()
        self.q = {e: [] for e in ENGINES}
        self.cnt = {}
        self.sem = {}
        names = list(ENGINES)
        for qn in ("spq", "poolq", "actq"):
            for l in range(NLANE):
                names.append(f"{qn}{l}")
        for e in names:
            self.sem[e] = self.es.enter_context(nc.semaphore(f"s_{e}"))
            self.cnt[e] = 0
        self.lane_rr = {"spq": 0, "poolq": 0, "actq": 0}
        self.last_wait = {e: {} for e in ENGINES}

    def tok(self, e):
        return (e, self.cnt[e])

    def _waits(self, e, waits):
        out = []
        lw = self.last_wait[e]
        for w in waits:
            if w is None:
                continue
            s, v = w
            if s == e or v <= 0:
                continue
            if lw.get(s, 0) >= v:
                continue
            lw[s] = v
            out.append((s, v))
        return out

    def emit(self, e, fn, waits=(), inc=None):
        import traceback as _tb
        fn._src = "".join(_tb.format_stack(limit=6)[:-2])
        wl = self._waits(e, waits)
        serial = e in ("dve", "act")
        do_inc = serial if inc is None else inc
        selfwait = self.cnt[e] if (serial and self.cnt[e] > 0) else None
        if do_inc:
            self.cnt[e] += 1
        n = self.cnt[e]
        self.q[e].append((wl, fn, n if do_inc else None, selfwait))
        return (e, n)

    def dma(self, e, out, in_, waits=(), **kw):
        q = {"sp": "spq", "pool": "poolq", "act": "actq"}[e]
        lane = self.lane_rr[q]
        self.lane_rr[q] = (lane + 1) % NLANE
        ln = f"{q}{lane}"
        wl = self._waits(e, list(waits) + [(ln, self.cnt[ln])])
        self.cnt[ln] += 16
        n = self.cnt[ln]
        sem = self.sem[ln]

        def fn(eng, out=out, in_=in_, kw=kw):
            return eng.dma_start(out=out, in_=in_, **kw)

        self.q[e].append((wl, fn, ("dma", sem), None))
        return (ln, n)

    def wait(self, e, *toks):
        wl = self._waits(e, [t for t in toks if t is not None])
        if wl:
            self.q[e].append((wl, None, None, None))

    def barrier(self):
        toks = [(s_, self.cnt[s_]) for s_ in self.sem if s_ not in ("sp", "pool")]
        for e in ENGINES:
            self.wait(e, *toks)

    def play(self):
        nc = self.nc
        sems = self.sem

        def mk(e):
            queue = self.q[e]

            def body(eng):
                for wl, fn, inc, selfwait in queue:
                    if selfwait:
                        eng.wait_ge(sems[e], selfwait)
                    for (s, v) in wl:
                        eng.wait_ge(sems[s], v)
                    if fn is None:
                        continue
                    try:
                        inst = fn(eng)
                    except Exception:
                        import traceback
                        print("FAILING OP on", e, "emitted at:")
                        print(getattr(fn, "_src", "?"))
                        raise
                    if inst is None:
                        continue
                    if isinstance(inc, tuple):
                        inst.then_inc(inc[1], 16)
                    elif inc is not None:
                        inst.then_inc(sems[e], 1)
            return body

        with nc.Block() as block:
            block.tensor(mk("pe"))
            block.vector(mk("dve"))
            block.scalar(mk("act"))
            block.sync(mk("sp"))
            block.gpsimd(mk("pool"))
        self.es.close()


def tt(c, out, a, b, op, w=()):
    return c.emit("dve", lambda v: v.tensor_tensor(out=out, in0=a, in1=b, op=op), w)


def _sc(x):
    if isinstance(x, (np.floating, np.integer)):
        return float(x)
    return x


def ts(c, out, a, s1, op0, s2=None, op1=None, w=()):
    kw = {}
    if op1 is not None:
        kw["op1"] = op1
    s1 = _sc(s1); s2 = _sc(s2)
    return c.emit("dve", lambda v: v.tensor_scalar(out=out, in0=a, scalar1=s1,
                                                   scalar2=s2, op0=op0, **kw), w)


def stt(c, out, in0, sc, in1, op0, op1, w=()):
    sc = _sc(sc)
    return c.emit("dve", lambda v: v.scalar_tensor_tensor(
        out=out, in0=in0, scalar=sc, in1=in1, op0=op0, op1=op1), w)


def vcopy(c, out, a, w=()):
    return c.emit("dve", lambda v: v.tensor_copy(out=out, in_=a), w)


def vmemset(c, ap, val, w=()):
    return c.emit("dve", lambda v: v.memset(ap, val), w)


def vred(c, out, a, op, w=()):
    return c.emit("dve", lambda v: v.tensor_reduce(out=out, in_=a, axis=AX.X, op=op), w)


def vmax8(c, out, a, w=()):
    return c.emit("dve", lambda v: v.max(out=out, in_=a), w)


def vmaxidx(c, out, mx, vals, w=()):
    return c.emit("dve", lambda v: v.max_index(out=out, in_max=mx, in_values=vals), w)


def act(c, out, a, func, bias=None, scale=None, w=()):
    kw = {}
    if bias is not None:
        kw["bias"] = np.float32(bias) if isinstance(bias, (float, np.floating)) else bias
    if scale is not None:
        kw["scale"] = np.float32(scale) if isinstance(scale, (float, np.floating)) else scale
    return c.emit("act", lambda s: s.activation(out=out, in_=a, func=func, **kw), w)


def mm(c, out, lhsT, rhs, start, stop, w=(), inc=False):
    return c.emit("pe", lambda t: t.matmul(out, lhsT, rhs, start=start, stop=stop),
                  w, inc=inc)


def transp(c, out, in_, ident, w=(), inc=False):
    return c.emit("pe", lambda t: t.transpose(out, in_, ident), w, inc=inc)


def bcast(ap_, np_=NP):
    return ap_.unsqueeze(1).broadcast_to((ap_.shape[0], np_, ap_.shape[1]))


def bcast_t(ap_, n):
    return ap_.unsqueeze(2).broadcast_to((ap_.shape[0], ap_.shape[1], n))



def build_program(parts):
    nc = bass.Bass("TRN2", target_bir_lowering=False)
    es = ExitStack()
    c = Ctx(nc)

    def dram(name, shape, dt=F32, kind="ExternalInput"):
        return nc.dram_tensor(name, shape, dt, kind=kind).ap()

    def sb(name, shape, dt=F32):
        return es.enter_context(nc.sbuf_tensor(name, shape, dt))

    def psum(name, shape, dt=F32):
        return es.enter_context(nc.psum_tensor(name, shape, dt))

    need_A = "A" in parts
    need_F = ("F1" in parts) or ("F2" in parts)
    need_B = "B" in parts

    Rin = dram("Rin", [NP * 128, T])
    Rout = dram("Rout", [NP * 128, T // 2], kind="ExternalOutput")
    consts = dram("consts", [128, 400])

    if need_B:
        qTd = dram("qTd", [128, 32 * 256])
        combd = dram("combd", [128, 32 * 256])
        zscr = dram("zscr", [D, T], BF16, kind="Internal")
        qbs = dram("qbs", [128, 32 * 256], BF16, kind="Internal")
        cbs = dram("cbs", [128, 32 * 256], BF16, kind="Internal")
    if need_F:
        sigsT_raw = dram("sigsT_raw", [D, 64])
        dirsT = dram("dirsT", [D, 64])
        dirsL = dram("dirsL", [64, D])
        cw1 = dram("cw1", [D, 256])
        cw2 = dram("cw2", [128, 4])
        w1f = dram("w1f", [2, 1024])
        w2blk = dram("w2blk", [1024, 64])
        vecsF = dram("vecsF", [128, 20])
        b2tmd = dram("b2tmd", [128, 64])
        b1fd = dram("b1fd", [128, 8])
    if need_A:
        w_inT = dram("w_inT", [D, 128])
        w_gateT = dram("w_gateT", [128, 8192])
        w_outT = dram("w_outT", [128, D])
        w1T = dram("w1T", [128, 32768])
        w2T = dram("w2T", [128, 32768])
        vecsA = dram("vecsA", [128, 48])
        bmlp1 = dram("bmlp1", [128, 32])
        eig = dram("eig", [64, 2])
        couple = dram("couple", [64, 64])
        halo = dram("halo", [128, 10])
        wgs = dram("wgs", [128, 8192], BF16, kind="Internal")
        w1s = dram("w1s", [128, 32768], BF16, kind="Internal")
        w2s = dram("w2s", [128, 32768], BF16, kind="Internal")

    R = sb("R", [128, NP * T])
    ones_bf = sb("ones_bf", [128, 128], BF16)
    onesf = sb("onesf", [128, 128])
    ident_bf = sb("ident_bf", [128, 128], BF16)
    smallsB = sb("smallsB", [128, 12])
    xn_bf = sb("xn_bf", [128, NP * CW], BF16)
    if need_A or need_F:
        u32 = sb("u32", [128, NP * CW])
    rbf = xn_bf[:, 0:NP * CW]
    sqbf = sb("sqbf", [128, NP * CW], BF16)
    bc = sb("bc", [128, 8 * CW])
    hid_bf = sb("hid_bf", [128, 32 * CW], BF16)
    wbig = sb("wbig", [128, 2 * 2048], BF16)
    wbig2 = sb("wbig2", [128, 3 * 1024], BF16)
    if need_B:
        gram = hid_bf[:, 0:4096].bitcast(F32)
        gram_bf = hid_bf[:, 4096:6144]
        t1T = hid_bf[:, 6144:8192]
        Msb = wbig[:, 0:2048]
        ztT = wbig[:, 2048:4096]
        wqa = wbig2[:, 0:2048]
        wcba = hid_bf[:, 0:2048]
    if need_A or need_F:
        x2n_bf = sb("x2n_bf", [128, NP * CW], BF16)
        gate_bf = sb("gate_bf", [128, NP * CW], BF16)
    if need_A:
        ccb = sb("ccb", [128, CW], BF16)
        carry = sb("carry", [64, 2])
        lam = sb("lam", [64, 20])
        lamt = sb("lamt", [64, 4])
        vA = sb("vA", [128, 48])
        bm1 = sb("bm1", [128, 32])
        eig_sb = sb("eig_sb", [64, 2])
        couple_sb = sb("couple_sb", [64, 64])
        winsb = sb("winsb", [128, 8 * 128], BF16)
        halo_sb = sb("halo_sb", [128, 10])
        woutsb = sb("woutsb", [128, 8 * 128], BF16)

    if need_F:
        wsc = sb("wsc", [128, 8 * 72], BF16)
        wdT = sb("wdT", [128, 8 * 64], BF16)
        wdirs = sb("wdirs", [64, D], BF16)
        wc1 = sb("wc1", [128, 8 * 256], BF16)
        wc2 = sb("wc2", [128, 4], BF16)
        ww1f = sb("ww1f", [2, 1024], BF16)
        ww2blk = sb("ww2blk", [128, 8 * 64], BF16)
        vF = sb("vF", [128, 20])
        b1f_sb = sb("b1f_sb", [128, 8])
        knotsB = sb("knotsB", [128, 4])
        dB = sb("dB", [128, 7])
        calF = sb("calF", [128, 8])
        tblp = x2n_bf[:, 0:640]
        prodp = x2n_bf[:, 640:1280]
        extrF = x2n_bf[:, 640:1152].bitcast(F32)
        rhsd_r = x2n_bf[:, 1280:1792].bitcast(F32)
        rowsP = gate_bf[:, 0:64].bitcast(F32)
        mxcB = gate_bf[:, 64:96]
        idxfB = gate_bf[:, 96:160].bitcast(F32)
        mxtF = gate_bf[:, 160:224].bitcast(F32)
        mskg = gate_bf[:, 224:480].bitcast(F32)
        magsbP = [gate_bf[:, 480:736].bitcast(F32),
                  gate_bf[:, 736:992].bitcast(F32)]
        rowsEP = [gate_bf[:, 992:1008].bitcast(F32),
                  gate_bf[:, 1008:1024].bitcast(F32)]
        egl = hid_bf[:, 4096:4608]
        xgl = hid_bf[:, 4608:5120]
        rhsd_m = gate_bf[:, 1536:2048].bitcast(F32)
        b2tm = gate_bf[:, 1184:1248]
        drows = gate_bf[0:64, 1248:1250]
        iota64 = gate_bf[:, 1252:1316]
        clus64 = gate_bf[:, 1316:1380]
        idxu = gate_bf[:, 1380:1444].bitcast(mybir.dt.uint32)
        uu = hid_bf[:, 2048:2560].bitcast(F32)[0:72, :]
        gts = hid_bf[:, 2560:3072].bitcast(F32)[0:72, :]
        gts_bf = hid_bf[0:72, 3072:3328]
        zd_bf = hid_bf[0:64, 3328:3584]
        h1_bf = hid_bf[:, 3584:4096]
        gtmP = [hid_bf[:, 5120:5280], gate_bf[:, 1024:1184]]
        ptm = hid_bf[:, 5408:5536]
        pT_bf = hid_bf[0:64, 5536:5792]
        oh = hid_bf[:, 5792:6304].bitcast(F32)
        comp_bf = hid_bf[0:2, 6304:6560]

    pb = [psum(f"pb{i}", [128, 512]) for i in range(7)]
    pbb = psum("pbb", [128, 1024], BF16)

    t_ones = c.dma("pool", ones_bf[:], consts[:, 0:128])
    t_onesf = c.dma("pool", onesf[:], consts[:, 0:128])
    t_id = c.dma("pool", ident_bf[:], consts[:, 128:256])
    t_sm = c.dma("pool", smallsB[:], consts[:, 256:268])
    for p in range(NP):
        t_R = c.dma("sp", R[:, p * T:(p + 1) * T], Rin[p * 128:(p + 1) * 128, :])

    c.barrier()

    def slot(i):
        return bc[:, i * CW:(i + 1) * CW]

    if need_A:
        cc = bc[0:64, 0:2 * CW]
        cct = bc[0:64, 3 * CW:4 * CW]

    state = {"stats_pe": None, "stats_dve": None, "bc_guard": None}

    def col_stats_mm(src, extra=(), bs=None):
        b1, b2 = bs if bs else (pb[5][:, 0:CW], pb[6][:, 0:CW])
        for p in range(NP):
            act(c, sqbf[:, p * CW:(p + 1) * CW], src(p), AF.Square,
                w=([state["stats_pe"]] + list(extra)) if p == 0 else ())
        t_sq = c.tok("act")
        c.wait("pe", t_sq, t_onesf, state["stats_dve"])
        for p in range(NP):
            mm(c, b1, onesf[:], src(p),
               start=(p == 0), stop=(p == NP - 1))
        last = None
        for p in range(NP):
            last = mm(c, b2, ones_bf[:], sqbf[:, p * CW:(p + 1) * CW],
                      start=(p == 0), stop=(p == NP - 1), inc=(p == NP - 1))
        state["stats_pe"] = last
        return last

    def col_stats_fin(mslot, vslot, last, bs=None):
        b1, b2 = bs if bs else (pb[5][:, 0:CW], pb[6][:, 0:CW])
        ts(c, slot(mslot), b1, f32(1.0 / D), MUL, w=[last])
        state["stats_dve"] = ts(c, slot(vslot), b2, f32(1.0 / D), MUL)
        tt(c, slot(7), slot(mslot), slot(mslot), MUL)
        return tt(c, slot(vslot), slot(vslot), slot(7), SUB)

    def col_stats(src, mslot, vslot, extra=()):
        last = col_stats_mm(src, extra)
        return col_stats_fin(mslot, vslot, last)

    def rstd_into(vslot, oslot):
        act(c, slot(oslot), slot(vslot), AF.Ln, bias=EPS, w=[c.tok("dve")])
        a = act(c, slot(oslot), slot(oslot), AF.Exp, scale=f32(-0.5))
        c.wait("dve", a)
        return a

    def Rcol(p, off, n=CW):
        return R[:, p * T + off: p * T + off + n]

    def Rview(off, n=CW):
        return R[:].rearrange("q (p t) -> q p t", p=NP)[:, :, off:off + n]

    def colview(tensor, npt=NP, n=CW):
        return tensor[:].rearrange("q (p t) -> q p t", p=npt)

    def sublayer_A():
        t_vA = c.dma("pool", vA[:], vecsA[:, :])
        t_bm1 = c.dma("pool", bm1[:], bmlp1[:, :])
        t_eig = c.dma("pool", eig_sb[:], eig[:, :])
        t_cp = c.dma("pool", couple_sb[:], couple[:, :])
        t_win = c.dma("pool", winsb[:].rearrange("p (k m) -> p k m", k=8),
                      w_inT[:, :].rearrange("(k p) m -> p k m", p=128))
        t_wout = c.dma("pool", woutsb[:], w_outT[:, :])
        t_wg = c.dma("pool", wgs[:, :], w_gateT[:, :])
        t_w1 = c.dma("pool", w1s[:, :], w1T[:, :])
        t_w2 = c.dma("pool", w2s[:, :], w2T[:, :])
        g1 = vA[:, 0:8]; b1 = vA[:, 8:16]; g2 = vA[:, 16:24]; b2 = vA[:, 24:32]
        bgate = vA[:, 32:40]; bmlp2v = vA[:, 40:48]

        ld_c = eig_sb[:, 0:1]; fq = eig_sb[:, 1:2]
        mag = lamt[:, 0:1]; fr = lamt[:, 1:2]; q0 = lamt[:, 2:3]; q1 = lamt[:, 3:4]
        act(c, mag, ld_c, AF.Sigmoid, w=[t_eig])
        vcopy(c, fr, fq, w=[t_eig])
        for _ in range(6):
            act(c, q0, fr, AF.Abs, w=[c.tok("dve")])
            act(c, q0, q0, AF.Sign, bias=f32(-np.pi))
            ts(c, q0, q0, f32(0.0), MAXO, w=[c.tok("act")])
            act(c, q1, fr, AF.Sign, w=[c.tok("dve")])
            tt(c, q0, q0, q1, MUL, w=[c.tok("act")])
            stt(c, fr, q0, f32(-2.0 * np.pi), fr, MUL, ADD)
        act(c, lam[:, 16:17], fr, AF.Sin, w=[c.tok("dve")])
        ts(c, q0, fr, f32(-1.0), MUL, f32(np.pi / 2), ADD, w=[c.tok("act")])
        act(c, q1, q0, AF.Sign, bias=f32(-np.pi), w=[c.tok("dve")])
        ts(c, q1, q1, f32(0.0), MAXO, w=[c.tok("act")])
        stt(c, q0, q1, f32(-2.0 * np.pi), q0, MUL, ADD)
        act(c, lam[:, 17:18], q0, AF.Sin, w=[c.tok("dve")])
        tt(c, lam[:, 0:1], mag, lam[:, 17:18], MUL, w=[c.tok("act")])
        tt(c, lam[:, 1:2], mag, lam[:, 16:17], MUL)
        for i in range(1, 6):
            pr = lam[:, 2 * i - 2:2 * i - 1]; pi_ = lam[:, 2 * i - 1:2 * i]
            qr = lam[:, 2 * i:2 * i + 1]; qi = lam[:, 2 * i + 1:2 * i + 2]
            tt(c, q0, pr, pr, MUL)
            tt(c, q1, pi_, pi_, MUL)
            tt(c, qr, q0, q1, SUB)
            tt(c, q0, pr, pi_, MUL)
            ts(c, qi, q0, f32(2.0), MUL)
        for i in range(6):
            ts(c, lam[:, 12 + i:13 + i], lam[:, 2 * i + 1:2 * i + 2], f32(-1.0), MUL)
        t_lam = c.tok("dve")
        t_halo = c.dma("pool", halo_sb[:], halo[:, :])
        hb = slot(6)
        act(c, hb[:, 16:24], halo_sb[:, 0:8], AF.Square, w=[t_halo])
        t_hsq = c.tok("act")
        c.wait("pe", t_halo, t_hsq, t_onesf)
        for p in range(NP):
            mm(c, pb[2][:, 0:1], onesf[:], halo_sb[:, p:p + 1],
               start=(p == 0), stop=(p == NP - 1))
        for p in range(NP):
            t_hs2 = mm(c, pb[2][:, 1:2], onesf[:], hb[:, 16 + p:17 + p],
                       start=(p == 0), stop=(p == NP - 1), inc=(p == NP - 1))
        ts(c, hb[:, 0:1], pb[2][:, 0:1], f32(1.0 / D), MUL, w=[t_hs2])
        ts(c, hb[:, 1:2], pb[2][:, 1:2], f32(1.0 / D), MUL)
        tt(c, hb[:, 2:3], hb[:, 0:1], hb[:, 0:1], MUL)
        tt(c, hb[:, 1:2], hb[:, 1:2], hb[:, 2:3], SUB)
        act(c, hb[:, 2:3], hb[:, 1:2], AF.Ln, bias=EPS, w=[c.tok("dve")])
        t_ha = act(c, hb[:, 2:3], hb[:, 2:3], AF.Exp, scale=f32(-0.5))
        tt(c, hb[:, 3:4], hb[:, 2:3], hb[:, 2:3], MUL, w=[t_ha])
        tt(c, hb[:, 4:5], hb[:, 1:2], hb[:, 3:4], MUL)
        act(c, hb[:, 5:6], hb[:, 4:5], AF.Ln, bias=EPS, w=[c.tok("dve")])
        t_ha2 = act(c, hb[:, 5:6], hb[:, 5:6], AF.Exp, scale=f32(-0.5))
        tt(c, hb[:, 6:7], hb[:, 2:3], hb[:, 5:6], MUL, w=[t_ha2])
        ts(c, hb[:, 8:16], halo_sb[:, 0:8], hb[:, 0:1], AluOpType.subtract)
        ts(c, hb[:, 8:16], hb[:, 8:16], hb[:, 6:7], MUL)
        tt(c, hb[:, 8:16], hb[:, 8:16], vA[:, 0:8], MUL, w=[t_vA])
        tt(c, hb[:, 8:16], hb[:, 8:16], vA[:, 8:16], ADD)
        t_xnp = vcopy(c, ccb[:, 0:8], hb[:, 8:16])
        c.wait("pe", t_xnp, t_win)
        for k in range(NP):
            mm(c, pb[2][0:64, 2:3], winsb[:, k * 128:k * 128 + 64],
               ccb[:, k:k + 1], start=(k == 0), stop=(k == NP - 1))
        for k in range(NP):
            t_bp = mm(c, pb[2][0:64, 3:4], winsb[:, k * 128 + 64:(k + 1) * 128],
                      ccb[:, k:k + 1], start=(k == 0), stop=(k == NP - 1),
                      inc=(k == NP - 1))
        ts(c, carry[:, 0:1], pb[2][0:64, 2:3], halo_sb[0:64, 8:9], MUL, w=[t_bp])
        t_carry = ts(c, carry[:, 1:2], pb[2][0:64, 3:4], halo_sb[0:64, 8:9], MUL)
        c.wait("pe", t_carry)

        c.wait("pe", t_win, t_wout, t_cp, t_R)
        c.wait("dve", t_vA, t_bm1, t_lam, t_carry)
        c.wait("act", t_vA, t_bm1, t_R)
        prevA = {"gate_pe": None, "x1_dve": None, "ccb_act": None,
                 "mlp2_dve": None, "m1_last": None, "u32_act": None,
                 "gate_act": None, "stats_dve_g": None}
        colstate = {}

        def slab1(col):
            off = col * CW
            col_stats(lambda p: Rcol(p, off), 0, 1,
                      extra=[prevA["gate_pe"]])
            rstd_into(1, 2)
            tt(c, slot(7), slot(2), slot(2), MUL)
            tt(c, slot(3), slot(1), slot(7), MUL)
            rstd_into(3, 3)
            tt(c, slot(4), slot(2), slot(3), MUL)
            tt(c, slot(5), slot(0), slot(4), MUL)
            tt(c, slot(6), slot(0), slot(2), MUL)
            prevA["stats_dve_g"] = state["stats_dve"]
            xv = colview(xn_bf)
            tt(c, xv, Rview(off), bcast(slot(4)), MUL, w=[prevA["gate_pe"]])
            t_xnp = tt(c, xv, xv, bcast(slot(5)), SUB)
            for p in range(NP):
                t_xn = act(c, xn_bf[:, p * CW:(p + 1) * CW], xn_bf[:, p * CW:(p + 1) * CW],
                           AF.Identity, scale=g1[:, p:p + 1], bias=b1[:, p:p + 1],
                           w=[t_xnp])
            uv = colview(u32)
            tt(c, uv, Rview(off), bcast(slot(2)), MUL, w=[prevA["u32_act"]])
            t_y = tt(c, uv, uv, bcast(slot(6)), SUB)
            c.wait("pe", t_xn, prevA["x1_dve"])
            for k in range(NP):
                mm(c, pb[2][0:64, 0:CW], winsb[:, k * 128:k * 128 + 64],
                   xn_bf[:, k * CW:(k + 1) * CW], start=(k == 0),
                   stop=(k == NP - 1))
            for k in range(NP):
                t_beta = mm(c, pb[2][0:64, CW:2 * CW], winsb[:, k * 128 + 64:(k + 1) * 128],
                            xn_bf[:, k * CW:(k + 1) * CW], start=(k == 0),
                            stop=(k == NP - 1), inc=(k == NP - 1))
            t_c0 = vcopy(c, cc[:, :], pb[2][0:64, 0:2 * CW], w=[t_beta])
            cr = cc[:, 0:CW]; ci = cc[:, CW:2 * CW]
            tmp = cct[:, 0:CW]
            d, i = 1, 0
            while d < CHUNK:
                pr = lam[:, 2 * i:2 * i + 1]
                pi_ = lam[:, 2 * i + 1:2 * i + 2]
                nli = lam[:, 12 + i:13 + i]
                crv = cr.rearrange("k (n s) -> k n s", s=CHUNK)
                civ = ci.rearrange("k (n s) -> k n s", s=CHUNK)
                tmv = tmp.rearrange("k (n s) -> k n s", s=CHUNK)
                dst_r = crv[:, :, 0:CHUNK - d]; src_r = crv[:, :, d:CHUNK]
                dst_i = civ[:, :, 0:CHUNK - d]; src_i = civ[:, :, d:CHUNK]
                tm = tmv[:, :, 0:CHUNK - d]
                vcopy(c, tm, src_r)
                stt(c, dst_r, src_r, pr, dst_r, MUL, ADD)
                stt(c, dst_r, src_i, nli, dst_r, MUL, ADD)
                stt(c, dst_i, src_i, pr, dst_i, MUL, ADD)
                stt(c, dst_i, tm, pi_, dst_i, MUL, ADD)
                d *= 2
                i += 1
            if col == NCOL // 2:
                ts(c, carry[:], carry[:], halo_sb[0:64, 9:10], MUL)
            nch = CW // CHUNK
            for half in range(2):
                cch = cc[:, half * CW:(half + 1) * CW]
                ccv2 = cch.rearrange("k (n s) -> k n s", s=CHUNK)
                bv = pb[2][0:64, half * CW:(half + 1) * CW].rearrange(
                    "k (n s) -> k n s", s=CHUNK)
                tt(c, ccv2[:, 1:nch, 0:1], ccv2[:, 1:nch, 0:1],
                   bv[:, 0:nch - 1, CHUNK - 1:CHUNK], ADD)
                tt(c, cch[:, 0:1], cch[:, 0:1], carry[:, half:half + 1], ADD)
                vcopy(c, carry[:, half:half + 1],
                      pb[2][0:64, (half + 1) * CW - 1:(half + 1) * CW])
            colstate[col] = [None, t_y, None, None, None, ([], []), c.tok("dve")]

        def slab1b(col):
            c.wait("pe", c.tok("dve"), prevA["ccb_act"])
            mm(c, pb[2][0:64, 0:CW], couple_sb[:], cc[:, 0:CW], start=True, stop=True)
            t_cpl = mm(c, pb[2][64:128, 0:CW], couple_sb[:], cc[:, CW:2 * CW],
                       start=True, stop=True, inc=True)
            act(c, ccb[0:64, :], pb[2][0:64, 0:CW], AF.Identity, w=[t_cpl])
            t_ccb = act(c, ccb[64:128, :], pb[2][64:128, 0:CW], AF.Identity)
            prevA["ccb_act"] = t_ccb
            colstate[col][0] = t_ccb

        def gate_step(col, do):
            gst = colstate[col][5]
            gate_mil, t_gb_hist = gst
            grp, ds = do // 2, do % 2
            if ds == 0:
                wslice = wbig[:, grp % 2 * 2048:(grp % 2 + 1) * 2048]
                t_wgl = c.dma("sp", wslice,
                              wgs[:, grp * 2048:(grp + 1) * 2048],
                              waits=[gate_mil[grp - 2], t_wg] if grp >= 2
                                    else [t_wg, prevA["m1_last"]])
                c.wait("pe", t_wgl)
            wslice = wbig[:, grp % 2 * 2048:(grp % 2 + 1) * 2048]
            gb_bank = pb[3] if do % 2 == 0 else pb[5]
            if do >= 2:
                c.wait("pe", t_gb_hist[do - 2])
            elif do == 0:
                c.wait("pe", prevA["gate_act"], prevA["stats_dve_g"])
            for k in range(NP):
                t_g = mm(c, gb_bank[:, 0:CW],
                         wslice[:, (k * 256 + ds * 128):(k * 256 + ds * 128 + 128)],
                         xn_bf[:, k * CW:(k + 1) * CW], start=(k == 0),
                         stop=(k == NP - 1), inc=(k == NP - 1))
            if ds == 1:
                gate_mil.append(t_g)
            t_gb = act(c, gate_bf[:, do * CW:(do + 1) * CW], gb_bank[:, 0:CW],
                       AF.Sigmoid, bias=bgate[:, do:do + 1],
                       w=[t_g, colstate[col][6]])
            t_gb_hist.append(t_gb)

        def slab2a(col):
            off = col * CW
            st = colstate[col]
            t_ccb, t_y = st[0], st[1]
            gate_mil, t_gb_hist = st[5]
            c.wait("pe", t_ccb, t_y)
            c.wait("pe", t_gb_hist[-2])
            t_h_hist = []
            t_hx1_hist = []
            for do in range(NP):
                hb = pb[2] if do % 2 == 0 else pb[4]
                if do >= 2:
                    c.wait("pe", t_hx1_hist[do - 2])
                t_h = mm(c, hb[:, 0:CW], woutsb[:, do * 128:(do + 1) * 128],
                         ccb[:], start=True, stop=True, inc=True)
                t_h_hist.append(t_h)
                tt(c, slot(7), gate_bf[:, do * CW:(do + 1) * CW], hb[:, 0:CW], MUL,
                   w=[t_gb_hist[do], t_h])
                t_hx1 = tt(c, u32[:, do * CW:(do + 1) * CW],
                           u32[:, do * CW:(do + 1) * CW], slot(7), ADD)
                t_hx1_hist.append(t_hx1)
            gate_mil.append(t_h_hist[-1])
            tt(c, Rview(off), Rview(off), colview(u32), ADD, w=[t_hx1])
            col_stats(lambda p: u32[:, p * CW:(p + 1) * CW], 0, 1,
                      extra=[c.tok("dve")])
            rstd_into(1, 2)
            tt(c, slot(5), slot(0), slot(2), MUL)
            uv2 = colview(u32)
            tt(c, uv2, uv2, bcast(slot(2)), MUL)
            t_x2p = tt(c, uv2, uv2, bcast(slot(5)), SUB)
            for p in range(NP):
                t_x2n = act(c, x2n_bf[:, p * CW:(p + 1) * CW], u32[:, p * CW:(p + 1) * CW],
                            AF.Identity, scale=g2[:, p:p + 1], bias=b2[:, p:p + 1],
                            w=[t_x2p])
            prevA["gate_pe"] = gate_mil[-1]
            prevA["gate_act"] = t_gb_hist[-1]
            prevA["x1_dve"] = t_hx1
            prevA["u32_act"] = c.tok("act")
            st[2] = t_x2n
            st[3] = gate_mil
            st[4] = t_hx1

        def slab2b1(col):
            off = col * CW
            st = colstate[col]
            t_x2n, gate_mil = st[2], st[3]
            c.wait("pe", t_x2n, t_w1)
            t_sil = c.tok("act")
            m1_mil = []
            for h in range(32):
                wslice = wbig[:, (h % 4) * 1024:(h % 4 + 1) * 1024]
                t_wl = c.dma("sp", wslice,
                             w1s[:, h * 1024:(h + 1) * 1024],
                             waits=[m1_mil[h - 4], t_w1] if h >= 4
                                   else [t_w1, gate_mil[-1]])
                c.wait("pe", t_wl)
                mb = (pb[0], pb[1], pb[4])[h % 3]
                if h >= 3:
                    c.wait("pe", t_sil_hist[h - 3])
                for k in range(NP):
                    t_m1 = mm(c, mb[:, 0:CW],
                              wslice[:, k * 128:(k + 1) * 128],
                              x2n_bf[:, k * CW:(k + 1) * CW], start=(k == 0),
                              stop=(k == NP - 1), inc=(k == NP - 1))
                m1_mil.append(t_m1)
                t_sil = act(c, hid_bf[:, h * CW:(h + 1) * CW], mb[:, 0:CW],
                            AF.Silu, bias=bm1[:, h:h + 1], w=[t_m1])
                if h == 0:
                    t_sil_hist = []
                t_sil_hist.append(t_sil)
            colstate[col][2] = (t_sil, gate_mil, m1_mil)

        def slab2b2(col):
            off = col * CW
            t_sil, gate_mil, m1_mil = colstate[col][2]
            colstate.pop(col)
            prevA["m1_last"] = m1_mil[-1]
            c.wait("pe", t_sil, t_w2)
            t_radd = None
            m2_mil = []
            for do in range(NP):
                buf = do % 2
                if do >= 2:
                    c.wait("pe", t_radd_hist[do - 2])
                for q in range(4):
                    qi = do * 4 + q
                    wslice = wbig2[:, (qi % 3) * 1024:(qi % 3 + 1) * 1024]
                    t_wl2 = c.dma("sp", wslice,
                                  w2s[:, qi * 1024:(qi + 1) * 1024],
                                  waits=[m2_mil[qi - 3], t_w2] if qi >= 3
                                        else [t_w2, t_sil, gate_mil[-1]])
                    c.wait("pe", t_wl2)
                    for k8 in range(8):
                        k = q * 8 + k8
                        t_m2 = mm(c, pb[0 + buf][:, 0:CW], wslice[:, k8 * 128:(k8 + 1) * 128],
                                  hid_bf[:, k * CW:(k + 1) * CW], start=(k == 0),
                                  stop=(k == 31), inc=(k8 == 7))
                    m2_mil.append(t_m2)
                t_radd = stt(c, Rcol(do, off), pb[0 + buf][:, 0:CW], bmlp2v[:, do:do + 1],
                             Rcol(do, off), ADD, ADD, w=[t_m2])
                if do == 0:
                    t_radd_hist = []
                t_radd_hist.append(t_radd)
                if col + 1 in colstate:
                    gate_step(col + 1, do)
            prevA["mlp2_dve"] = t_radd

        slab1(0)
        slab1b(0)
        for do in range(NP):
            gate_step(0, do)
        for col in range(NCOL):
            slab2a(col)
            if col + 1 < NCOL:
                slab1(col + 1)
            slab2b1(col)
            if col + 1 < NCOL:
                slab1b(col + 1)
            slab2b2(col)
        c.barrier()

    def prologue_F():
        t_vF = c.dma("pool", vF[:], vecsF[:, :])
        t_b1f = c.dma("pool", b1f_sb[:], b1fd[:, :])
        t_wsc0 = c.dma("pool", wdT[:].rearrange("p (k m) -> p k m", k=8),
                       dirsT[:, :].rearrange("(k p) m -> p k m", p=128))
        t_wdir = c.dma("pool", wdirs[:], dirsL[:, :])
        ts(c, wdirs[:], wdirs[:], smallsB[0:64, 9:10], MUL, w=[t_wdir, t_sm])
        t_wc1 = c.dma("pool", wc1[:].rearrange("p (k m) -> p k m", k=8),
                      cw1[:, :].rearrange("(k p) m -> p k m", p=128))
        t_wc2 = c.dma("pool", wc2[:], cw2[:, :])
        t_w1f = c.dma("pool", ww1f[:], w1f[:, :])
        t_w2b = c.dma("pool", ww2blk[:].rearrange("p (k m) -> p k m", k=8),
                      w2blk[:, :].rearrange("(k p) m -> p k m", p=128))
        t_io = c.dma("pool", iota64[:], consts[:, 272:336])
        t_cl = c.dma("pool", clus64[:], consts[:, 336:400])
        t_b2 = c.dma("pool", b2tm[:], b2tmd[:, :])
        sraw = hid_bf[:, 0:1024].bitcast(F32)
        qa = hid_bf[:, 1024:2048].bitcast(F32)
        t_sr = c.dma("pool", sraw[:].rearrange("p (k m) -> p k m", k=8),
                     sigsT_raw[:, :].rearrange("(k p) m -> p k m", p=128))
        c.wait("act", t_sr)
        act(c, qa[:], sraw[:], AF.Sign, bias=f32(-0.3))
        t_q1 = c.tok("act")
        ts(c, qa[:], qa[:], 0.0, MAXO, w=[t_q1])
        act(c, sraw[:], sraw[:], AF.Sign, bias=f32(-0.3), scale=f32(-1.0),
            w=[c.tok("dve")])
        t_q2 = c.tok("act")
        ts(c, sraw[:], sraw[:], 0.0, MAXO, w=[t_q2])
        wscv = wsc[:].rearrange("p (k m) -> p k m", m=72)
        qv = qa[:].rearrange("p (k m) -> p k m", m=64)
        sv_ = sraw[:].rearrange("p (k m) -> p k m", m=64)
        tt(c, qv, qv, sv_, SUB)
        t_q = tt(c, wscv[:, :, 0:64], qv, qv, MINO)
        qg = qa[:].rearrange("p (k cl t) -> p k cl t", k=8, cl=8)
        red = hid_bf[:, 2048:2176].bitcast(F32)
        rv = red[:].rearrange("p (k cl) -> p k cl", k=8)
        vred(c, rv, qg, ADD)
        t_red = ts(c, red[:], red[:], 0.125, MUL)
        t_cs = act(c, red[:], red[:], AF.Sign, w=[t_red])
        tt(c, wscv[:, :, 64:72], rv, rv, MINO, w=[t_cs])
        c.wait("dve", t_sm)
        vcopy(c, knotsB[:, 0:1], smallsB[:, 0:1])
        ts(c, knotsB[:, 3:4], smallsB[:, 8:9], 1e-6, ADD)
        c.emit("dve", lambda v: v.reciprocal(out=knotsB[:, 1:2], in_=knotsB[:, 3:4]))
        vcopy(c, knotsB[:, 2:3], smallsB[:, 9:10])
        tt(c, dB[:], smallsB[:, 1:8], smallsB[:, 0:7], SUB)
        tt(c, calF[:, 0:6], dB[:, 0:6], dB[:, 1:7], SUB)
        vcopy(c, calF[:, 6:7], dB[:, 6:7])
        ts(c, calF[:, 0:7], calF[:, 0:7], 7.0, MUL)
        rowsb = hid_bf[:, 4096:5120].bitcast(F32)[0:1, :]
        c.wait("pe", t_wsc0, t_id, c.tok("dve"), c.tok("act"))
        for k in range(NP):
            t_dm = mm(c, pb[0][0:64, 0:1], wdT[:, k * 64:(k + 1) * 64],
                      ones_bf[:, 0:1], start=(k == 0), stop=(k == NP - 1),
                      inc=(k == NP - 1))
        t_sq2 = c.dma("pool", qa[:, 0:512].rearrange("p (k m) -> p k m", k=8),
                      dirsT[:, :].rearrange("(k p) m -> p k m", p=128),
                      waits=[c.tok("dve"), c.tok("act")])
        t_sqa = act(c, sqbf[:, 0:512], qa[:, 0:512], AF.Square, w=[t_sq2])
        c.wait("pe", t_sqa)
        for k in range(NP):
            t_ds = mm(c, pb[1][0:64, 0:1], sqbf[:, k * 64:(k + 1) * 64],
                      ones_bf[:, 0:1], start=(k == 0), stop=(k == NP - 1),
                      inc=(k == NP - 1))
        ts(c, drows[:, 0:1], pb[0][0:64, 0:1], f32(1.0 / D), MUL, w=[t_dm, t_ds])
        ts(c, drows[:, 1:2], pb[1][0:64, 0:1], f32(1.0 / D), MUL)
        c.wait("pe", c.tok("dve"))
        transp(c, pbb[0:1, 544:608], drows[:, 0:1], ident_bf[0:64, 0:64], inc=True)
        t_tr = c.tok("pe")
        vcopy(c, rowsb[:, 0:64], pbb[0:1, 544:608], w=[("pe", t_tr[1])])
        transp(c, pbb[0:1, 608:672], drows[:, 1:2], ident_bf[0:64, 0:64], inc=True,
               w=[c.tok("dve")])
        t_tr2 = c.tok("pe")
        vcopy(c, rowsb[:, 64:128], pbb[0:1, 608:672], w=[("pe", t_tr2[1])])
        c.wait("pe", c.tok("dve"))
        mm(c, pb[3][:, 0:64], onesf[0:1, :], rowsb[0:1, 0:64], start=True, stop=True)
        t_bb = mm(c, pb[4][:, 0:64], onesf[0:1, :], rowsb[0:1, 64:128], start=True,
                  stop=True, inc=True)
        tb2 = tblp.rearrange("p (c w) -> p c w", c=2)
        vcopy(c, tb2[:, :, 128:192],
              pb[3][:, 0:64].unsqueeze(1).broadcast_to((128, 2, 64)), w=[t_bb])
        vcopy(c, tb2[:, :, 192:256],
              pb[4][:, 0:64].unsqueeze(1).broadcast_to((128, 2, 64)))
        c.wait("dve", t_b2)
        vcopy(c, tb2[:, :, 256:320],
              b2tm[:].unsqueeze(1).broadcast_to((128, 2, 64)))
        c.barrier()

    def sublayer_F(tag, fcols=None):
        gamma = vF[:, 0:8]; beta_ = vF[:, 8:16]; cb1 = vF[:, 16:18]
        cb2c = vF[0:2, 18:19]
        if fcols is None:
            fcols = NCOL
        tbz = tblp.rearrange("p (c w) -> p c w", c=2)
        tbl4 = tblp.rearrange("p (c t w) -> p c t w", c=2, t=5)
        pr4 = prodp.rearrange("p (c t w) -> p c t w", c=2, t=5)
        sels = rowsP[:, 16:26].rearrange("p (c t) -> p c t", c=2)

        def q(j):
            return rowsP[:, 2 * j:2 * j + 2].unsqueeze(2)

        def qb(j, n):
            return q(j).broadcast_to((128, 2, n))
        mskp = mskg.rearrange("p (c w) -> p c w", c=2)
        ohp = oh[:, 0:128]
        ohv3 = ohp.rearrange("p (c w) -> p c w", c=2)
        pr2f = oh[:, 128:256].rearrange("p (c w) -> p c w", c=2)
        identb = ident_bf[:].unsqueeze(1).broadcast_to((128, 2, 128))
        rr3 = rhsd_r.rearrange("p (c w) -> p c w", c=2)
        rm3 = rhsd_m.rearrange("p (c w) -> p c w", c=2)
        ex3 = extrF.rearrange("p (c w) -> p c w", c=2)
        fst = {}

        fstats_bs = (pb[1][:, CW:2 * CW], pb[3][:, CW:2 * CW])

        def fEs(col):
            off = col * CW
            c.wait("pe", state.get("bc_guard"), state.get("f_guard"))
            fst[("S", col)] = col_stats_mm(lambda p: Rcol(p, off), bs=fstats_bs)

        def fEa(col):
            off = col * CW
            col_stats_fin(0, 1, fst.pop(("S", col)), bs=fstats_bs)
            rstd_into(1, 2)
            tt(c, slot(5), slot(0), slot(2), MUL)
            tt(c, slot(7), slot(2), slot(2), MUL)
            tt(c, slot(3), slot(1), slot(7), MUL)
            uv = colview(u32)
            tt(c, uv, Rview(off), bcast(slot(2)), MUL)
            tt(c, uv, uv, bcast(slot(5)), SUB)
            t_z = vcopy(c, colview(xn_bf), uv, w=[state.get("xn_guard")])
            c.wait("pe", t_z, state.get("pb2_guard"))
            for k in range(NP):
                mm(c, pb[2][0:72, 0:CW], wsc[:, k * 72:(k + 1) * 72],
                   xn_bf[:, k * CW:(k + 1) * CW], start=(k == 0), stop=(k == NP - 1))
            for k in range(NP):
                t_zd = mm(c, pb[2][0:64, CW:2 * CW], wdT[:, k * 64:(k + 1) * 64],
                          xn_bf[:, k * CW:(k + 1) * CW], start=(k == 0),
                          stop=(k == NP - 1), inc=(k == NP - 1))
            c.wait("pe", state.get("pb4_guard"))
            for half in range(2):
                for k in range(NP):
                    t_c1 = mm(c, pb[4][:, half * CW:(half + 1) * CW],
                              wc1[:, k * 256 + half * 128:k * 256 + half * 128 + 128],
                              xn_bf[:, k * CW:(k + 1) * CW], start=(k == 0),
                              stop=(k == NP - 1), inc=(k == NP - 1 and half == 1))
            state["xn_guard"] = t_c1
            for half in range(2):
                t_h1 = act(c, h1_bf[:, half * CW:(half + 1) * CW],
                           pb[4][:, half * CW:(half + 1) * CW],
                           AF.Gelu, bias=cb1[:, half:half + 1],
                           w=[t_c1, state.get("c2_guard")] if half == 0 else [t_c1])
            state["pb4_guard"] = t_h1
            t_zde = act(c, zd_bf[:], pb[2][0:64, CW:2 * CW], AF.Identity,
                        w=[t_zd, state.get("tp_guard")])
            t_sg = act(c, uu[:], pb[2][0:72, 0:CW], AF.Sigmoid,
                       scale=knotsB[0:72, 1:2], w=[t_zd, state.get("uu_guard")])
            state["pb2_guard"] = t_sg
            fst[("Ea", col)] = (t_sg, t_h1, t_zde)

        def fEb(col):
            par = col % 2
            gtm = gtmP[par]
            rowsE = rowsEP[par]
            t_sg, t_h1, t_zde = fst.pop(("Ea", col))
            for j, sl in enumerate((2, 5, 3)):
                tt(c, ex3, identb, slot(sl).rearrange("p (c w) -> p c w", c=2), MUL)
                vred(c, rowsE[:, 2 * j:2 * j + 2].unsqueeze(2), ex3, ADD)
            t_cal = ts(c, gts[:], uu[:], calF[0:72, 6:7], MUL, knotsB[0:72, 0:1],
                       ADD, w=[t_sg])
            for m in range(1, 7):
                ts(c, slot(7)[0:72, :], uu[:], float(m / 7.0), MINO)
                t_cal = stt(c, gts[:], slot(7)[0:72, :], calF[0:72, m - 1:m], gts[:],
                            MUL, ADD)
            state["uu_guard"] = t_cal
            t_gts = vcopy(c, gts_bf[:], gts[:], w=[state.get("tp_guard")])
            c.wait("pe", t_h1)
            for k2 in range(2):
                t_c2 = mm(c, pb[5][0:2, 0:CW], wc2[:, k2 * 2:(k2 + 1) * 2],
                          h1_bf[:, k2 * CW:(k2 + 1) * CW], start=(k2 == 0),
                          stop=(k2 == 1), inc=(k2 == 1))
            t_cp2 = act(c, comp_bf[:], pb[5][0:2, 0:CW], AF.Tanh,
                        bias=cb2c, w=[t_c2, state.get("hh_guard")])
            state["c2_guard"] = t_c2
            c.wait("pe", t_cp2, t_gts, state.get("bc_guard"))
            t_hh = None
            t_hh_hist = []
            for gch in range(8):
                if gch >= 2:
                    c.wait("pe", t_hh_hist[gch - 2])
                t_hmm = mm(c, pb[5 + gch % 2][:, 0:CW], ww1f[:, gch * 128:(gch + 1) * 128],
                           comp_bf[:], start=True, stop=True, inc=True)
                t_hh = act(c, hid_bf[:, gch * CW:(gch + 1) * CW], pb[5 + gch % 2][:, 0:CW],
                           AF.Relu, bias=b1f_sb[:, gch:gch + 1],
                           w=[t_hmm, state.get("magmm_guard")] if gch == 0
                             else [t_hmm])
                t_hh_hist.append(t_hh)
                if gch == 7:
                    state["hh_guard"] = t_hmm
            c.wait("pe", t_gts, t_zde, state.get("pbb_guard2"))
            for tch in range(2):
                transp(c, pbb[:, tch * 80:tch * 80 + 72],
                       gts_bf[:, tch * 128:(tch + 1) * 128], ident_bf[0:72, 0:72])
                transp(c, pbb[:, 160 + tch * 64:160 + (tch + 1) * 64],
                       zd_bf[:, tch * 128:(tch + 1) * 128], ident_bf[0:64, 0:64],
                       inc=(tch == 1))
            t_tp = c.tok("pe")
            state["tp_guard"] = t_tp
            vcopy(c, gtm[:, 0:72], pbb[:, 0:72], w=[t_tp])
            vcopy(c, gtm[:, 80:152], pbb[:, 80:152])
            t_ztm = vcopy(c, tbz[:, :, par * 64:par * 64 + 64],
                          pbb[:, 160:288].rearrange("p (c w) -> p c w", c=2))
            c.wait("pe", t_hh, t_ztm, state.get("mag_evac"))
            for tch in range(2):
                for gch in range(8):
                    t_mg = mm(c, pb[0][:, tch * 64:(tch + 1) * 64],
                              hid_bf[:, gch * CW + tch * 128:gch * CW + (tch + 1) * 128],
                              ww2blk[:, gch * 64:(gch + 1) * 64], start=(gch == 0),
                              stop=(gch == 7), inc=(gch == 7 and tch == 1))
            t_mge = act(c, magsbP[par][:], pb[0][:, 0:128], AF.Identity,
                        w=[t_mg, fst.get(("mag_read", par))])
            state["mag_evac"] = t_mge
            state["magmm_guard"] = t_mg
            fst[("E", col)] = t_mge

        def fD(col):
            off = col * CW
            par = col % 2
            gtm = gtmP[par]
            rowsE = rowsEP[par]
            t_mge = fst.pop(("E", col))
            for tch in range(2):
                vmax8(c, mxcB[:, tch * 8:tch * 8 + 8],
                      gtm[:, tch * 80 + 64:tch * 80 + 72])
                vmaxidx(c, idxu[:, tch * 8:tch * 8 + 8],
                        mxcB[:, tch * 8:tch * 8 + 8],
                        gtm[:, tch * 80 + 64:tch * 80 + 72])
            vcopy(c, idxfB[:, 0:16], idxu[:, 0:16])
            cidxb = idxfB[:, 0:16].rearrange("p (c w) -> p c w", c=2)[:, :, 0:1]
            tt(c, mskp, clus64[:].unsqueeze(1).broadcast_to((128, 2, 64)),
               cidxb.broadcast_to((128, 2, 64)), AluOpType.is_equal)
            g64p = gtm[:, 0:160].rearrange("p (c w) -> p c w", c=2)[:, :, 0:64]
            stt(c, mskp, mskp, 1000.0, g64p, MUL, ADD)
            for tch in range(2):
                vmax8(c, mxtF[:, tch * 8:tch * 8 + 8],
                      mskg[:, tch * 64:(tch + 1) * 64])
                vmaxidx(c, idxu[:, 16 + tch * 8:24 + tch * 8],
                        mxtF[:, tch * 8:tch * 8 + 8],
                        mskg[:, tch * 64:(tch + 1) * 64])
            vcopy(c, idxfB[:, 16:32], idxu[:, 16:32])
            tidxb = idxfB[:, 16:32].rearrange("p (c w) -> p c w", c=2)[:, :, 0:1]
            tt(c, ohv3, iota64[:].unsqueeze(1).broadcast_to((128, 2, 64)),
               tidxb.broadcast_to((128, 2, 64)), AluOpType.is_equal)
            tt(c, pr4, tbl4, ohv3.unsqueeze(2).broadcast_to((128, 2, 5, 64)), MUL)
            vred(c, sels.unsqueeze(3), pr4, ADD)
            zselv = sels[:, :, par:par + 1]
            tt(c, pr2f, ohv3,
               magsbP[par][:].rearrange("p (c w) -> p c w", c=2), MUL, w=[t_mge])
            fst[("mag_read", par)] = vred(c, q(6), pr2f, ADD)
            tt(c, q(1), q(6), sels[:, :, 4:5], ADD)
            ts(c, rowsP[:, 2:4], rowsP[:, 2:4], knotsB[:, 2:3], MUL)
            vcopy(c, rowsP[:, 0:2], rowsE[:, 4:6], w=[state.get("rp_guard")])
            stt(c, q(3), q(1), float(2.0 / D), zselv, MUL, MUL)
            tt(c, q(0), q(0), q(3), ADD)
            tt(c, q(3), q(1), q(1), MUL)
            tt(c, q(3), q(3), sels[:, :, 3:4], MUL)
            tt(c, q(0), q(0), q(3), ADD)
            tt(c, q(2), q(1), sels[:, :, 2:3], MUL)
            tt(c, q(3), q(2), q(2), MUL)
            t_varp = tt(c, q(0), q(0), q(3), SUB)
            act(c, rowsP[:, 8:10], rowsP[:, 0:2], AF.Ln, bias=EPS, w=[t_varp])
            t_rp = act(c, rowsP[:, 8:10], rowsP[:, 8:10], AF.Exp, scale=f32(-0.5))
            state["rp_guard"] = t_rp
            tt(c, rowsP[:, 12:14], rowsE[:, 0:2], rowsP[:, 8:10], MUL, w=[t_rp])
            tt(c, rowsP[:, 14:16], rowsE[:, 2:4], rowsP[:, 4:6], ADD)
            tt(c, rowsP[:, 14:16], rowsP[:, 14:16], rowsP[:, 8:10], MUL)
            t_ptm = tt(c, ptm[:].rearrange("p (c w) -> p c w", c=2), ohv3,
                       qb(4, 64), MUL, w=[state.get("ptm_guard")])
            t_dr = tt(c, rr3, identb, qb(6, 128), MUL)
            t_dm2 = tt(c, rm3, identb, qb(7, 128), MUL)
            c.wait("pe", t_dr, t_dm2, state.get("bc_guard"), state["stats_dve"])
            mm(c, pb[5][:, CW:2 * CW], onesf[:], rhsd_r[:, :], start=True, stop=True)
            t_bc2 = mm(c, pb[6][:, CW:2 * CW], onesf[:], rhsd_m[:, :], start=True,
                       stop=True, inc=True)
            c.wait("pe", t_ptm)
            for tch in range(2):
                transp(c, pbb[0:64, 288 + tch * 128:288 + (tch + 1) * 128],
                       ptm[:, tch * 64:(tch + 1) * 64], ident_bf[:], inc=(tch == 1))
            t_pt = c.tok("pe")
            state["ptm_guard"] = t_pt
            vcopy(c, pT_bf[:, 0:128], pbb[0:64, 288:288 + 128],
                  w=[t_pt, state.get("pT_guard")])
            t_pte = vcopy(c, pT_bf[:, 128:256], pbb[0:64, 416:416 + 128])
            state["pbb_guard2"] = t_pte
            vcopy(c, slot(4), pb[5][:, CW:2 * CW], w=[t_bc2])
            t_bce = vcopy(c, slot(6), pb[6][:, CW:2 * CW])
            state["bc_guard"] = t_bce
            c.wait("pe", t_pte)
            t3_hist = []
            for do in range(NP):
                tmp = rhsd_r if do % 2 == 0 else rhsd_m
                fb = pb[3] if do % 2 == 0 else pb[1]
                if do >= 2:
                    c.wait("pe", t3_hist[do - 2])
                elif state.get("f_guard") is not None:
                    c.wait("pe", state["f_guard"])
                t_f = mm(c, fb[:, 0:CW], wdirs[:, do * 128:(do + 1) * 128],
                         pT_bf[:], start=True, stop=True, inc=True)
                tt(c, tmp[:, :], Rcol(do, off), slot(4), MUL,
                   w=[t_bce] if do == 0 else ())
                tt(c, tmp[:, :], tmp[:, :], slot(6), SUB)
                t3 = tt(c, tmp[:, :], tmp[:, :], fb[:, 0:CW], ADD, w=[t_f])
                t3_hist.append(t3)
                ts(c, tmp[:, :], tmp[:, :], gamma[:, do:do + 1], MUL,
                   beta_[:, do:do + 1], ADD)
                tt(c, Rcol(do, off), Rcol(do, off), tmp[:, :], ADD)
            state["f_guard"] = t3_hist[-1]
            state["pT_guard"] = t_f

        fEs(0)
        fEa(0)
        for i in range(fcols):
            if i + 1 < fcols:
                fEs(i + 1)
            fEb(i)
            if i + 1 < fcols:
                fEa(i + 1)
            if i >= 1:
                fD(i - 1)
        fD(fcols - 1)
        c.barrier()

    def sublayer_B():
        t_wq = c.dma("pool", qbs[:, :], qTd[:, :])
        t_wcb = c.dma("pool", cbs[:, :], combd[:, :])
        t_st = None
        for col in range(NCOL):
            off = col * CW
            col_stats(lambda p: Rcol(p, off), 0, 1, extra=[t_st])
            rstd_into(1, 2)
            tt(c, slot(5), slot(0), slot(2), MUL)
            xv = colview(xn_bf)
            tt(c, xv, Rview(off), bcast(slot(2)), MUL)
            t_z = tt(c, xv, xv, bcast(slot(5)), SUB)
            t_st = c.dma("sp", zscr[:, off:off + CW].rearrange("(k p) t -> p k t", p=128),
                         xn_bf[:].rearrange("p (k t) -> p k t", k=NP), waits=[t_z])
            c.wait("dve", t_st)
        c.barrier()
        NTCH = T // 128
        gmil = []
        t_zl = None
        for fh in range(2):
            for tch in range(NTCH):
                gi = fh * NTCH + tch
                buf = gi % 2
                zt = ztT[:, buf * 1024:(buf + 1) * 1024]
                tw = [t_zl] if t_zl is not None else []
                if gi >= 2:
                    tw.append(gmil[gi - 2])
                t_zl = c.dma("sp", zt, zscr[:, tch * 128:(tch + 1) * 128],
                             transpose=True, waits=tw)
                c.wait("pe", t_zl)
                for q in range(4):
                    t_gq = mm(c, pb[q][:, fh * 256:fh * 256 + 256],
                              zt[:, q * 256 + fh * 128:q * 256 + fh * 128 + 128],
                              zt[:, q * 256:(q + 1) * 256],
                              start=(tch == 0), stop=(tch == NTCH - 1),
                              inc=(q == 3))
                gmil.append(t_gq)
            t_gr = c.tok("pe")
            for q in range(4):
                vcopy(c, gram[:, (q * 2 + fh) * 256:(q * 2 + fh) * 256 + 256],
                      pb[q][:, fh * 256:fh * 256 + 256], w=[t_gr])
        t_gf = vcopy(c, gram_bf, gram)
        c.barrier()
        for a in range(4):
            t_wqa = c.dma("sp", wqa, qbs[:, a * 2048:(a + 1) * 2048],
                          waits=[("pe", c.cnt["pe"]), t_wq])
            t_wca = c.dma("sp", wcba, cbs[:, a * 2048:(a + 1) * 2048],
                          waits=[t_wcb, t_gf, ("pe", c.cnt["pe"])])
            c.wait("pe", t_wqa, t_wca)
            t1hist = []
            for gi, (q, gch) in enumerate([(q, g) for q in range(4) for g in range(2)]):
                if gi >= 2:
                    c.wait("pe", t1hist[gi - 2])
                for fh in range(2):
                    t_t1 = mm(c, pb[4 + gi % 2][:, 0:256],
                              gram_bf[:, (q * 2 + fh) * 256 + gch * 128:
                                      (q * 2 + fh) * 256 + gch * 128 + 128],
                              wqa[:, (q * 2 + fh) * 256:
                                  (q * 2 + fh) * 256 + 256],
                              start=(fh == 0), stop=(fh == 1),
                              inc=(fh == 1))
                t1hist.append(act(c, t1T[:, (q * 2 + gch) * 256:(q * 2 + gch) * 256 + 256],
                                  pb[4 + gi % 2][:, 0:256], AF.Identity, w=[t_t1]))
            t_t1e = c.tok("act")
            c.wait("pe", t_t1e)
            for ech in range(2):
                if ech == 1:
                    c.wait("pe", c.tok("act"))
                for k in range(8):
                    t_M = mm(c, pb[6][:, ech * 256:ech * 256 + 256],
                             t1T[:, k * 256 + ech * 128:k * 256 + ech * 128 + 128],
                             wcba[:, k * 256:(k + 1) * 256],
                             start=(k == 0), stop=(k == 7), inc=(k == 7))
                act(c, Msb[:, (a * 2 + ech) * 256:(a * 2 + ech) * 256 + 256],
                    pb[6][:, ech * 256:ech * 256 + 256], AF.Identity, w=[t_M])
            c.wait("pe", c.tok("act"))
        c.barrier()
        for col in range(NCOL // 2):
            off = col * CW
            t_zl2 = c.dma("sp", xn_bf[:].rearrange("p (k t) -> p k t", k=NP),
                          zscr[:, off:off + CW].rearrange("(k p) t -> p k t", p=128))
            c.wait("pe", t_zl2)
            t_ap_hist = []
            for do in range(NP):
                a, gh = do // 2, do % 2
                if do >= 2:
                    c.wait("pe", t_ap_hist[do - 2])
                for ech in range(2):
                    t_ap = mm(c, pb[do % 2][:, 0:CW],
                              Msb[:, (a * 2 + ech) * 256 + gh * 128:
                                  (a * 2 + ech) * 256 + gh * 128 + 128],
                              xn_bf[:, (a * 2 + ech) * CW:(a * 2 + ech + 1) * CW],
                              start=(ech == 0), stop=(ech == 1), inc=(ech == 1))
                t_r = tt(c, Rcol(do, off), Rcol(do, off), pb[do % 2][:, 0:CW], ADD,
                         w=[t_ap])
                t_ap_hist.append(t_r)
            c.barrier()

    order = [p for p in ("A", "F1", "B", "F2") if p in parts]
    did_A = False
    did_Fpro = False
    for part in order:
        if part == "A":
            sublayer_A()
        elif part == "B":
            sublayer_B()
        elif part in ("F1", "F2"):
            if not did_Fpro:
                prologue_F()
                did_Fpro = True
            sublayer_F(part, fcols=NCOL // 2 if part == "F2" else None)

    c.barrier()
    outs = []
    for p in range(NP):
        outs.append(c.dma("sp", Rout[p * 128:(p + 1) * 128, :],
                          R[:, p * T:p * T + T // 2]))
    c.wait("sp", *outs)
    c.play()
    es.close()
    return nc



def _host_prep(inputs):
    ap = {}
    g = lambda k: np.ascontiguousarray(np.asarray(inputs[k], dtype=np.float32))
    ones = np.ones((128, 128), np.float32)
    ident = np.eye(128, dtype=np.float32)
    smalls = np.zeros((16,), np.float32)
    smalls[0:8] = g("L0_ffn_knot_values")
    smalls[8] = g("L0_ffn_temperature")[0]
    smalls[9] = g("L0_ffn_output_scale")[0]
    smalls[10:12] = g("L0_ffn_compress_b2")
    iota64 = np.broadcast_to(np.arange(64, dtype=np.float32), (128, 64))
    clus64 = np.broadcast_to((np.arange(64) // 8).astype(np.float32), (128, 64))
    consts = np.concatenate([ones, ident, np.broadcast_to(smalls, (128, 16)),
                             iota64, clus64], axis=1)
    ap["consts"] = np.ascontiguousarray(consts.astype(np.float32))
    ap["w_inT"] = np.ascontiguousarray(g("L0_attn_a_w_in").T)
    ap["w_outT"] = np.ascontiguousarray(g("L0_attn_a_w_out").T)

    def group_w(wT, grpcols):
        K8, M = wT.shape[0] // 128, wT.shape[1]
        ngrp = M // grpcols
        out = wT.reshape(K8, 128, ngrp, grpcols).transpose(1, 2, 0, 3)
        return np.ascontiguousarray(out.reshape(128, ngrp * K8 * grpcols))

    ap["w_gateT"] = group_w(g("L0_attn_a_w_gate").T, 256)
    ap["w1T"] = group_w(g("L0_attn_a_w_mlp1").T, 128)
    ap["w2T"] = group_w(g("L0_attn_a_w_mlp2").T, 128)
    vecsA = np.zeros((128, 48), np.float32)
    for i, k in enumerate(["ln1_g", "ln1_b", "ln2_g", "ln2_b", "b_gate", "b_mlp2"]):
        vecsA[:, 8 * i:8 * (i + 1)] = g(f"L0_attn_a_{k}").reshape(8, 128).T
    ap["vecsA"] = vecsA
    ap["bmlp1"] = np.ascontiguousarray(g("L0_attn_a_b_mlp1").reshape(32, 128).T)
    ap["eig"] = np.ascontiguousarray(
        np.stack([g("L0_attn_a_log_decay"), g("L0_attn_a_frequency")], axis=1))
    cpl = np.zeros((64, 64), np.float32)
    cp = g("L0_attn_a_coupling")
    for h in range(4):
        cpl[h * 16:(h + 1) * 16, h * 16:(h + 1) * 16] = cp[h].T
    ap["couple"] = cpl
    ap["sigsT_raw"] = np.ascontiguousarray(g("L0_ffn_signatures_raw").T)
    ap["dirsT"] = np.ascontiguousarray(g("L0_ffn_directions").T)
    ap["dirsL"] = g("L0_ffn_directions")
    ap["cw1"] = g("L0_ffn_compress_w1")
    cw2 = g("L0_ffn_compress_w2")
    ap["cw2"] = np.ascontiguousarray(cw2.reshape(2, 128, 2).transpose(1, 0, 2).reshape(128, 4))
    ap["w1f"] = np.ascontiguousarray(g("L0_ffn_spline_w1").transpose(1, 0, 2).reshape(2, 1024))
    w2b = np.zeros((1024, 64), np.float32)
    sw2 = g("L0_ffn_spline_w2")
    for t in range(64):
        w2b[t * 16:(t + 1) * 16, t] = sw2[t, :, 0]
    ap["w2blk"] = w2b
    vecsF = np.zeros((128, 20), np.float32)
    vecsF[:, 0:8] = g("L0_ffn_norm_gamma").reshape(8, 128).T
    vecsF[:, 8:16] = g("L0_ffn_norm_beta").reshape(8, 128).T
    vecsF[:, 16:18] = g("L0_ffn_compress_b1").reshape(2, 128).T
    vecsF[0:2, 18] = g("L0_ffn_compress_b2")
    ap["vecsF"] = vecsF
    ap["b2tmd"] = np.ascontiguousarray(
        np.broadcast_to(g("L0_ffn_spline_b2")[:, 0], (128, 64)).astype(np.float32))
    qs = g("L0_attn_b_queries")
    qT = np.zeros((128, 32 * 256), np.float32)
    for a in range(4):
        for q in range(4):
            blk = qs[a][:, q * 256:(q + 1) * 256].T
            for fch in range(2):
                idx = a * 8 + q * 2 + fch
                qT[:, idx * 256:(idx + 1) * 256] = blk[fch * 128:(fch + 1) * 128, :]
    ap["qTd"] = qT
    cb = g("L0_attn_b_combiners")
    cbd = np.zeros((128, 32 * 256), np.float32)
    for a in range(4):
        for k in range(8):
            idx = a * 8 + k
            cbd[:, idx * 256:(idx + 1) * 256] = cb[a][k * 128:(k + 1) * 128, :]
    ap["combd"] = cbd
    ap["b1fd"] = np.ascontiguousarray(g("L0_ffn_spline_b1").reshape(64, 16).reshape(8, 8, 16).transpose(0, 2, 1).reshape(128, 8) if False else g("L0_ffn_spline_b1").reshape(1024).reshape(8, 128).T)
    return ap


def _mk_core_inputs(ap, Xb, half):
    m = dict(ap)
    if half == 0:
        m["Rin"] = np.ascontiguousarray(Xb.T)
        xprev = Xb[T - 1]
        prevmask, midmask = 0.0, 1.0
    else:
        m["Rin"] = np.ascontiguousarray(
            np.concatenate([Xb[T // 2:], Xb[:T // 2]], axis=0).T)
        xprev = Xb[T // 2 - 1]
        prevmask, midmask = 1.0, 0.0
    halo = np.zeros((128, 10), np.float32)
    halo[:, 0:8] = xprev.reshape(8, 128).T
    halo[:, 8] = prevmask
    halo[:, 9] = midmask
    m["halo"] = halo
    return m


def kernel(**inputs):
    ap = _host_prep(inputs)
    X = np.asarray(inputs["L0_attn_a_x"], dtype=np.float32)
    nc = build_program(("A", "F1", "B", "F2"))
    in_maps = []
    for core in range(8):
        in_maps.append(_mk_core_inputs(ap, X[core // 2], core % 2))
    res = run_bass_kernel_spmd(nc, in_maps, core_ids=list(range(8)))
    out = np.zeros((4, T, D), np.float32)
    for b in range(4):
        out[b, 0:T // 2] = res.results[2 * b]["Rout"].T
        out[b, T // 2:] = res.results[2 * b + 1]["Rout"].T
    return out


# version 21
# speedup vs baseline: 1.1723x; 1.0058x over previous
import numpy as np
from contextlib import ExitStack
import concourse.bass as bass
import concourse.mybir as mybir
from concourse.alu_op_type import AluOpType
from concourse.bass_utils import run_bass_kernel_spmd

F32 = mybir.dt.float32
BF16 = mybir.dt.bfloat16
AF = mybir.ActivationFunctionType
AX = mybir.AxisListType
f32 = np.float32
ADD, MUL, SUB = AluOpType.add, AluOpType.mult, AluOpType.subtract
MAXO, MINO, BYP = AluOpType.max, AluOpType.min, AluOpType.bypass

D = 1024
T = 4096
NP = 8
CW = 256
NCOL = T // CW
CHUNK = 64
EPS = f32(1e-5)

ENGINES = ("pe", "dve", "act", "sp", "pool")


NLANE = 4


class Ctx:
    def __init__(self, nc):
        self.nc = nc
        self.es = ExitStack()
        self.q = {e: [] for e in ENGINES}
        self.cnt = {}
        self.sem = {}
        names = list(ENGINES)
        for qn in ("spq", "poolq", "actq"):
            for l in range(NLANE):
                names.append(f"{qn}{l}")
        for e in names:
            self.sem[e] = self.es.enter_context(nc.semaphore(f"s_{e}"))
            self.cnt[e] = 0
        self.lane_rr = {"spq": 0, "poolq": 0, "actq": 0}
        self.last_wait = {e: {} for e in ENGINES}

    def tok(self, e):
        return (e, self.cnt[e])

    def _waits(self, e, waits):
        out = []
        lw = self.last_wait[e]
        for w in waits:
            if w is None:
                continue
            s, v = w
            if s == e or v <= 0:
                continue
            if lw.get(s, 0) >= v:
                continue
            lw[s] = v
            out.append((s, v))
        return out

    def emit(self, e, fn, waits=(), inc=None):
        import traceback as _tb
        fn._src = "".join(_tb.format_stack(limit=6)[:-2])
        wl = self._waits(e, waits)
        serial = e in ("dve", "act")
        do_inc = serial if inc is None else inc
        selfwait = self.cnt[e] if (serial and self.cnt[e] > 0) else None
        if do_inc:
            self.cnt[e] += 1
        n = self.cnt[e]
        self.q[e].append((wl, fn, n if do_inc else None, selfwait))
        return (e, n)

    def dma(self, e, out, in_, waits=(), **kw):
        q = {"sp": "spq", "pool": "poolq", "act": "actq"}[e]
        lane = self.lane_rr[q]
        self.lane_rr[q] = (lane + 1) % NLANE
        ln = f"{q}{lane}"
        wl = self._waits(e, list(waits) + [(ln, self.cnt[ln])])
        self.cnt[ln] += 16
        n = self.cnt[ln]
        sem = self.sem[ln]

        def fn(eng, out=out, in_=in_, kw=kw):
            return eng.dma_start(out=out, in_=in_, **kw)

        self.q[e].append((wl, fn, ("dma", sem), None))
        return (ln, n)

    def wait(self, e, *toks):
        wl = self._waits(e, [t for t in toks if t is not None])
        if wl:
            self.q[e].append((wl, None, None, None))

    def barrier(self):
        toks = [(s_, self.cnt[s_]) for s_ in self.sem if s_ not in ("sp", "pool")]
        for e in ENGINES:
            self.wait(e, *toks)

    def play(self):
        nc = self.nc
        sems = self.sem

        def mk(e):
            queue = self.q[e]

            def body(eng):
                for wl, fn, inc, selfwait in queue:
                    if selfwait:
                        eng.wait_ge(sems[e], selfwait)
                    for (s, v) in wl:
                        eng.wait_ge(sems[s], v)
                    if fn is None:
                        continue
                    try:
                        inst = fn(eng)
                    except Exception:
                        import traceback
                        print("FAILING OP on", e, "emitted at:")
                        print(getattr(fn, "_src", "?"))
                        raise
                    if inst is None:
                        continue
                    if isinstance(inc, tuple):
                        inst.then_inc(inc[1], 16)
                    elif inc is not None:
                        inst.then_inc(sems[e], 1)
            return body

        with nc.Block() as block:
            block.tensor(mk("pe"))
            block.vector(mk("dve"))
            block.scalar(mk("act"))
            block.sync(mk("sp"))
            block.gpsimd(mk("pool"))
        self.es.close()


def tt(c, out, a, b, op, w=()):
    return c.emit("dve", lambda v: v.tensor_tensor(out=out, in0=a, in1=b, op=op), w)


def _sc(x):
    if isinstance(x, (np.floating, np.integer)):
        return float(x)
    return x


def ts(c, out, a, s1, op0, s2=None, op1=None, w=()):
    kw = {}
    if op1 is not None:
        kw["op1"] = op1
    s1 = _sc(s1); s2 = _sc(s2)
    return c.emit("dve", lambda v: v.tensor_scalar(out=out, in0=a, scalar1=s1,
                                                   scalar2=s2, op0=op0, **kw), w)


def stt(c, out, in0, sc, in1, op0, op1, w=()):
    sc = _sc(sc)
    return c.emit("dve", lambda v: v.scalar_tensor_tensor(
        out=out, in0=in0, scalar=sc, in1=in1, op0=op0, op1=op1), w)


def vcopy(c, out, a, w=()):
    return c.emit("dve", lambda v: v.tensor_copy(out=out, in_=a), w)


def vmemset(c, ap, val, w=()):
    return c.emit("dve", lambda v: v.memset(ap, val), w)


def vred(c, out, a, op, w=()):
    return c.emit("dve", lambda v: v.tensor_reduce(out=out, in_=a, axis=AX.X, op=op), w)


def vmax8(c, out, a, w=()):
    return c.emit("dve", lambda v: v.max(out=out, in_=a), w)


def vmaxidx(c, out, mx, vals, w=()):
    return c.emit("dve", lambda v: v.max_index(out=out, in_max=mx, in_values=vals), w)


def act(c, out, a, func, bias=None, scale=None, w=()):
    kw = {}
    if bias is not None:
        kw["bias"] = np.float32(bias) if isinstance(bias, (float, np.floating)) else bias
    if scale is not None:
        kw["scale"] = np.float32(scale) if isinstance(scale, (float, np.floating)) else scale
    return c.emit("act", lambda s: s.activation(out=out, in_=a, func=func, **kw), w)


def mm(c, out, lhsT, rhs, start, stop, w=(), inc=False):
    return c.emit("pe", lambda t: t.matmul(out, lhsT, rhs, start=start, stop=stop),
                  w, inc=inc)


def transp(c, out, in_, ident, w=(), inc=False):
    return c.emit("pe", lambda t: t.transpose(out, in_, ident), w, inc=inc)


def bcast(ap_, np_=NP):
    return ap_.unsqueeze(1).broadcast_to((ap_.shape[0], np_, ap_.shape[1]))


def bcast_t(ap_, n):
    return ap_.unsqueeze(2).broadcast_to((ap_.shape[0], ap_.shape[1], n))



def build_program(parts):
    nc = bass.Bass("TRN2", target_bir_lowering=False)
    es = ExitStack()
    c = Ctx(nc)

    def dram(name, shape, dt=F32, kind="ExternalInput"):
        return nc.dram_tensor(name, shape, dt, kind=kind).ap()

    def sb(name, shape, dt=F32):
        return es.enter_context(nc.sbuf_tensor(name, shape, dt))

    def psum(name, shape, dt=F32):
        return es.enter_context(nc.psum_tensor(name, shape, dt))

    need_A = "A" in parts
    need_F = ("F1" in parts) or ("F2" in parts)
    need_B = "B" in parts

    Rin = dram("Rin", [NP * 128, T])
    Rout = dram("Rout", [NP * 128, T // 2], kind="ExternalOutput")
    consts = dram("consts", [128, 400])

    if need_B:
        qTd = dram("qTd", [128, 32 * 256])
        combd = dram("combd", [128, 32 * 256])
        zscr = dram("zscr", [D, T], BF16, kind="Internal")
        qbs = dram("qbs", [128, 32 * 256], BF16, kind="Internal")
        cbs = dram("cbs", [128, 32 * 256], BF16, kind="Internal")
    if need_F:
        sigsT_raw = dram("sigsT_raw", [D, 64])
        dirsT = dram("dirsT", [D, 64])
        dirsL = dram("dirsL", [64, D])
        cw1 = dram("cw1", [D, 256])
        cw2 = dram("cw2", [128, 4])
        w1f = dram("w1f", [2, 1024])
        w2blk = dram("w2blk", [1024, 64])
        vecsF = dram("vecsF", [128, 20])
        b2tmd = dram("b2tmd", [128, 64])
        b1fd = dram("b1fd", [128, 8])
    if need_A:
        w_inT = dram("w_inT", [D, 128])
        w_gateT = dram("w_gateT", [128, 8192])
        w_outT = dram("w_outT", [128, D])
        w1T = dram("w1T", [128, 32768])
        w2T = dram("w2T", [128, 32768])
        vecsA = dram("vecsA", [128, 48])
        bmlp1 = dram("bmlp1", [128, 32])
        eig = dram("eig", [64, 2])
        couple = dram("couple", [64, 64])
        halo = dram("halo", [128, 10])
        wgs = dram("wgs", [128, 8192], BF16, kind="Internal")
        w1s = dram("w1s", [128, 32768], BF16, kind="Internal")
        w2s = dram("w2s", [128, 32768], BF16, kind="Internal")

    R = sb("R", [128, NP * T])
    ones_bf = sb("ones_bf", [128, 128], BF16)
    onesf = sb("onesf", [128, 128])
    ident_bf = sb("ident_bf", [128, 128], BF16)
    smallsB = sb("smallsB", [128, 12])
    xn_bf = sb("xn_bf", [128, NP * CW], BF16)
    if need_A or need_F:
        u32 = sb("u32", [128, NP * CW])
    rbf = xn_bf[:, 0:NP * CW]
    sqbf = sb("sqbf", [128, NP * CW], BF16)
    bc = sb("bc", [128, 8 * CW])
    hid_bf = sb("hid_bf", [128, 32 * CW], BF16)
    wbig = sb("wbig", [128, 2 * 2048], BF16)
    wbig2 = sb("wbig2", [128, 3 * 1024], BF16)
    if need_B:
        gram = hid_bf[:, 0:4096].bitcast(F32)
        gram_bf = hid_bf[:, 4096:6144]
        t1T = hid_bf[:, 6144:8192]
        Msb = wbig[:, 0:2048]
        ztT = wbig[:, 2048:4096]
        wqa = wbig2[:, 0:2048]
        wcba = hid_bf[:, 0:2048]
    if need_A or need_F:
        x2n_bf = sb("x2n_bf", [128, NP * CW], BF16)
        gate_bf = sb("gate_bf", [128, NP * CW], BF16)
    if need_A:
        ccb = sb("ccb", [128, CW], BF16)
        carry = sb("carry", [64, 2])
        lam = sb("lam", [64, 20])
        lamt = sb("lamt", [64, 4])
        vA = sb("vA", [128, 48])
        bm1 = sb("bm1", [128, 32])
        eig_sb = sb("eig_sb", [64, 2])
        couple_sb = sb("couple_sb", [64, 64])
        winsb = sb("winsb", [128, 8 * 128], BF16)
        halo_sb = sb("halo_sb", [128, 10])
        woutsb = sb("woutsb", [128, 8 * 128], BF16)

    if need_F:
        wsc = sb("wsc", [128, 8 * 72], BF16)
        wdT = sb("wdT", [128, 8 * 64], BF16)
        wdirs = sb("wdirs", [64, D], BF16)
        wc1 = sb("wc1", [128, 8 * 256], BF16)
        wc2 = sb("wc2", [128, 4], BF16)
        ww1f = sb("ww1f", [2, 1024], BF16)
        ww2blk = sb("ww2blk", [128, 8 * 64], BF16)
        vF = sb("vF", [128, 20])
        b1f_sb = sb("b1f_sb", [128, 8])
        knotsB = sb("knotsB", [128, 4])
        dB = sb("dB", [128, 7])
        calF = sb("calF", [128, 8])
        tblp = x2n_bf[:, 0:640]
        prodp = x2n_bf[:, 640:1280]
        extrF = x2n_bf[:, 640:1152].bitcast(F32)
        rhsd_r = x2n_bf[:, 1280:1792].bitcast(F32)
        rowsP = gate_bf[:, 0:64].bitcast(F32)
        mxcB = gate_bf[:, 64:96]
        idxfB = gate_bf[:, 96:160].bitcast(F32)
        mxtF = gate_bf[:, 160:224].bitcast(F32)
        mskg = gate_bf[:, 224:480].bitcast(F32)
        magsbP = [gate_bf[:, 480:736].bitcast(F32),
                  gate_bf[:, 736:992].bitcast(F32)]
        rowsEP = [gate_bf[:, 992:1008].bitcast(F32),
                  gate_bf[:, 1008:1024].bitcast(F32)]
        egl = hid_bf[:, 4096:4608]
        xgl = hid_bf[:, 4608:5120]
        rhsd_m = gate_bf[:, 1536:2048].bitcast(F32)
        b2tm = gate_bf[:, 1184:1248]
        drows = gate_bf[0:64, 1248:1250]
        iota64 = gate_bf[:, 1252:1316]
        clus64 = gate_bf[:, 1316:1380]
        idxu = gate_bf[:, 1380:1444].bitcast(mybir.dt.uint32)
        uu = hid_bf[:, 2048:2560].bitcast(F32)[0:72, :]
        gts = hid_bf[:, 2560:3072].bitcast(F32)[0:72, :]
        gts_bf = hid_bf[0:72, 3072:3328]
        zd_bf = hid_bf[0:64, 3328:3584]
        h1_bf = hid_bf[:, 3584:4096]
        gtmP = [hid_bf[:, 5120:5280], gate_bf[:, 1024:1184]]
        ptm = hid_bf[:, 5408:5536]
        pT_bf = hid_bf[0:64, 5536:5792]
        oh = hid_bf[:, 5792:6304].bitcast(F32)
        comp_bf = hid_bf[0:2, 6304:6560]

    pb = [psum(f"pb{i}", [128, 512]) for i in range(7)]
    pbb = psum("pbb", [128, 1024], BF16)

    t_ones = c.dma("pool", ones_bf[:], consts[:, 0:128])
    t_onesf = c.dma("pool", onesf[:], consts[:, 0:128])
    t_id = c.dma("pool", ident_bf[:], consts[:, 128:256])
    t_sm = c.dma("pool", smallsB[:], consts[:, 256:268])
    for p in range(NP):
        t_R = c.dma("sp", R[:, p * T:(p + 1) * T], Rin[p * 128:(p + 1) * 128, :])

    c.barrier()

    def slot(i):
        return bc[:, i * CW:(i + 1) * CW]

    if need_A:
        cc = bc[0:64, 0:2 * CW]
        cct = bc[0:64, 3 * CW:4 * CW]

    state = {"stats_pe": None, "stats_dve": None, "bc_guard": None}

    def col_stats_mm(src, extra=(), bs=None):
        b1, b2 = bs if bs else (pb[5][:, 0:CW], pb[6][:, 0:CW])
        for p in range(NP):
            act(c, sqbf[:, p * CW:(p + 1) * CW], src(p), AF.Square,
                w=([state["stats_pe"]] + list(extra)) if p == 0 else ())
        t_sq = c.tok("act")
        c.wait("pe", t_sq, t_onesf, state["stats_dve"])
        for p in range(NP):
            mm(c, b1, onesf[:], src(p),
               start=(p == 0), stop=(p == NP - 1))
        last = None
        for p in range(NP):
            last = mm(c, b2, ones_bf[:], sqbf[:, p * CW:(p + 1) * CW],
                      start=(p == 0), stop=(p == NP - 1), inc=(p == NP - 1))
        state["stats_pe"] = last
        return last

    def col_stats_fin(mslot, vslot, last, bs=None):
        b1, b2 = bs if bs else (pb[5][:, 0:CW], pb[6][:, 0:CW])
        ts(c, slot(mslot), b1, f32(1.0 / D), MUL, w=[last])
        state["stats_dve"] = ts(c, slot(vslot), b2, f32(1.0 / D), MUL)
        tt(c, slot(7), slot(mslot), slot(mslot), MUL)
        return tt(c, slot(vslot), slot(vslot), slot(7), SUB)

    def col_stats(src, mslot, vslot, extra=()):
        last = col_stats_mm(src, extra)
        return col_stats_fin(mslot, vslot, last)

    def rstd_into(vslot, oslot):
        act(c, slot(oslot), slot(vslot), AF.Ln, bias=EPS, w=[c.tok("dve")])
        a = act(c, slot(oslot), slot(oslot), AF.Exp, scale=f32(-0.5))
        c.wait("dve", a)
        return a

    def Rcol(p, off, n=CW):
        return R[:, p * T + off: p * T + off + n]

    def Rview(off, n=CW):
        return R[:].rearrange("q (p t) -> q p t", p=NP)[:, :, off:off + n]

    def colview(tensor, npt=NP, n=CW):
        return tensor[:].rearrange("q (p t) -> q p t", p=npt)

    def sublayer_A():
        t_vA = c.dma("pool", vA[:], vecsA[:, :])
        t_bm1 = c.dma("pool", bm1[:], bmlp1[:, :])
        t_eig = c.dma("pool", eig_sb[:], eig[:, :])
        t_cp = c.dma("pool", couple_sb[:], couple[:, :])
        t_win = c.dma("pool", winsb[:].rearrange("p (k m) -> p k m", k=8),
                      w_inT[:, :].rearrange("(k p) m -> p k m", p=128))
        t_wout = c.dma("pool", woutsb[:], w_outT[:, :])
        t_wg = c.dma("pool", wgs[:, :], w_gateT[:, :])
        t_w1 = c.dma("pool", w1s[:, :], w1T[:, :])
        t_w2 = c.dma("pool", w2s[:, :], w2T[:, :])
        g1 = vA[:, 0:8]; b1 = vA[:, 8:16]; g2 = vA[:, 16:24]; b2 = vA[:, 24:32]
        bgate = vA[:, 32:40]; bmlp2v = vA[:, 40:48]

        ld_c = eig_sb[:, 0:1]; fq = eig_sb[:, 1:2]
        mag = lamt[:, 0:1]; fr = lamt[:, 1:2]; q0 = lamt[:, 2:3]; q1 = lamt[:, 3:4]
        act(c, mag, ld_c, AF.Sigmoid, w=[t_eig])
        vcopy(c, fr, fq, w=[t_eig])
        for _ in range(6):
            act(c, q0, fr, AF.Abs, w=[c.tok("dve")])
            act(c, q0, q0, AF.Sign, bias=f32(-np.pi))
            ts(c, q0, q0, f32(0.0), MAXO, w=[c.tok("act")])
            act(c, q1, fr, AF.Sign, w=[c.tok("dve")])
            tt(c, q0, q0, q1, MUL, w=[c.tok("act")])
            stt(c, fr, q0, f32(-2.0 * np.pi), fr, MUL, ADD)
        act(c, lam[:, 16:17], fr, AF.Sin, w=[c.tok("dve")])
        ts(c, q0, fr, f32(-1.0), MUL, f32(np.pi / 2), ADD, w=[c.tok("act")])
        act(c, q1, q0, AF.Sign, bias=f32(-np.pi), w=[c.tok("dve")])
        ts(c, q1, q1, f32(0.0), MAXO, w=[c.tok("act")])
        stt(c, q0, q1, f32(-2.0 * np.pi), q0, MUL, ADD)
        act(c, lam[:, 17:18], q0, AF.Sin, w=[c.tok("dve")])
        tt(c, lam[:, 0:1], mag, lam[:, 17:18], MUL, w=[c.tok("act")])
        tt(c, lam[:, 1:2], mag, lam[:, 16:17], MUL)
        for i in range(1, 6):
            pr = lam[:, 2 * i - 2:2 * i - 1]; pi_ = lam[:, 2 * i - 1:2 * i]
            qr = lam[:, 2 * i:2 * i + 1]; qi = lam[:, 2 * i + 1:2 * i + 2]
            tt(c, q0, pr, pr, MUL)
            tt(c, q1, pi_, pi_, MUL)
            tt(c, qr, q0, q1, SUB)
            tt(c, q0, pr, pi_, MUL)
            ts(c, qi, q0, f32(2.0), MUL)
        for i in range(6):
            ts(c, lam[:, 12 + i:13 + i], lam[:, 2 * i + 1:2 * i + 2], f32(-1.0), MUL)
        t_lam = c.tok("dve")
        t_halo = c.dma("pool", halo_sb[:], halo[:, :])
        hb = slot(6)
        act(c, hb[:, 16:24], halo_sb[:, 0:8], AF.Square, w=[t_halo])
        t_hsq = c.tok("act")
        c.wait("pe", t_halo, t_hsq, t_onesf)
        for p in range(NP):
            mm(c, pb[2][:, 0:1], onesf[:], halo_sb[:, p:p + 1],
               start=(p == 0), stop=(p == NP - 1))
        for p in range(NP):
            t_hs2 = mm(c, pb[2][:, 1:2], onesf[:], hb[:, 16 + p:17 + p],
                       start=(p == 0), stop=(p == NP - 1), inc=(p == NP - 1))
        ts(c, hb[:, 0:1], pb[2][:, 0:1], f32(1.0 / D), MUL, w=[t_hs2])
        ts(c, hb[:, 1:2], pb[2][:, 1:2], f32(1.0 / D), MUL)
        tt(c, hb[:, 2:3], hb[:, 0:1], hb[:, 0:1], MUL)
        tt(c, hb[:, 1:2], hb[:, 1:2], hb[:, 2:3], SUB)
        act(c, hb[:, 2:3], hb[:, 1:2], AF.Ln, bias=EPS, w=[c.tok("dve")])
        t_ha = act(c, hb[:, 2:3], hb[:, 2:3], AF.Exp, scale=f32(-0.5))
        tt(c, hb[:, 3:4], hb[:, 2:3], hb[:, 2:3], MUL, w=[t_ha])
        tt(c, hb[:, 4:5], hb[:, 1:2], hb[:, 3:4], MUL)
        act(c, hb[:, 5:6], hb[:, 4:5], AF.Ln, bias=EPS, w=[c.tok("dve")])
        t_ha2 = act(c, hb[:, 5:6], hb[:, 5:6], AF.Exp, scale=f32(-0.5))
        tt(c, hb[:, 6:7], hb[:, 2:3], hb[:, 5:6], MUL, w=[t_ha2])
        ts(c, hb[:, 8:16], halo_sb[:, 0:8], hb[:, 0:1], AluOpType.subtract)
        ts(c, hb[:, 8:16], hb[:, 8:16], hb[:, 6:7], MUL)
        tt(c, hb[:, 8:16], hb[:, 8:16], vA[:, 0:8], MUL, w=[t_vA])
        tt(c, hb[:, 8:16], hb[:, 8:16], vA[:, 8:16], ADD)
        t_xnp = vcopy(c, ccb[:, 0:8], hb[:, 8:16])
        c.wait("pe", t_xnp, t_win)
        for k in range(NP):
            mm(c, pb[2][0:64, 2:3], winsb[:, k * 128:k * 128 + 64],
               ccb[:, k:k + 1], start=(k == 0), stop=(k == NP - 1))
        for k in range(NP):
            t_bp = mm(c, pb[2][0:64, 3:4], winsb[:, k * 128 + 64:(k + 1) * 128],
                      ccb[:, k:k + 1], start=(k == 0), stop=(k == NP - 1),
                      inc=(k == NP - 1))
        ts(c, carry[:, 0:1], pb[2][0:64, 2:3], halo_sb[0:64, 8:9], MUL, w=[t_bp])
        t_carry = ts(c, carry[:, 1:2], pb[2][0:64, 3:4], halo_sb[0:64, 8:9], MUL)
        c.wait("pe", t_carry)

        c.wait("pe", t_win, t_wout, t_cp, t_R)
        c.wait("dve", t_vA, t_bm1, t_lam, t_carry)
        c.wait("act", t_vA, t_bm1, t_R)
        prevA = {"gate_pe": None, "x1_dve": None, "ccb_act": None,
                 "mlp2_dve": None, "m1_last": None, "u32_act": None,
                 "gate_act": None, "stats_dve_g": None}
        colstate = {}

        def slab1(col):
            off = col * CW
            col_stats(lambda p: Rcol(p, off), 0, 1,
                      extra=[prevA["gate_pe"]])
            rstd_into(1, 2)
            tt(c, slot(7), slot(2), slot(2), MUL)
            tt(c, slot(3), slot(1), slot(7), MUL)
            rstd_into(3, 3)
            tt(c, slot(4), slot(2), slot(3), MUL)
            tt(c, slot(5), slot(0), slot(4), MUL)
            tt(c, slot(6), slot(0), slot(2), MUL)
            prevA["stats_dve_g"] = state["stats_dve"]
            xv = colview(xn_bf)
            tt(c, xv, Rview(off), bcast(slot(4)), MUL, w=[prevA["gate_pe"]])
            t_xnp = tt(c, xv, xv, bcast(slot(5)), SUB)
            for p in range(NP):
                t_xn = act(c, xn_bf[:, p * CW:(p + 1) * CW], xn_bf[:, p * CW:(p + 1) * CW],
                           AF.Identity, scale=g1[:, p:p + 1], bias=b1[:, p:p + 1],
                           w=[t_xnp])
            uv = colview(u32)
            tt(c, uv, Rview(off), bcast(slot(2)), MUL, w=[prevA["u32_act"]])
            t_y = tt(c, uv, uv, bcast(slot(6)), SUB)
            c.wait("pe", t_xn, prevA["x1_dve"])
            for k in range(NP):
                mm(c, pb[2][0:64, 0:CW], winsb[:, k * 128:k * 128 + 64],
                   xn_bf[:, k * CW:(k + 1) * CW], start=(k == 0),
                   stop=(k == NP - 1))
            for k in range(NP):
                t_beta = mm(c, pb[2][0:64, CW:2 * CW], winsb[:, k * 128 + 64:(k + 1) * 128],
                            xn_bf[:, k * CW:(k + 1) * CW], start=(k == 0),
                            stop=(k == NP - 1), inc=(k == NP - 1))
            t_c0 = vcopy(c, cc[:, :], pb[2][0:64, 0:2 * CW], w=[t_beta])
            cr = cc[:, 0:CW]; ci = cc[:, CW:2 * CW]
            tmp = cct[:, 0:CW]
            d, i = 1, 0
            while d < CHUNK:
                pr = lam[:, 2 * i:2 * i + 1]
                pi_ = lam[:, 2 * i + 1:2 * i + 2]
                nli = lam[:, 12 + i:13 + i]
                crv = cr.rearrange("k (n s) -> k n s", s=CHUNK)
                civ = ci.rearrange("k (n s) -> k n s", s=CHUNK)
                tmv = tmp.rearrange("k (n s) -> k n s", s=CHUNK)
                dst_r = crv[:, :, 0:CHUNK - d]; src_r = crv[:, :, d:CHUNK]
                dst_i = civ[:, :, 0:CHUNK - d]; src_i = civ[:, :, d:CHUNK]
                tm = tmv[:, :, 0:CHUNK - d]
                vcopy(c, tm, src_r)
                stt(c, dst_r, src_r, pr, dst_r, MUL, ADD)
                stt(c, dst_r, src_i, nli, dst_r, MUL, ADD)
                stt(c, dst_i, src_i, pr, dst_i, MUL, ADD)
                stt(c, dst_i, tm, pi_, dst_i, MUL, ADD)
                d *= 2
                i += 1
            if col == NCOL // 2:
                ts(c, carry[:], carry[:], halo_sb[0:64, 9:10], MUL)
            nch = CW // CHUNK
            for half in range(2):
                cch = cc[:, half * CW:(half + 1) * CW]
                ccv2 = cch.rearrange("k (n s) -> k n s", s=CHUNK)
                bv = pb[2][0:64, half * CW:(half + 1) * CW].rearrange(
                    "k (n s) -> k n s", s=CHUNK)
                tt(c, ccv2[:, 1:nch, 0:1], ccv2[:, 1:nch, 0:1],
                   bv[:, 0:nch - 1, CHUNK - 1:CHUNK], ADD)
                tt(c, cch[:, 0:1], cch[:, 0:1], carry[:, half:half + 1], ADD)
                vcopy(c, carry[:, half:half + 1],
                      pb[2][0:64, (half + 1) * CW - 1:(half + 1) * CW])
            colstate[col] = [None, t_y, None, None, None, ([], []), c.tok("dve")]

        def slab1b(col):
            c.wait("pe", c.tok("dve"), prevA["ccb_act"])
            mm(c, pb[2][0:64, 0:CW], couple_sb[:], cc[:, 0:CW], start=True, stop=True)
            t_cpl = mm(c, pb[2][64:128, 0:CW], couple_sb[:], cc[:, CW:2 * CW],
                       start=True, stop=True, inc=True)
            act(c, ccb[0:64, :], pb[2][0:64, 0:CW], AF.Identity, w=[t_cpl])
            t_ccb = act(c, ccb[64:128, :], pb[2][64:128, 0:CW], AF.Identity)
            prevA["ccb_act"] = t_ccb
            colstate[col][0] = t_ccb

        def gate_step(col, do):
            gst = colstate[col][5]
            gate_mil, t_gb_hist = gst
            wslice = wbig[:, (do % 4) * 1024:(do % 4 + 1) * 1024]
            t_wgl = c.dma("sp", wslice,
                          wgs[:, do * 1024:(do + 1) * 1024],
                          waits=[gate_mil[do - 4], t_wg] if do >= 4
                                else [t_wg, prevA["m1_last"]])
            c.wait("pe", t_wgl)
            gb_bank = pb[3] if do % 2 == 0 else pb[5]
            if do >= 2:
                c.wait("pe", t_gb_hist[do - 2])
            if do == 0:
                c.wait("pe", prevA["gate_act"], prevA["stats_dve_g"])
            for k in range(NP):
                t_g = mm(c, gb_bank[:, 0:CW],
                         wslice[:, k * 128:(k + 1) * 128],
                         xn_bf[:, k * CW:(k + 1) * CW], start=(k == 0),
                         stop=(k == NP - 1), inc=(k == NP - 1))
            gate_mil.append(t_g)
            t_gb = act(c, gate_bf[:, do * CW:(do + 1) * CW], gb_bank[:, 0:CW],
                       AF.Sigmoid, bias=bgate[:, do:do + 1],
                       w=[t_g, colstate[col][6]])
            t_gb_hist.append(t_gb)

        def slab2a(col):
            off = col * CW
            st = colstate[col]
            t_ccb, t_y = st[0], st[1]
            gate_mil, t_gb_hist = st[5]
            c.wait("pe", t_ccb, t_y)
            c.wait("pe", t_gb_hist[-2])
            t_h_hist = []
            t_hx1_hist = []
            for do in range(NP):
                hb = pb[2] if do % 2 == 0 else pb[4]
                if do >= 2:
                    c.wait("pe", t_hx1_hist[do - 2])
                t_h = mm(c, hb[:, 0:CW], woutsb[:, do * 128:(do + 1) * 128],
                         ccb[:], start=True, stop=True, inc=True)
                t_h_hist.append(t_h)
                tt(c, slot(7), gate_bf[:, do * CW:(do + 1) * CW], hb[:, 0:CW], MUL,
                   w=[t_gb_hist[do], t_h])
                t_hx1 = tt(c, u32[:, do * CW:(do + 1) * CW],
                           u32[:, do * CW:(do + 1) * CW], slot(7), ADD)
                t_hx1_hist.append(t_hx1)
            gate_mil.append(t_h_hist[-1])
            tt(c, Rview(off), Rview(off), colview(u32), ADD, w=[t_hx1])
            col_stats(lambda p: u32[:, p * CW:(p + 1) * CW], 0, 1,
                      extra=[c.tok("dve")])
            rstd_into(1, 2)
            tt(c, slot(5), slot(0), slot(2), MUL)
            uv2 = colview(u32)
            tt(c, uv2, uv2, bcast(slot(2)), MUL)
            t_x2p = tt(c, uv2, uv2, bcast(slot(5)), SUB)
            for p in range(NP):
                t_x2n = act(c, x2n_bf[:, p * CW:(p + 1) * CW], u32[:, p * CW:(p + 1) * CW],
                            AF.Identity, scale=g2[:, p:p + 1], bias=b2[:, p:p + 1],
                            w=[t_x2p])
            prevA["gate_pe"] = gate_mil[-1]
            prevA["gate_act"] = t_gb_hist[-1]
            prevA["x1_dve"] = t_hx1
            prevA["u32_act"] = c.tok("act")
            st[2] = t_x2n
            st[3] = gate_mil
            st[4] = t_hx1

        def slab2b1(col):
            off = col * CW
            st = colstate[col]
            t_x2n, gate_mil = st[2], st[3]
            c.wait("pe", t_x2n, t_w1)
            t_sil = c.tok("act")
            m1_mil = []
            for h in range(32):
                wslice = wbig[:, (h % 4) * 1024:(h % 4 + 1) * 1024]
                t_wl = c.dma("sp", wslice,
                             w1s[:, h * 1024:(h + 1) * 1024],
                             waits=[m1_mil[h - 4], t_w1] if h >= 4
                                   else [t_w1, gate_mil[-1]])
                c.wait("pe", t_wl)
                mb = (pb[0], pb[1], pb[4])[h % 3]
                if h >= 3:
                    c.wait("pe", t_sil_hist[h - 3])
                for k in range(NP):
                    t_m1 = mm(c, mb[:, 0:CW],
                              wslice[:, k * 128:(k + 1) * 128],
                              x2n_bf[:, k * CW:(k + 1) * CW], start=(k == 0),
                              stop=(k == NP - 1), inc=(k == NP - 1))
                m1_mil.append(t_m1)
                t_sil = act(c, hid_bf[:, h * CW:(h + 1) * CW], mb[:, 0:CW],
                            AF.Silu, bias=bm1[:, h:h + 1], w=[t_m1])
                if h == 0:
                    t_sil_hist = []
                t_sil_hist.append(t_sil)
            colstate[col][2] = (t_sil, gate_mil, m1_mil)

        def slab2b2(col):
            off = col * CW
            t_sil, gate_mil, m1_mil = colstate[col][2]
            colstate.pop(col)
            prevA["m1_last"] = m1_mil[-1]
            c.wait("pe", t_sil, t_w2)
            t_radd = None
            m2_mil = []
            for do in range(NP):
                buf = do % 2
                if do >= 2:
                    c.wait("pe", t_radd_hist[do - 2])
                for q in range(4):
                    qi = do * 4 + q
                    wslice = wbig2[:, (qi % 3) * 1024:(qi % 3 + 1) * 1024]
                    t_wl2 = c.dma("sp", wslice,
                                  w2s[:, qi * 1024:(qi + 1) * 1024],
                                  waits=[m2_mil[qi - 3], t_w2] if qi >= 3
                                        else [t_w2, t_sil, gate_mil[-1]])
                    c.wait("pe", t_wl2)
                    for k8 in range(8):
                        k = q * 8 + k8
                        t_m2 = mm(c, pb[0 + buf][:, 0:CW], wslice[:, k8 * 128:(k8 + 1) * 128],
                                  hid_bf[:, k * CW:(k + 1) * CW], start=(k == 0),
                                  stop=(k == 31), inc=(k8 == 7))
                    m2_mil.append(t_m2)
                t_radd = stt(c, Rcol(do, off), pb[0 + buf][:, 0:CW], bmlp2v[:, do:do + 1],
                             Rcol(do, off), ADD, ADD, w=[t_m2])
                if do == 0:
                    t_radd_hist = []
                t_radd_hist.append(t_radd)
                if col + 1 in colstate:
                    gate_step(col + 1, do)
            prevA["mlp2_dve"] = t_radd

        slab1(0)
        slab1b(0)
        for do in range(NP):
            gate_step(0, do)
        for col in range(NCOL):
            slab2a(col)
            if col + 1 < NCOL:
                slab1(col + 1)
            slab2b1(col)
            if col + 1 < NCOL:
                slab1b(col + 1)
            slab2b2(col)
        c.barrier()

    def prologue_F():
        t_vF = c.dma("pool", vF[:], vecsF[:, :])
        t_b1f = c.dma("pool", b1f_sb[:], b1fd[:, :])
        t_wsc0 = c.dma("pool", wdT[:].rearrange("p (k m) -> p k m", k=8),
                       dirsT[:, :].rearrange("(k p) m -> p k m", p=128))
        t_wdir = c.dma("pool", wdirs[:], dirsL[:, :])
        ts(c, wdirs[:], wdirs[:], smallsB[0:64, 9:10], MUL, w=[t_wdir, t_sm])
        t_wc1 = c.dma("pool", wc1[:].rearrange("p (k m) -> p k m", k=8),
                      cw1[:, :].rearrange("(k p) m -> p k m", p=128))
        t_wc2 = c.dma("pool", wc2[:], cw2[:, :])
        t_w1f = c.dma("pool", ww1f[:], w1f[:, :])
        t_w2b = c.dma("pool", ww2blk[:].rearrange("p (k m) -> p k m", k=8),
                      w2blk[:, :].rearrange("(k p) m -> p k m", p=128))
        t_io = c.dma("pool", iota64[:], consts[:, 272:336])
        t_cl = c.dma("pool", clus64[:], consts[:, 336:400])
        t_b2 = c.dma("pool", b2tm[:], b2tmd[:, :])
        sraw = hid_bf[:, 0:1024].bitcast(F32)
        qa = hid_bf[:, 1024:2048].bitcast(F32)
        t_sr = c.dma("pool", sraw[:].rearrange("p (k m) -> p k m", k=8),
                     sigsT_raw[:, :].rearrange("(k p) m -> p k m", p=128))
        c.wait("act", t_sr)
        act(c, qa[:], sraw[:], AF.Sign, bias=f32(-0.3))
        t_q1 = c.tok("act")
        ts(c, qa[:], qa[:], 0.0, MAXO, w=[t_q1])
        act(c, sraw[:], sraw[:], AF.Sign, bias=f32(-0.3), scale=f32(-1.0),
            w=[c.tok("dve")])
        t_q2 = c.tok("act")
        ts(c, sraw[:], sraw[:], 0.0, MAXO, w=[t_q2])
        wscv = wsc[:].rearrange("p (k m) -> p k m", m=72)
        qv = qa[:].rearrange("p (k m) -> p k m", m=64)
        sv_ = sraw[:].rearrange("p (k m) -> p k m", m=64)
        tt(c, qv, qv, sv_, SUB)
        t_q = tt(c, wscv[:, :, 0:64], qv, qv, MINO)
        qg = qa[:].rearrange("p (k cl t) -> p k cl t", k=8, cl=8)
        red = hid_bf[:, 2048:2176].bitcast(F32)
        rv = red[:].rearrange("p (k cl) -> p k cl", k=8)
        vred(c, rv, qg, ADD)
        t_red = ts(c, red[:], red[:], 0.125, MUL)
        t_cs = act(c, red[:], red[:], AF.Sign, w=[t_red])
        tt(c, wscv[:, :, 64:72], rv, rv, MINO, w=[t_cs])
        c.wait("dve", t_sm)
        vcopy(c, knotsB[:, 0:1], smallsB[:, 0:1])
        ts(c, knotsB[:, 3:4], smallsB[:, 8:9], 1e-6, ADD)
        c.emit("dve", lambda v: v.reciprocal(out=knotsB[:, 1:2], in_=knotsB[:, 3:4]))
        vcopy(c, knotsB[:, 2:3], smallsB[:, 9:10])
        tt(c, dB[:], smallsB[:, 1:8], smallsB[:, 0:7], SUB)
        tt(c, calF[:, 0:6], dB[:, 0:6], dB[:, 1:7], SUB)
        vcopy(c, calF[:, 6:7], dB[:, 6:7])
        ts(c, calF[:, 0:7], calF[:, 0:7], 7.0, MUL)
        rowsb = hid_bf[:, 4096:5120].bitcast(F32)[0:1, :]
        c.wait("pe", t_wsc0, t_id, c.tok("dve"), c.tok("act"))
        for k in range(NP):
            t_dm = mm(c, pb[0][0:64, 0:1], wdT[:, k * 64:(k + 1) * 64],
                      ones_bf[:, 0:1], start=(k == 0), stop=(k == NP - 1),
                      inc=(k == NP - 1))
        t_sq2 = c.dma("pool", qa[:, 0:512].rearrange("p (k m) -> p k m", k=8),
                      dirsT[:, :].rearrange("(k p) m -> p k m", p=128),
                      waits=[c.tok("dve"), c.tok("act")])
        t_sqa = act(c, sqbf[:, 0:512], qa[:, 0:512], AF.Square, w=[t_sq2])
        c.wait("pe", t_sqa)
        for k in range(NP):
            t_ds = mm(c, pb[1][0:64, 0:1], sqbf[:, k * 64:(k + 1) * 64],
                      ones_bf[:, 0:1], start=(k == 0), stop=(k == NP - 1),
                      inc=(k == NP - 1))
        ts(c, drows[:, 0:1], pb[0][0:64, 0:1], f32(1.0 / D), MUL, w=[t_dm, t_ds])
        ts(c, drows[:, 1:2], pb[1][0:64, 0:1], f32(1.0 / D), MUL)
        c.wait("pe", c.tok("dve"))
        transp(c, pbb[0:1, 544:608], drows[:, 0:1], ident_bf[0:64, 0:64], inc=True)
        t_tr = c.tok("pe")
        vcopy(c, rowsb[:, 0:64], pbb[0:1, 544:608], w=[("pe", t_tr[1])])
        transp(c, pbb[0:1, 608:672], drows[:, 1:2], ident_bf[0:64, 0:64], inc=True,
               w=[c.tok("dve")])
        t_tr2 = c.tok("pe")
        vcopy(c, rowsb[:, 64:128], pbb[0:1, 608:672], w=[("pe", t_tr2[1])])
        c.wait("pe", c.tok("dve"))
        mm(c, pb[3][:, 0:64], onesf[0:1, :], rowsb[0:1, 0:64], start=True, stop=True)
        t_bb = mm(c, pb[4][:, 0:64], onesf[0:1, :], rowsb[0:1, 64:128], start=True,
                  stop=True, inc=True)
        tb2 = tblp.rearrange("p (c w) -> p c w", c=2)
        vcopy(c, tb2[:, :, 128:192],
              pb[3][:, 0:64].unsqueeze(1).broadcast_to((128, 2, 64)), w=[t_bb])
        vcopy(c, tb2[:, :, 192:256],
              pb[4][:, 0:64].unsqueeze(1).broadcast_to((128, 2, 64)))
        c.wait("dve", t_b2)
        vcopy(c, tb2[:, :, 256:320],
              b2tm[:].unsqueeze(1).broadcast_to((128, 2, 64)))
        c.barrier()

    def sublayer_F(tag, fcols=None):
        gamma = vF[:, 0:8]; beta_ = vF[:, 8:16]; cb1 = vF[:, 16:18]
        cb2c = vF[0:2, 18:19]
        if fcols is None:
            fcols = NCOL
        tbz = tblp.rearrange("p (c w) -> p c w", c=2)
        tbl4 = tblp.rearrange("p (c t w) -> p c t w", c=2, t=5)
        pr4 = prodp.rearrange("p (c t w) -> p c t w", c=2, t=5)
        sels = rowsP[:, 16:26].rearrange("p (c t) -> p c t", c=2)

        def q(j):
            return rowsP[:, 2 * j:2 * j + 2].unsqueeze(2)

        def qb(j, n):
            return q(j).broadcast_to((128, 2, n))
        mskp = mskg.rearrange("p (c w) -> p c w", c=2)
        ohp = oh[:, 0:128]
        ohv3 = ohp.rearrange("p (c w) -> p c w", c=2)
        pr2f = oh[:, 128:256].rearrange("p (c w) -> p c w", c=2)
        identb = ident_bf[:].unsqueeze(1).broadcast_to((128, 2, 128))
        rr3 = rhsd_r.rearrange("p (c w) -> p c w", c=2)
        rm3 = rhsd_m.rearrange("p (c w) -> p c w", c=2)
        ex3 = extrF.rearrange("p (c w) -> p c w", c=2)
        fst = {}

        fstats_bs = (pb[1][:, CW:2 * CW], pb[3][:, CW:2 * CW])

        def fEs(col):
            off = col * CW
            c.wait("pe", state.get("bc_guard"), state.get("f_guard"))
            fst[("S", col)] = col_stats_mm(lambda p: Rcol(p, off), bs=fstats_bs)

        def fEa(col):
            off = col * CW
            col_stats_fin(0, 1, fst.pop(("S", col)), bs=fstats_bs)
            rstd_into(1, 2)
            tt(c, slot(5), slot(0), slot(2), MUL)
            tt(c, slot(7), slot(2), slot(2), MUL)
            tt(c, slot(3), slot(1), slot(7), MUL)
            uv = colview(u32)
            tt(c, uv, Rview(off), bcast(slot(2)), MUL)
            tt(c, uv, uv, bcast(slot(5)), SUB)
            t_z = vcopy(c, colview(xn_bf), uv, w=[state.get("xn_guard")])
            c.wait("pe", t_z, state.get("pb2_guard"))
            for k in range(NP):
                mm(c, pb[2][0:72, 0:CW], wsc[:, k * 72:(k + 1) * 72],
                   xn_bf[:, k * CW:(k + 1) * CW], start=(k == 0), stop=(k == NP - 1))
            for k in range(NP):
                t_zd = mm(c, pb[2][0:64, CW:2 * CW], wdT[:, k * 64:(k + 1) * 64],
                          xn_bf[:, k * CW:(k + 1) * CW], start=(k == 0),
                          stop=(k == NP - 1), inc=(k == NP - 1))
            c.wait("pe", state.get("pb4_guard"))
            for half in range(2):
                for k in range(NP):
                    t_c1 = mm(c, pb[4][:, half * CW:(half + 1) * CW],
                              wc1[:, k * 256 + half * 128:k * 256 + half * 128 + 128],
                              xn_bf[:, k * CW:(k + 1) * CW], start=(k == 0),
                              stop=(k == NP - 1), inc=(k == NP - 1 and half == 1))
            state["xn_guard"] = t_c1
            for half in range(2):
                t_h1 = act(c, h1_bf[:, half * CW:(half + 1) * CW],
                           pb[4][:, half * CW:(half + 1) * CW],
                           AF.Gelu, bias=cb1[:, half:half + 1],
                           w=[t_c1, state.get("c2_guard")] if half == 0 else [t_c1])
            state["pb4_guard"] = t_h1
            t_zde = act(c, zd_bf[:], pb[2][0:64, CW:2 * CW], AF.Identity,
                        w=[t_zd, state.get("tp_guard")])
            t_sg = act(c, uu[:], pb[2][0:72, 0:CW], AF.Sigmoid,
                       scale=knotsB[0:72, 1:2], w=[t_zd, state.get("uu_guard")])
            state["pb2_guard"] = t_sg
            fst[("Ea", col)] = (t_sg, t_h1, t_zde)

        def fEb(col):
            par = col % 2
            gtm = gtmP[par]
            rowsE = rowsEP[par]
            t_sg, t_h1, t_zde = fst.pop(("Ea", col))
            for j, sl in enumerate((2, 5, 3)):
                tt(c, ex3, identb, slot(sl).rearrange("p (c w) -> p c w", c=2), MUL)
                vred(c, rowsE[:, 2 * j:2 * j + 2].unsqueeze(2), ex3, ADD)
            t_cal = ts(c, gts[:], uu[:], calF[0:72, 6:7], MUL, knotsB[0:72, 0:1],
                       ADD, w=[t_sg])
            for m in range(1, 7):
                ts(c, slot(7)[0:72, :], uu[:], float(m / 7.0), MINO)
                t_cal = stt(c, gts[:], slot(7)[0:72, :], calF[0:72, m - 1:m], gts[:],
                            MUL, ADD)
            state["uu_guard"] = t_cal
            t_gts = vcopy(c, gts_bf[:], gts[:], w=[state.get("tp_guard")])
            c.wait("pe", t_h1)
            for k2 in range(2):
                t_c2 = mm(c, pb[5][0:2, 0:CW], wc2[:, k2 * 2:(k2 + 1) * 2],
                          h1_bf[:, k2 * CW:(k2 + 1) * CW], start=(k2 == 0),
                          stop=(k2 == 1), inc=(k2 == 1))
            t_cp2 = act(c, comp_bf[:], pb[5][0:2, 0:CW], AF.Tanh,
                        bias=cb2c, w=[t_c2, state.get("hh_guard")])
            state["c2_guard"] = t_c2
            c.wait("pe", t_cp2, t_gts, state.get("bc_guard"))
            t_hh = None
            t_hh_hist = []
            for gch in range(8):
                if gch >= 2:
                    c.wait("pe", t_hh_hist[gch - 2])
                t_hmm = mm(c, pb[5 + gch % 2][:, 0:CW], ww1f[:, gch * 128:(gch + 1) * 128],
                           comp_bf[:], start=True, stop=True, inc=True)
                t_hh = act(c, hid_bf[:, gch * CW:(gch + 1) * CW], pb[5 + gch % 2][:, 0:CW],
                           AF.Relu, bias=b1f_sb[:, gch:gch + 1],
                           w=[t_hmm, state.get("magmm_guard")] if gch == 0
                             else [t_hmm])
                t_hh_hist.append(t_hh)
                if gch == 7:
                    state["hh_guard"] = t_hmm
            c.wait("pe", t_gts, t_zde, state.get("pbb_guard2"))
            for tch in range(2):
                transp(c, pbb[:, tch * 80:tch * 80 + 72],
                       gts_bf[:, tch * 128:(tch + 1) * 128], ident_bf[0:72, 0:72])
                transp(c, pbb[:, 160 + tch * 64:160 + (tch + 1) * 64],
                       zd_bf[:, tch * 128:(tch + 1) * 128], ident_bf[0:64, 0:64],
                       inc=(tch == 1))
            t_tp = c.tok("pe")
            state["tp_guard"] = t_tp
            vcopy(c, gtm[:, 0:72], pbb[:, 0:72], w=[t_tp])
            vcopy(c, gtm[:, 80:152], pbb[:, 80:152])
            t_ztm = vcopy(c, tbz[:, :, par * 64:par * 64 + 64],
                          pbb[:, 160:288].rearrange("p (c w) -> p c w", c=2))
            c.wait("pe", t_hh, t_ztm, state.get("mag_evac"))
            for tch in range(2):
                for gch in range(8):
                    t_mg = mm(c, pb[0][:, tch * 64:(tch + 1) * 64],
                              hid_bf[:, gch * CW + tch * 128:gch * CW + (tch + 1) * 128],
                              ww2blk[:, gch * 64:(gch + 1) * 64], start=(gch == 0),
                              stop=(gch == 7), inc=(gch == 7 and tch == 1))
            t_mge = act(c, magsbP[par][:], pb[0][:, 0:128], AF.Identity,
                        w=[t_mg, fst.get(("mag_read", par))])
            state["mag_evac"] = t_mge
            state["magmm_guard"] = t_mg
            fst[("E", col)] = t_mge

        def fD(col):
            off = col * CW
            par = col % 2
            gtm = gtmP[par]
            rowsE = rowsEP[par]
            t_mge = fst.pop(("E", col))
            for tch in range(2):
                vmax8(c, mxcB[:, tch * 8:tch * 8 + 8],
                      gtm[:, tch * 80 + 64:tch * 80 + 72])
                vmaxidx(c, idxu[:, tch * 8:tch * 8 + 8],
                        mxcB[:, tch * 8:tch * 8 + 8],
                        gtm[:, tch * 80 + 64:tch * 80 + 72])
            vcopy(c, idxfB[:, 0:16], idxu[:, 0:16])
            cidxb = idxfB[:, 0:16].rearrange("p (c w) -> p c w", c=2)[:, :, 0:1]
            tt(c, mskp, clus64[:].unsqueeze(1).broadcast_to((128, 2, 64)),
               cidxb.broadcast_to((128, 2, 64)), AluOpType.is_equal)
            g64p = gtm[:, 0:160].rearrange("p (c w) -> p c w", c=2)[:, :, 0:64]
            stt(c, mskp, mskp, 1000.0, g64p, MUL, ADD)
            for tch in range(2):
                vmax8(c, mxtF[:, tch * 8:tch * 8 + 8],
                      mskg[:, tch * 64:(tch + 1) * 64])
                vmaxidx(c, idxu[:, 16 + tch * 8:24 + tch * 8],
                        mxtF[:, tch * 8:tch * 8 + 8],
                        mskg[:, tch * 64:(tch + 1) * 64])
            vcopy(c, idxfB[:, 16:32], idxu[:, 16:32])
            tidxb = idxfB[:, 16:32].rearrange("p (c w) -> p c w", c=2)[:, :, 0:1]
            tt(c, ohv3, iota64[:].unsqueeze(1).broadcast_to((128, 2, 64)),
               tidxb.broadcast_to((128, 2, 64)), AluOpType.is_equal)
            tt(c, pr4, tbl4, ohv3.unsqueeze(2).broadcast_to((128, 2, 5, 64)), MUL)
            vred(c, sels.unsqueeze(3), pr4, ADD)
            zselv = sels[:, :, par:par + 1]
            tt(c, pr2f, ohv3,
               magsbP[par][:].rearrange("p (c w) -> p c w", c=2), MUL, w=[t_mge])
            fst[("mag_read", par)] = vred(c, q(6), pr2f, ADD)
            tt(c, q(1), q(6), sels[:, :, 4:5], ADD)
            ts(c, rowsP[:, 2:4], rowsP[:, 2:4], knotsB[:, 2:3], MUL)
            vcopy(c, rowsP[:, 0:2], rowsE[:, 4:6], w=[state.get("rp_guard")])
            stt(c, q(3), q(1), float(2.0 / D), zselv, MUL, MUL)
            tt(c, q(0), q(0), q(3), ADD)
            tt(c, q(3), q(1), q(1), MUL)
            tt(c, q(3), q(3), sels[:, :, 3:4], MUL)
            tt(c, q(0), q(0), q(3), ADD)
            tt(c, q(2), q(1), sels[:, :, 2:3], MUL)
            tt(c, q(3), q(2), q(2), MUL)
            t_varp = tt(c, q(0), q(0), q(3), SUB)
            act(c, rowsP[:, 8:10], rowsP[:, 0:2], AF.Ln, bias=EPS, w=[t_varp])
            t_rp = act(c, rowsP[:, 8:10], rowsP[:, 8:10], AF.Exp, scale=f32(-0.5))
            state["rp_guard"] = t_rp
            tt(c, rowsP[:, 12:14], rowsE[:, 0:2], rowsP[:, 8:10], MUL, w=[t_rp])
            tt(c, rowsP[:, 14:16], rowsE[:, 2:4], rowsP[:, 4:6], ADD)
            tt(c, rowsP[:, 14:16], rowsP[:, 14:16], rowsP[:, 8:10], MUL)
            t_ptm = tt(c, ptm[:].rearrange("p (c w) -> p c w", c=2), ohv3,
                       qb(4, 64), MUL, w=[state.get("ptm_guard")])
            t_dr = tt(c, rr3, identb, qb(6, 128), MUL)
            t_dm2 = tt(c, rm3, identb, qb(7, 128), MUL)
            c.wait("pe", t_dr, t_dm2, state.get("bc_guard"), state["stats_dve"])
            mm(c, pb[5][:, CW:2 * CW], onesf[:], rhsd_r[:, :], start=True, stop=True)
            t_bc2 = mm(c, pb[6][:, CW:2 * CW], onesf[:], rhsd_m[:, :], start=True,
                       stop=True, inc=True)
            c.wait("pe", t_ptm)
            for tch in range(2):
                transp(c, pbb[0:64, 288 + tch * 128:288 + (tch + 1) * 128],
                       ptm[:, tch * 64:(tch + 1) * 64], ident_bf[:], inc=(tch == 1))
            t_pt = c.tok("pe")
            state["ptm_guard"] = t_pt
            vcopy(c, pT_bf[:, 0:128], pbb[0:64, 288:288 + 128],
                  w=[t_pt, state.get("pT_guard")])
            t_pte = vcopy(c, pT_bf[:, 128:256], pbb[0:64, 416:416 + 128])
            state["pbb_guard2"] = t_pte
            vcopy(c, slot(4), pb[5][:, CW:2 * CW], w=[t_bc2])
            t_bce = vcopy(c, slot(6), pb[6][:, CW:2 * CW])
            state["bc_guard"] = t_bce
            c.wait("pe", t_pte)
            t3_hist = []
            for do in range(NP):
                tmp = rhsd_r if do % 2 == 0 else rhsd_m
                fb = pb[3] if do % 2 == 0 else pb[1]
                if do >= 2:
                    c.wait("pe", t3_hist[do - 2])
                elif state.get("f_guard") is not None:
                    c.wait("pe", state["f_guard"])
                t_f = mm(c, fb[:, 0:CW], wdirs[:, do * 128:(do + 1) * 128],
                         pT_bf[:], start=True, stop=True, inc=True)
                tt(c, tmp[:, :], Rcol(do, off), slot(4), MUL,
                   w=[t_bce] if do == 0 else ())
                tt(c, tmp[:, :], tmp[:, :], slot(6), SUB)
                t3 = tt(c, tmp[:, :], tmp[:, :], fb[:, 0:CW], ADD, w=[t_f])
                t3_hist.append(t3)
                ts(c, tmp[:, :], tmp[:, :], gamma[:, do:do + 1], MUL,
                   beta_[:, do:do + 1], ADD)
                tt(c, Rcol(do, off), Rcol(do, off), tmp[:, :], ADD)
            state["f_guard"] = t3_hist[-1]
            state["pT_guard"] = t_f

        fEs(0)
        fEa(0)
        for i in range(fcols):
            if i + 1 < fcols:
                fEs(i + 1)
            fEb(i)
            if i + 1 < fcols:
                fEa(i + 1)
            if i >= 1:
                fD(i - 1)
        fD(fcols - 1)
        c.barrier()

    def sublayer_B():
        t_wq = c.dma("pool", qbs[:, :], qTd[:, :])
        t_wcb = c.dma("pool", cbs[:, :], combd[:, :])
        t_st = None
        for col in range(NCOL):
            off = col * CW
            col_stats(lambda p: Rcol(p, off), 0, 1, extra=[t_st])
            rstd_into(1, 2)
            tt(c, slot(5), slot(0), slot(2), MUL)
            xv = colview(xn_bf)
            tt(c, xv, Rview(off), bcast(slot(2)), MUL)
            t_z = tt(c, xv, xv, bcast(slot(5)), SUB)
            t_st = c.dma("sp", zscr[:, off:off + CW].rearrange("(k p) t -> p k t", p=128),
                         xn_bf[:].rearrange("p (k t) -> p k t", k=NP), waits=[t_z])
            c.wait("dve", t_st)
        c.barrier()
        NTCH = T // 128
        gmil = []
        t_zl = None
        for fh in range(2):
            for tch in range(NTCH):
                gi = fh * NTCH + tch
                buf = gi % 2
                zt = ztT[:, buf * 1024:(buf + 1) * 1024]
                tw = [t_zl] if t_zl is not None else []
                if gi >= 2:
                    tw.append(gmil[gi - 2])
                t_zl = c.dma("sp", zt, zscr[:, tch * 128:(tch + 1) * 128],
                             transpose=True, waits=tw)
                c.wait("pe", t_zl)
                for q in range(4):
                    t_gq = mm(c, pb[q][:, fh * 256:fh * 256 + 256],
                              zt[:, q * 256 + fh * 128:q * 256 + fh * 128 + 128],
                              zt[:, q * 256:(q + 1) * 256],
                              start=(tch == 0), stop=(tch == NTCH - 1),
                              inc=(q == 3))
                gmil.append(t_gq)
            t_gr = c.tok("pe")
            for q in range(4):
                vcopy(c, gram[:, (q * 2 + fh) * 256:(q * 2 + fh) * 256 + 256],
                      pb[q][:, fh * 256:fh * 256 + 256], w=[t_gr])
        t_gf = vcopy(c, gram_bf, gram)
        c.barrier()
        for a in range(4):
            t_wqa = c.dma("sp", wqa, qbs[:, a * 2048:(a + 1) * 2048],
                          waits=[("pe", c.cnt["pe"]), t_wq])
            t_wca = c.dma("sp", wcba, cbs[:, a * 2048:(a + 1) * 2048],
                          waits=[t_wcb, t_gf, ("pe", c.cnt["pe"])])
            c.wait("pe", t_wqa, t_wca)
            t1hist = []
            for gi, (q, gch) in enumerate([(q, g) for q in range(4) for g in range(2)]):
                if gi >= 2:
                    c.wait("pe", t1hist[gi - 2])
                for fh in range(2):
                    t_t1 = mm(c, pb[4 + gi % 2][:, 0:256],
                              gram_bf[:, (q * 2 + fh) * 256 + gch * 128:
                                      (q * 2 + fh) * 256 + gch * 128 + 128],
                              wqa[:, (q * 2 + fh) * 256:
                                  (q * 2 + fh) * 256 + 256],
                              start=(fh == 0), stop=(fh == 1),
                              inc=(fh == 1))
                t1hist.append(act(c, t1T[:, (q * 2 + gch) * 256:(q * 2 + gch) * 256 + 256],
                                  pb[4 + gi % 2][:, 0:256], AF.Identity, w=[t_t1]))
            t_t1e = c.tok("act")
            c.wait("pe", t_t1e)
            for ech in range(2):
                if ech == 1:
                    c.wait("pe", c.tok("act"))
                for k in range(8):
                    t_M = mm(c, pb[6][:, ech * 256:ech * 256 + 256],
                             t1T[:, k * 256 + ech * 128:k * 256 + ech * 128 + 128],
                             wcba[:, k * 256:(k + 1) * 256],
                             start=(k == 0), stop=(k == 7), inc=(k == 7))
                act(c, Msb[:, (a * 2 + ech) * 256:(a * 2 + ech) * 256 + 256],
                    pb[6][:, ech * 256:ech * 256 + 256], AF.Identity, w=[t_M])
            c.wait("pe", c.tok("act"))
        c.barrier()
        for col in range(NCOL // 2):
            off = col * CW
            t_zl2 = c.dma("sp", xn_bf[:].rearrange("p (k t) -> p k t", k=NP),
                          zscr[:, off:off + CW].rearrange("(k p) t -> p k t", p=128))
            c.wait("pe", t_zl2)
            t_ap_hist = []
            for do in range(NP):
                a, gh = do // 2, do % 2
                if do >= 2:
                    c.wait("pe", t_ap_hist[do - 2])
                for ech in range(2):
                    t_ap = mm(c, pb[do % 2][:, 0:CW],
                              Msb[:, (a * 2 + ech) * 256 + gh * 128:
                                  (a * 2 + ech) * 256 + gh * 128 + 128],
                              xn_bf[:, (a * 2 + ech) * CW:(a * 2 + ech + 1) * CW],
                              start=(ech == 0), stop=(ech == 1), inc=(ech == 1))
                t_r = tt(c, Rcol(do, off), Rcol(do, off), pb[do % 2][:, 0:CW], ADD,
                         w=[t_ap])
                t_ap_hist.append(t_r)
            c.barrier()

    order = [p for p in ("A", "F1", "B", "F2") if p in parts]
    did_A = False
    did_Fpro = False
    for part in order:
        if part == "A":
            sublayer_A()
        elif part == "B":
            sublayer_B()
        elif part in ("F1", "F2"):
            if not did_Fpro:
                prologue_F()
                did_Fpro = True
            sublayer_F(part, fcols=NCOL // 2 if part == "F2" else None)

    c.barrier()
    outs = []
    for p in range(NP):
        outs.append(c.dma("sp", Rout[p * 128:(p + 1) * 128, :],
                          R[:, p * T:p * T + T // 2]))
    c.wait("sp", *outs)
    c.play()
    es.close()
    return nc



def _host_prep(inputs):
    ap = {}
    g = lambda k: np.ascontiguousarray(np.asarray(inputs[k], dtype=np.float32))
    ones = np.ones((128, 128), np.float32)
    ident = np.eye(128, dtype=np.float32)
    smalls = np.zeros((16,), np.float32)
    smalls[0:8] = g("L0_ffn_knot_values")
    smalls[8] = g("L0_ffn_temperature")[0]
    smalls[9] = g("L0_ffn_output_scale")[0]
    smalls[10:12] = g("L0_ffn_compress_b2")
    iota64 = np.broadcast_to(np.arange(64, dtype=np.float32), (128, 64))
    clus64 = np.broadcast_to((np.arange(64) // 8).astype(np.float32), (128, 64))
    consts = np.concatenate([ones, ident, np.broadcast_to(smalls, (128, 16)),
                             iota64, clus64], axis=1)
    ap["consts"] = np.ascontiguousarray(consts.astype(np.float32))
    ap["w_inT"] = np.ascontiguousarray(g("L0_attn_a_w_in").T)
    ap["w_outT"] = np.ascontiguousarray(g("L0_attn_a_w_out").T)

    def group_w(wT, grpcols):
        K8, M = wT.shape[0] // 128, wT.shape[1]
        ngrp = M // grpcols
        out = wT.reshape(K8, 128, ngrp, grpcols).transpose(1, 2, 0, 3)
        return np.ascontiguousarray(out.reshape(128, ngrp * K8 * grpcols))

    ap["w_gateT"] = group_w(g("L0_attn_a_w_gate").T, 128)
    ap["w1T"] = group_w(g("L0_attn_a_w_mlp1").T, 128)
    ap["w2T"] = group_w(g("L0_attn_a_w_mlp2").T, 128)
    vecsA = np.zeros((128, 48), np.float32)
    for i, k in enumerate(["ln1_g", "ln1_b", "ln2_g", "ln2_b", "b_gate", "b_mlp2"]):
        vecsA[:, 8 * i:8 * (i + 1)] = g(f"L0_attn_a_{k}").reshape(8, 128).T
    ap["vecsA"] = vecsA
    ap["bmlp1"] = np.ascontiguousarray(g("L0_attn_a_b_mlp1").reshape(32, 128).T)
    ap["eig"] = np.ascontiguousarray(
        np.stack([g("L0_attn_a_log_decay"), g("L0_attn_a_frequency")], axis=1))
    cpl = np.zeros((64, 64), np.float32)
    cp = g("L0_attn_a_coupling")
    for h in range(4):
        cpl[h * 16:(h + 1) * 16, h * 16:(h + 1) * 16] = cp[h].T
    ap["couple"] = cpl
    ap["sigsT_raw"] = np.ascontiguousarray(g("L0_ffn_signatures_raw").T)
    ap["dirsT"] = np.ascontiguousarray(g("L0_ffn_directions").T)
    ap["dirsL"] = g("L0_ffn_directions")
    ap["cw1"] = g("L0_ffn_compress_w1")
    cw2 = g("L0_ffn_compress_w2")
    ap["cw2"] = np.ascontiguousarray(cw2.reshape(2, 128, 2).transpose(1, 0, 2).reshape(128, 4))
    ap["w1f"] = np.ascontiguousarray(g("L0_ffn_spline_w1").transpose(1, 0, 2).reshape(2, 1024))
    w2b = np.zeros((1024, 64), np.float32)
    sw2 = g("L0_ffn_spline_w2")
    for t in range(64):
        w2b[t * 16:(t + 1) * 16, t] = sw2[t, :, 0]
    ap["w2blk"] = w2b
    vecsF = np.zeros((128, 20), np.float32)
    vecsF[:, 0:8] = g("L0_ffn_norm_gamma").reshape(8, 128).T
    vecsF[:, 8:16] = g("L0_ffn_norm_beta").reshape(8, 128).T
    vecsF[:, 16:18] = g("L0_ffn_compress_b1").reshape(2, 128).T
    vecsF[0:2, 18] = g("L0_ffn_compress_b2")
    ap["vecsF"] = vecsF
    ap["b2tmd"] = np.ascontiguousarray(
        np.broadcast_to(g("L0_ffn_spline_b2")[:, 0], (128, 64)).astype(np.float32))
    qs = g("L0_attn_b_queries")
    qT = np.zeros((128, 32 * 256), np.float32)
    for a in range(4):
        for q in range(4):
            blk = qs[a][:, q * 256:(q + 1) * 256].T
            for fch in range(2):
                idx = a * 8 + q * 2 + fch
                qT[:, idx * 256:(idx + 1) * 256] = blk[fch * 128:(fch + 1) * 128, :]
    ap["qTd"] = qT
    cb = g("L0_attn_b_combiners")
    cbd = np.zeros((128, 32 * 256), np.float32)
    for a in range(4):
        for k in range(8):
            idx = a * 8 + k
            cbd[:, idx * 256:(idx + 1) * 256] = cb[a][k * 128:(k + 1) * 128, :]
    ap["combd"] = cbd
    ap["b1fd"] = np.ascontiguousarray(g("L0_ffn_spline_b1").reshape(64, 16).reshape(8, 8, 16).transpose(0, 2, 1).reshape(128, 8) if False else g("L0_ffn_spline_b1").reshape(1024).reshape(8, 128).T)
    return ap


def _mk_core_inputs(ap, Xb, half):
    m = dict(ap)
    if half == 0:
        m["Rin"] = np.ascontiguousarray(Xb.T)
        xprev = Xb[T - 1]
        prevmask, midmask = 0.0, 1.0
    else:
        m["Rin"] = np.ascontiguousarray(
            np.concatenate([Xb[T // 2:], Xb[:T // 2]], axis=0).T)
        xprev = Xb[T // 2 - 1]
        prevmask, midmask = 1.0, 0.0
    halo = np.zeros((128, 10), np.float32)
    halo[:, 0:8] = xprev.reshape(8, 128).T
    halo[:, 8] = prevmask
    halo[:, 9] = midmask
    m["halo"] = halo
    return m


def kernel(**inputs):
    ap = _host_prep(inputs)
    X = np.asarray(inputs["L0_attn_a_x"], dtype=np.float32)
    nc = build_program(("A", "F1", "B", "F2"))
    in_maps = []
    for core in range(8):
        in_maps.append(_mk_core_inputs(ap, X[core // 2], core % 2))
    res = run_bass_kernel_spmd(nc, in_maps, core_ids=list(range(8)))
    out = np.zeros((4, T, D), np.float32)
    for b in range(4):
        out[b, 0:T // 2] = res.results[2 * b]["Rout"].T
        out[b, T // 2:] = res.results[2 * b + 1]["Rout"].T
    return out
